# version 08
# baseline (speedup 1.0000x reference)
import os
import numpy as np
import ml_dtypes
import concourse.bass as bass
import concourse.mybir as mybir
from concourse.bass_utils import run_bass_kernel_spmd
from contextlib import ExitStack

F32 = mybir.dt.float32
BF16 = mybir.dt.bfloat16
I32 = mybir.dt.int32
AF = mybir.ActivationFunctionType
ALU = mybir.AluOpType
AX = mybir.AxisListType
BF = ml_dtypes.bfloat16

SEQ = 2
L = 2048
D = 2048
NH = 8
HD = 256
I = 5632
TT = 128
NTS = L // TT
NC = 16
NCI = I // 128
TB = 512
NB = SEQ * L // TB
NTB = TB // TT
NJ = I // 512
TL = SEQ * L
ENGS = ("sync", "tensor", "vector", "scalar", "gpsimd")
NEGBIG = -30000.0
ALLPH = ("A0", "W", "A1", "A2", "B", "C")


class Sem:
    def __init__(self, es, nc, name, queue=None):
        self.h = es.enter_context(nc.semaphore(name))
        self.n = 0
        self.queue = queue


DRAIN_ENGS = ("vector", "scalar", "gpsimd")


class Ph:
    def __init__(self, nc):
        self.nc = nc
        self.ops = {e: [] for e in ENGS}
        self.dma_sems = []
        self.drained = {e: True for e in ENGS}

    def raw(self, eng, fn):
        if eng in DRAIN_ENGS and not self.drained[eng]:
            self.ops[eng].append(lambda e: e.drain())
        self.drained[eng] = False
        self.ops[eng].append(fn)

    def op(self, eng, fn, inc=None):
        self.raw(eng, fn)
        if inc is None:
            return None
        sem, k = inc
        sem.n += k
        self.ops[eng].append(lambda e, h=sem.h, k=k: e.drain().then_inc(h, k))
        self.drained[eng] = True
        return sem.n

    def wait(self, eng, sem, val=None):
        v = sem.n if val is None else val
        if v > 0:
            self.ops[eng].append(lambda e, h=sem.h, v=v: e.wait_ge(h, v))

    def dma(self, q, dst, src, sem, transpose=False):
        assert sem.queue == q
        if sem not in self.dma_sems:
            self.dma_sems.append(sem)
        sem.n += 16
        if transpose:
            self.ops[q].append(lambda e, d=dst, s=src, h=sem.h:
                               e.dma_start(d, s, transpose=True).then_inc(h, 16))
        else:
            self.ops[q].append(lambda e, d=dst, s=src, h=sem.h:
                               e.dma_start(d, s).then_inc(h, 16))
        return sem.n

    def mark(self, eng, sem):
        sem.n += 1
        self.ops[eng].append(lambda e, h=sem.h: e.drain().then_inc(h, 1))
        self.drained[eng] = True
        return sem.n

    def inc_last(self, eng, sem, k=1):
        sem.n += k
        old = self.ops[eng][-1]
        self.ops[eng][-1] = lambda e, old=old, h=sem.h, k=k: old(e).then_inc(h, k)
        return sem.n

    def flush(self):
        for sem in self.dma_sems:
            self.wait(sem.queue, sem)
        self.dma_sems = []
        with self.nc.Block() as blk:
            for e in ENGS:
                lst = self.ops[e]
                if lst:
                    getattr(blk, e)(lambda eng, lst=lst: [f(eng) for f in lst])
        self.ops = {e: [] for e in ENGS}
        self.drained = {e: True for e in ENGS}


def build(debug=False, phases=ALLPH):
    nc = bass.Bass("TRN2", target_bir_lowering=False)

    x_in = nc.dram_tensor("x_in", [TL, D], F32, kind="ExternalInput").ap()
    qTw_h = nc.dram_tensor("qTw", [D, D], F32, kind="ExternalInput").ap()
    kTw_h = nc.dram_tensor("kTw", [D, HD], F32, kind="ExternalInput").ap()
    vTw_h = nc.dram_tensor("vTw", [D, HD], F32, kind="ExternalInput").ap()
    oTw_h = nc.dram_tensor("oTw", [D, D], F32, kind="ExternalInput").ap()
    w1_h = nc.dram_tensor("w1n", [2 * I, D], F32, kind="ExternalInput").ap()
    w2_h = nc.dram_tensor("w2n", [D, I], F32, kind="ExternalInput").ap()
    qnb_h = nc.dram_tensor("qnb", [128, D], F32, kind="ExternalInput").ap()
    knb_h = nc.dram_tensor("knb", [128, HD], F32, kind="ExternalInput").ap()
    ctab_h = nc.dram_tensor("ctab", [L, HD], F32, kind="ExternalInput").ap()
    stab_h = nc.dram_tensor("stab", [L, HD], F32, kind="ExternalInput").ap()
    mskd_h = nc.dram_tensor("mskd", [128, 128], F32, kind="ExternalInput").ap()
    id16_h = nc.dram_tensor("id16", [128, 128], BF16, kind="ExternalInput").ap()
    cst_h = nc.dram_tensor("cst", [128, 8], F32, kind="ExternalInput").ap()
    y_out = nc.dram_tensor("y", [TL, D], F32, kind="ExternalOutput").ap()

    dbg = "ExternalOutput" if debug else "Internal"
    xbf_d = nc.dram_tensor("xbf_d", [TL, D], BF16, kind=dbg).ap()
    qbf_d = nc.dram_tensor("qbf_d", [TL, D], BF16, kind=dbg).ap()
    kbf_d = nc.dram_tensor("kbf_d", [TL, HD], BF16, kind=dbg).ap()
    x1_d = nc.dram_tensor("x1_d", [TL, D], BF16, kind=dbg).ap()
    qx_d = nc.dram_tensor("qx_d", [TL, D], BF16, kind=dbg).ap()
    qw1_d = nc.dram_tensor("qw1_d", [D, 2 * I], BF16, kind=dbg).ap()
    qw2_d = nc.dram_tensor("qw2_d", [I, D], BF16, kind=dbg).ap()
    accT_d = nc.dram_tensor("accT_d", [D, TL], BF16, kind=dbg).ap()

    es = ExitStack()
    with es:
        sPE = Sem(es, nc, "sPE")
        sDV = Sem(es, nc, "sDV")
        sAC = Sem(es, nc, "sAC")
        sGP = Sem(es, nc, "sGP")

        id16 = es.enter_context(nc.sbuf_tensor("id16s", [128, 128], BF16))
        mskd = es.enter_context(nc.sbuf_tensor("mskds", [128, 128], F32))
        sca = es.enter_context(nc.sbuf_tensor("sca", [128, 64], F32))
        cst = es.enter_context(nc.sbuf_tensor("cst_s", [128, 8], F32))

        def CST(c):
            return cst[:, c:c + 1]

        def SC(c, n=1):
            return sca[:, c:c + n]

        ph = Ph(nc)

        sLc0 = Sem(es, nc, "sLc0", "sync")
        ph.dma("sync", id16[:, :], id16_h[:, :], sLc0)
        ph.dma("sync", mskd[:, :], mskd_h[:, :], sLc0)
        ph.dma("sync", cst[:, :], cst_h[:, :], sLc0)
        consts_ready = sLc0.n
        a0_sems = []
        if "A0" in phases:
            sLa = Sem(es, nc, "sLa", "gpsimd")
            a0_sems = [sLa]
            for it in range(TL // TT):
                ph.dma("gpsimd", xbf_d[it * TT:(it + 1) * TT, :],
                       x_in[it * TT:(it + 1) * TT, :], sLa)

        w_store_sems = []
        if "W" in phases:
            esW = ExitStack()
            NWB = 4
            wst = [esW.enter_context(nc.sbuf_tensor(f"wst{b}", [128, I], F32))
                   for b in range(NWB)]
            i32w = [esW.enter_context(nc.sbuf_tensor(f"i32w{b}", [128, I], I32))
                    for b in range(2)]
            qwst = [esW.enter_context(nc.sbuf_tensor(f"qwst{b}", [128, I], BF16))
                    for b in range(2)]
            stg = [esW.enter_context(nc.sbuf_tensor(f"stg{b}", [128, I], BF16))
                   for b in range(NWB)]
            pst = [esW.enter_context(nc.psum_tensor(f"pst{b}", [128, 2048], BF16))
                   for b in range(2)]
            sLw = [Sem(es, nc, f"sLw{b}", "sync") for b in range(NWB)]
            sSw = [Sem(es, nc, f"sSw{b}", "scalar") for b in range(NWB)]
            w_store_sems = sSw

            strips = [("w1", s, D, NC) for s in range(2 * I // 128)] + \
                     [("w2", s, I, NCI) for s in range(D // 128)]
            _wl = os.environ.get("K_W_STRIPS")
            if _wl:
                strips = strips[:int(_wl)]
            amax_done = {}
            rnd_done = {}
            dq_done = {}
            tr_done = {}
            ev_done = {}
            for si, (wn, s, wd, nch) in enumerate(strips):
                b = si % NWB
                b2 = si % 2
                cq = 4 + 3 * (si % 4)
                src_ = (w1_h[s * 128:(s + 1) * 128, :] if wn == "w1"
                        else w2_h[s * 128:(s + 1) * 128, :])
                if si >= NWB:
                    ph.wait("sync", sAC, rnd_done[si - NWB])
                ph.dma("sync", wst[b][:, 0:wd], src_, sLw[b])
                ph.wait("vector", sLw[b])
                ph.raw("vector", lambda e, b=b, wd=wd, cq=cq: e.reduce_max(
                    SC(cq), wst[b][:, 0:wd], axis=AX.X, apply_absolute_value=True))
                ph.raw("vector", lambda e, cq=cq: e.tensor_scalar(
                    SC(cq + 1), SC(cq), 1.0 / 127.0, 1e-8, ALU.mult, ALU.max))
                amax_done[si] = ph.op("vector", lambda e, cq=cq: e.reciprocal(
                    SC(cq + 2), SC(cq + 1)), (sDV, 1))
                ph.wait("scalar", sDV, amax_done[si])
                if si >= 2:
                    ph.wait("scalar", sGP, dq_done[si - 2])
                ph.raw("scalar", lambda e, b=b, b2=b2, wd=wd, cq=cq: e.activation(
                    i32w[b2][:, 0:wd], wst[b][:, 0:wd], AF.Identity, scale=SC(cq + 2)))
                rnd_done[si] = ph.inc_last("scalar", sAC)
                ph.wait("gpsimd", sAC, rnd_done[si])
                if si >= 2:
                    ph.wait("gpsimd", sPE, tr_done[si - 2])
                ph.raw("gpsimd", lambda e, b2=b2, wd=wd, cq=cq: e.tensor_scalar(
                    qwst[b2][:, 0:wd], i32w[b2][:, 0:wd], SC(cq + 1), None, ALU.mult))
                dq_done[si] = ph.mark("gpsimd", sGP)
                ph.wait("tensor", sGP, dq_done[si])
                ph.wait("tensor", sLc0, consts_ready)
                ph.wait("tensor", sSw[b])
                if si >= 2:
                    ph.wait("tensor", sDV, ev_done[si - 2])
                ngr = (nch + 15) // 16
                for g in range(ngr):
                    c0, c1 = g * 16, min((g + 1) * 16, nch)
                    if g > 0:
                        ph.wait("tensor", sDV, sDV.n)
                    for c in range(c0, c1):
                        ph.raw("tensor", lambda e, c=c, c0=c0, b2=b2: e.transpose(
                            pst[b2][:, (c - c0) * 128:(c - c0 + 1) * 128],
                            qwst[b2][:, c * 128:(c + 1) * 128], id16[:, :]))
                    pm = ph.inc_last("tensor", sPE)
                    ph.wait("vector", sPE, pm)
                    ph.op("vector", lambda e, b=b, b2=b2, c0=c0, c1=c1: e.tensor_copy(
                        stg[b][:, c0 * 128:c1 * 128], pst[b2][:, 0:(c1 - c0) * 128]),
                        (sDV, 1))
                tr_done[si] = pm
                ev_done[si] = sDV.n
                ph.wait("scalar", sDV, ev_done[si])
                nsp = 2 if wn == "w1" else 4
                csz = (nch + nsp - 1) // nsp
                dT, drows = (qw1_d, 2 * I) if wn == "w1" else (qw2_d, D)
                for sp in range(nsp):
                    c0, c1 = sp * csz, min((sp + 1) * csz, nch)
                    if c0 >= c1:
                        continue
                    srcap = bass.AP(stg[b], c0 * 128, [[I, 128], [128, c1 - c0], [1, 128]])
                    dstap = bass.AP(dT.tensor, c0 * 128 * drows + s * 128,
                                    [[drows, 128], [128 * drows, c1 - c0], [1, 128]])
                    ph.dma("scalar", dstap, srcap, sSw[b])
            ph.flush()
            esW.close()

        esAT = ExitStack()
        sqk_done = {}
        sSqk = None
        sx_done = {}
        if "A1" in phases or "A2" in phases:
            v_sb = esAT.enter_context(nc.sbuf_tensor("v_sb", [128, SEQ * NTS * HD], BF16))

        if "A1" in phases:
            esA1 = ExitStack()
            qTw = esA1.enter_context(nc.sbuf_tensor("qTwS", [128, NC * D], BF16))
            kTw = esA1.enter_context(nc.sbuf_tensor("kTwS", [128, NC * HD], BF16))
            vTw = esA1.enter_context(nc.sbuf_tensor("vTwS", [128, NC * HD], BF16))
            xT = [esA1.enter_context(nc.sbuf_tensor(f"xT{b}", [128, NC * 512], BF16))
                  for b in range(2)]
            qnb = esA1.enter_context(nc.sbuf_tensor("qnbS", [128, D], F32))
            knb = esA1.enter_context(nc.sbuf_tensor("knbS", [128, HD], F32))
            ctab = [esA1.enter_context(nc.sbuf_tensor(f"ctab{b}", [128, HD], F32))
                    for b in range(2)]
            stab = [esA1.enter_context(nc.sbuf_tensor(f"stab{b}", [128, HD], F32))
                    for b in range(2)]
            qhat = esA1.enter_context(nc.sbuf_tensor("qhat", [128, D], F32))
            rt1 = esA1.enter_context(nc.sbuf_tensor("rt1", [128, D], F32))
            rt2 = esA1.enter_context(nc.sbuf_tensor("rt2", [128, D], F32))
            qbf = esA1.enter_context(nc.sbuf_tensor("qbf", [128, D], BF16))
            khat = esA1.enter_context(nc.sbuf_tensor("khat", [128, HD], F32))
            kr1 = esA1.enter_context(nc.sbuf_tensor("kr1", [128, HD], F32))
            kr2 = esA1.enter_context(nc.sbuf_tensor("kr2", [128, HD], F32))
            kbf = esA1.enter_context(nc.sbuf_tensor("kbf", [128, HD], BF16))
            sqg = esA1.enter_context(nc.sbuf_tensor("sqg", [128, HD], F32))
            qps = esA1.enter_context(nc.psum_tensor("qps", [128, D], F32))
            kps = esA1.enter_context(nc.psum_tensor("kps", [128, 512], F32))
            vps = esA1.enter_context(nc.psum_tensor("vps", [128, 512], F32))
            sLwq = Sem(es, nc, "sLwq", "gpsimd")
            sLnb = Sem(es, nc, "sLnb", "sync")
            sLxT = [Sem(es, nc, f"sLxT{b}", "sync") for b in range(2)]
            sLtab = [Sem(es, nc, f"sLtab{b}", "scalar") for b in range(2)]
            sSqk = Sem(es, nc, "sSqk", "scalar")

            for c in range(NC):
                ph.dma("gpsimd", qTw[:, c * D:(c + 1) * D],
                       qTw_h[c * 128:(c + 1) * 128, :], sLwq)
                ph.dma("gpsimd", kTw[:, c * HD:(c + 1) * HD],
                       kTw_h[c * 128:(c + 1) * 128, :], sLwq)
                ph.dma("gpsimd", vTw[:, c * HD:(c + 1) * HD],
                       vTw_h[c * 128:(c + 1) * 128, :], sLwq)
            wload = sLwq.n
            ph.dma("sync", qnb[:, :], qnb_h[:, :], sLnb)
            ph.dma("sync", knb[:, :], knb_h[:, :], sLnb)
            nb_ready = sLnb.n

            NQ = 4
            pe_tile_done = {}
            psum_free = {}
            gp_done = {}
            dv_rope_done = {}
            for s_ in a0_sems:
                ph.wait("sync", s_)
            for qb_i in range(TL // (NQ * TT)):
                b = qb_i % 2
                sq = qb_i // 4
                q_in_seq = qb_i % 4
                if qb_i >= 2:
                    ph.wait("sync", sPE, pe_tile_done.get((qb_i - 2, NQ - 1), 0))
                for c in range(NC):
                    ph.dma("sync", xT[b][:, c * 512:(c + 1) * 512],
                           xbf_d[sq * L + q_in_seq * 512: sq * L + (q_in_seq + 1) * 512,
                                 c * 128:(c + 1) * 128], sLxT[b], transpose=True)
                xt_ready = sLxT[b].n
                for ti in range(NQ):
                    it = qb_i * NQ + ti
                    trow = it * TT
                    tb_ = it % 2
                    if it >= 2:
                        ph.wait("scalar", sGP, gp_done.get(it - 2, 0))
                        ph.wait("scalar", sDV, dv_rope_done.get(it - 2, 0))
                    ph.dma("scalar", ctab[tb_][:, :],
                           ctab_h[(it % NTS) * TT:((it % NTS) + 1) * TT, :], sLtab[tb_])
                    ph.dma("scalar", stab[tb_][:, :],
                           stab_h[(it % NTS) * TT:((it % NTS) + 1) * TT, :], sLtab[tb_])
                    tabs_ready = sLtab[tb_].n
                    ph.wait("tensor", sLwq, wload)
                    ph.wait("tensor", sLxT[b], xt_ready)
                    if it >= 1:
                        ph.wait("tensor", sDV, psum_free[it - 1])
                    for c in range(NC):
                        lhs = xT[b][:, c * 512 + ti * 128: c * 512 + (ti + 1) * 128]
                        st, sp = (c == 0), (c == NC - 1)
                        for ob in range(4):
                            ph.raw("tensor", lambda e, l=lhs, ob=ob, st=st, sp=sp, c=c:
                                   e.matmul(qps[:, ob * 512:(ob + 1) * 512], l,
                                            qTw[:, c * D + ob * 512: c * D + (ob + 1) * 512],
                                            start=st, stop=sp))
                        ph.raw("tensor", lambda e, l=lhs, st=st, sp=sp, c=c: e.matmul(
                            kps[:, 0:HD], l, kTw[:, c * HD:(c + 1) * HD],
                            start=st, stop=sp))
                        ph.raw("tensor", lambda e, l=lhs, st=st, sp=sp, c=c: e.matmul(
                            vps[:, 0:HD], l, vTw[:, c * HD:(c + 1) * HD],
                            start=st, stop=sp))
                    pe_tile_done[(qb_i, ti)] = ph.inc_last("tensor", sPE)
                    ph.wait("scalar", sPE, pe_tile_done[(qb_i, ti)])
                    for h in range(NH):
                        ph.raw("scalar", lambda e, h=h: e.activation(
                            sqg[:, :], qps[:, h * HD:(h + 1) * HD], AF.Square,
                            accum_out=SC(12 + h)))
                    ph.raw("scalar", lambda e: e.activation(
                        sqg[:, :], kps[:, 0:HD], AF.Square, accum_out=SC(20)))
                    ph.wait("scalar", sLc0, consts_ready)
                    ph.raw("scalar", lambda e: e.activation(
                        SC(21, 9), SC(12, 9), AF.Ln, scale=1.0 / 256.0, bias=CST(0)))
                    ph.raw("scalar", lambda e: e.activation(
                        SC(21, 9), SC(21, 9), AF.Identity, scale=-0.5))
                    ac_rms = ph.op("scalar", lambda e: e.activation(
                        SC(21, 9), SC(21, 9), AF.Exp), (sAC, 1))
                    ph.wait("vector", sAC, ac_rms)
                    ph.wait("vector", sLnb, nb_ready)
                    for h in range(NH):
                        ph.raw("vector", lambda e, h=h: e.scalar_tensor_tensor(
                            qhat[:, h * HD:(h + 1) * HD], qps[:, h * HD:(h + 1) * HD],
                            SC(21 + h), qnb[:, h * HD:(h + 1) * HD], ALU.mult, ALU.mult))
                    ph.raw("vector", lambda e: e.scalar_tensor_tensor(
                        khat[:, :], kps[:, 0:HD], SC(29), knb[:, :], ALU.mult, ALU.mult))
                    psum_free[it] = ph.op("vector", lambda e, it=it: e.tensor_copy(
                        v_sb[:, it * HD:(it + 1) * HD], vps[:, 0:HD]), (sDV, 1))
                    ct, st_ = ctab[tb_], stab[tb_]
                    ph.wait("gpsimd", sDV, psum_free[it])
                    ph.wait("gpsimd", sLtab[tb_], tabs_ready)
                    ph.wait("vector", sLtab[tb_], tabs_ready)
                    for h in range(NH):
                        q0 = h * HD
                        ph.raw("vector", lambda e, q0=q0, ct=ct: e.tensor_tensor(
                            rt1[:, q0:q0 + HD], qhat[:, q0:q0 + HD], ct[:, :], ALU.mult))
                        ph.raw("gpsimd", lambda e, q0=q0, st_=st_: e.tensor_tensor(
                            rt2[:, q0:q0 + 128], qhat[:, q0 + 128:q0 + HD],
                            st_[:, 0:128], ALU.mult))
                        ph.raw("gpsimd", lambda e, q0=q0, st_=st_: e.tensor_tensor(
                            rt2[:, q0 + 128:q0 + HD], qhat[:, q0:q0 + 128],
                            st_[:, 128:HD], ALU.mult))
                    gp_done[it] = ph.mark("gpsimd", sGP)
                    ph.wait("vector", sGP, gp_done[it])
                    ph.wait("vector", sSqk)
                    ph.raw("vector", lambda e: e.tensor_tensor(
                        qbf[:, :], rt1[:, :], rt2[:, :], ALU.add))
                    ph.raw("vector", lambda e, ct=ct: e.tensor_tensor(
                        kr1[:, :], khat[:, :], ct[:, :], ALU.mult))
                    ph.raw("vector", lambda e, st_=st_: e.tensor_tensor(
                        kr2[:, 0:128], khat[:, 128:HD], st_[:, 0:128], ALU.mult))
                    ph.raw("vector", lambda e, st_=st_: e.tensor_tensor(
                        kr2[:, 128:HD], khat[:, 0:128], st_[:, 128:HD], ALU.mult))
                    dv_rope_done[it] = ph.op("vector", lambda e: e.tensor_tensor(
                        kbf[:, :], kr1[:, :], kr2[:, :], ALU.add), (sDV, 1))
                    ph.wait("scalar", sDV, dv_rope_done[it])
                    ph.dma("scalar", qbf_d[trow:trow + TT, :], qbf[:, :], sSqk)
                    ph.dma("scalar", kbf_d[trow:trow + TT, :], kbf[:, :], sSqk)
                    sqk_done[it] = sSqk.n
            ph.flush()
            esA1.close()

        if "A2" in phases:
            esA2 = ExitStack()
            oTw = esA2.enter_context(nc.sbuf_tensor("oTwS", [128, NC * D], BF16))
            qT = esA2.enter_context(nc.sbuf_tensor("qTS", [128, NC * 1024], BF16))
            kT = esA2.enter_context(nc.sbuf_tensor("kTS", [128, 2 * L], BF16))
            w_sb = [esA2.enter_context(nc.sbuf_tensor(f"w_sb{b}", [128, 1024], BF16))
                    for b in range(2)]
            wT_sb = [esA2.enter_context(nc.sbuf_tensor(f"wT_sb{b}", [128, 1024], BF16))
                     for b in range(2)]
            ctxT = esA2.enter_context(nc.sbuf_tensor("ctxT", [128, D], BF16))
            x_sb = [esA2.enter_context(nc.sbuf_tensor(f"x_sb{b}", [128, D], F32))
                    for b in range(2)]
            y_sb = esA2.enter_context(nc.sbuf_tensor("y_sb", [128, D], F32))
            x1_sb = esA2.enter_context(nc.sbuf_tensor("x1_sb", [128, D], F32))
            i32a = esA2.enter_context(nc.sbuf_tensor("i32a", [128, D], I32))
            qx_sb = esA2.enter_context(nc.sbuf_tensor("qx_sb", [128, D], BF16))
            sqga = esA2.enter_context(nc.sbuf_tensor("sqga", [128, D], F32))
            scps = [esA2.enter_context(nc.psum_tensor(f"scps{b}", [128, 1024], F32))
                    for b in range(2)]
            psw = [esA2.enter_context(nc.psum_tensor(f"psw{b}", [128, 1024], BF16))
                   for b in range(2)]
            ctxp = [esA2.enter_context(nc.psum_tensor(f"ctxp{b}", [128, 512], F32))
                    for b in range(2)]
            sLo = Sem(es, nc, "sLo", "gpsimd")
            sLkT = Sem(es, nc, "sLkT", "sync")
            sLqT = Sem(es, nc, "sLqT", "sync")
            sLx = [Sem(es, nc, f"sLx{b}", "sync") for b in range(2)]
            sSx = Sem(es, nc, "sSx", "scalar")
            sSx1 = Sem(es, nc, "sSx1", "gpsimd")

            for c in range(NC):
                ph.dma("gpsimd", oTw[:, c * D:(c + 1) * D],
                       oTw_h[c * 128:(c + 1) * 128, :], sLo)
            oload = sLo.n

            uix = 0
            sc_free = {0: None, 1: None}
            psw_free = {0: 0, 1: 0}
            wsb_free = {0: 0, 1: 0}
            wt_free = {0: 0, 1: 0}
            xfree = {}
            last_sc_pm = 0
            kT_free = 0
            qT_free = 0
            for sq in range(SEQ):
                if sqk_done:
                    ph.wait("sync", sSqk, sqk_done[sq * NTS + NTS - 1])
                if sq > 0:
                    ph.wait("sync", sPE, kT_free)
                for c2 in range(2):
                    ph.dma("sync", kT[:, c2 * L:(c2 + 1) * L],
                           kbf_d[sq * L:(sq + 1) * L, c2 * 128:(c2 + 1) * 128], sLkT,
                           transpose=True)
                kt_ready = sLkT.n
                for hf in range(2):
                    if sq + hf > 0:
                        ph.wait("sync", sPE, qT_free)
                    for c in range(NC):
                        ph.dma("sync", qT[:, c * 1024:(c + 1) * 1024],
                               qbf_d[sq * L + hf * 1024: sq * L + (hf + 1) * 1024,
                                     c * 128:(c + 1) * 128], sLqT, transpose=True)
                    qt_ready = sLqT.n
                    for ti in range(8):
                        i = hf * 8 + ti
                        it = sq * NTS + i
                        trow = it * TT
                        Lk = (i + 1) * 128
                        nparts = (Lk + 1023) // 1024
                        xb = it % 2
                        if it >= 2:
                            ph.wait("sync", sDV, xfree.get(it - 2, 0))
                        ph.dma("sync", x_sb[xb][:, :], x_in[trow:trow + TT, :], sLx[xb])
                        x_ready = sLx[xb].n
                        dv_cx = 0
                        for h in range(NH):
                            B0 = 8 * (uix % 4)
                            uix += 1
                            qts = [qT[:, c_ * 1024 + ti * 128: c_ * 1024 + (ti + 1) * 128]
                                   for c_ in (2 * h, 2 * h + 1)]
                            pinfo = []
                            for p in range(nparts):
                                par = p if nparts == 2 else (uix % 2)
                                tk0 = p * 1024
                                tkw = min(1024, Lk - tk0)
                                fullw = min(tkw, i * 128 - tk0)
                                has_diag = (tk0 + tkw > i * 128)
                                n5 = fullw // 512
                                segs = [(s5 * 512, 512) for s5 in range(n5)]
                                if fullw > n5 * 512:
                                    segs.append((n5 * 512, fullw - n5 * 512))
                                if has_diag:
                                    segs.append((fullw, 128))
                                if h == 0 and p == 0:
                                    ph.wait("tensor", sLqT, qt_ready)
                                    ph.wait("tensor", sLkT, kt_ready)
                                if sc_free[par] is not None:
                                    sem_, v_ = sc_free[par]
                                    ph.wait("tensor", sem_, v_)
                                for o0, wseg in segs:
                                    for cc in range(2):
                                        ph.raw("tensor",
                                               lambda e, q_=qts[cc], cc=cc, o0=o0,
                                               wseg=wseg, par=par, tk0=tk0: e.matmul(
                                            scps[par][:, o0:o0 + wseg], q_,
                                            kT[:, cc * L + tk0 + o0:
                                               cc * L + tk0 + o0 + wseg],
                                            start=(cc == 0), stop=(cc == 1)))
                                pm_sc = ph.inc_last("tensor", sPE)
                                last_sc_pm = pm_sc
                                ph.wait("vector", sPE, pm_sc)
                                if has_diag:
                                    ph.raw("vector", lambda e, par=par, do=fullw:
                                           e.tensor_tensor(
                                        scps[par][:, do:do + 128],
                                        scps[par][:, do:do + 128], mskd[:, :],
                                        ALU.add))
                                mcol = B0 + 5 + p
                                bcol = B0 if p == 0 else B0 + 7
                                ph.raw("vector", lambda e, par=par, tkw=tkw, mcol=mcol:
                                       e.reduce_max(
                                    SC(mcol), scps[par][:, 0:tkw], axis=AX.X,
                                    negate=True))
                                dv_b = ph.op("vector", lambda e, mcol=mcol, bcol=bcol:
                                             e.tensor_scalar(
                                    SC(bcol), SC(mcol), 1.0 / 16.0, None, ALU.mult),
                                    (sDV, 1))
                                ph.wait("scalar", sDV, dv_b)
                                ph.wait("scalar", sPE, wsb_free[par])
                                scol = B0 + 2 + p
                                ac_e = ph.op("scalar", lambda e, par=par, tkw=tkw,
                                             bcol=bcol, scol=scol: e.activation(
                                    w_sb[par][:, 0:tkw], scps[par][:, 0:tkw], AF.Exp,
                                    bias=SC(bcol), scale=1.0 / 16.0,
                                    accum_out=SC(scol)), (sAC, 1))
                                sc_free[par] = (sAC, ac_e)
                                pinfo.append((par, tk0, tkw, scol, ac_e))
                            ph.wait("vector", sAC, pinfo[-1][4])
                            if nparts == 1:
                                (par, tk0, tkw, scol, _) = pinfo[0]
                                ph.raw("vector", lambda e, B0=B0, scol=scol:
                                       e.reciprocal(SC(B0 + 1), SC(scol)))
                                norm_done = {par: None}
                                gcol = {par: B0 + 1}
                            else:
                                ph.raw("vector", lambda e, B0=B0: e.tensor_tensor(
                                    SC(B0 + 7), SC(B0 + 5), SC(B0 + 6), ALU.min))
                                ph.raw("vector", lambda e, B0=B0: e.tensor_scalar(
                                    SC(B0 + 5, 2), SC(B0 + 5, 2), SC(B0 + 7), None,
                                    ALU.subtract))
                                dv_f = ph.op("vector", lambda e, B0=B0:
                                             e.tensor_scalar(
                                    SC(B0 + 5, 2), SC(B0 + 5, 2), -1.0 / 16.0, None,
                                    ALU.mult), (sDV, 1))
                                ph.wait("scalar", sDV, dv_f)
                                ac_f = ph.op("scalar", lambda e, B0=B0: e.activation(
                                    SC(B0 + 5, 2), SC(B0 + 5, 2), AF.Exp), (sAC, 1))
                                ph.wait("vector", sAC, ac_f)
                                ph.raw("vector", lambda e, B0=B0: e.tensor_tensor(
                                    SC(B0 + 2, 2), SC(B0 + 2, 2), SC(B0 + 5, 2),
                                    ALU.mult))
                                ph.raw("vector", lambda e, B0=B0: e.reduce_sum(
                                    SC(B0 + 4), SC(B0 + 2, 2), axis=AX.X))
                                ph.raw("vector", lambda e, B0=B0: e.reciprocal(
                                    SC(B0 + 1), SC(B0 + 4)))
                                ph.raw("vector", lambda e, B0=B0: e.tensor_scalar(
                                    SC(B0 + 5, 2), SC(B0 + 5, 2), SC(B0 + 1), None,
                                    ALU.mult))
                                norm_done = {}
                                gcol = {pinfo[0][0]: B0 + 5, pinfo[1][0]: B0 + 6}
                            nch_tot = (Lk + 127) // 128
                            done_ch = 0
                            for pi, (par, tk0, tkw, scol, _) in enumerate(pinfo):
                                nch = (tkw + 127) // 128
                                dv_n = ph.op("vector", lambda e, par=par, tkw=tkw,
                                             g=gcol[par]: e.tensor_scalar(
                                    w_sb[par][:, 0:tkw], w_sb[par][:, 0:tkw],
                                    SC(g), None, ALU.mult), (sDV, 1))
                                ph.wait("tensor", sDV, dv_n)
                                ph.wait("tensor", sDV, psw_free[par])
                                for j in range(nch):
                                    ph.raw("tensor", lambda e, par=par, j=j:
                                           e.transpose(
                                        psw[par][:, j * 128:(j + 1) * 128],
                                        w_sb[par][:, j * 128:(j + 1) * 128],
                                        id16[:, :]))
                                pm_tr = ph.inc_last("tensor", sPE)
                                wsb_free[par] = pm_tr
                                ph.wait("vector", sPE, pm_tr)
                                ph.wait("vector", sPE, wt_free[par])
                                dv_ev = ph.op("vector", lambda e, par=par, tkw=tkw:
                                              e.tensor_copy(
                                    wT_sb[par][:, 0:tkw], psw[par][:, 0:tkw]),
                                    (sDV, 1))
                                psw_free[par] = dv_ev
                                ph.wait("tensor", sDV, dv_ev)
                                for half in range(2):
                                    for j in range(nch):
                                        ph.raw("tensor",
                                               lambda e, half=half, j=j, sq=sq,
                                               tk0=tk0, par=par,
                                               st=(done_ch == 0 and j == 0),
                                               sp=(done_ch + nch == nch_tot
                                                   and j == nch - 1): e.matmul(
                                            ctxp[half][:, 0:128],
                                            v_sb[:, (sq * NTS + tk0 // 128 + j) * HD
                                                 + half * 128:
                                                 (sq * NTS + tk0 // 128 + j) * HD
                                                 + half * 128 + 128],
                                            wT_sb[par][:, j * 128:(j + 1) * 128],
                                            start=st, stop=sp))
                                pm_cx = ph.inc_last("tensor", sPE)
                                wt_free[par] = pm_cx
                                done_ch += nch
                            ph.wait("vector", sPE, pm_cx)
                            for half in range(2):
                                ph.raw("vector", lambda e, half=half, h=h:
                                       e.tensor_copy(
                                    ctxT[:, (2 * h + half) * 128:
                                         (2 * h + half + 1) * 128],
                                    ctxp[half][:, 0:128]))
                            dv_cx = ph.mark("vector", sDV)
                        ph.wait("tensor", sDV, dv_cx)
                        ph.wait("tensor", sLo, oload)
                        for par in range(2):
                            if sc_free[par] is not None:
                                sem_, v_ = sc_free[par]
                                ph.wait("tensor", sem_, v_)
                        for c in range(NC):
                            st, sp = (c == 0), (c == NC - 1)
                            for ob in range(4):
                                ph.raw("tensor", lambda e, c=c, ob=ob, st=st, sp=sp:
                                       e.matmul(
                                    scps[ob // 2][:, (ob % 2) * 512:
                                                  (ob % 2 + 1) * 512],
                                    ctxT[:, c * 128:(c + 1) * 128],
                                    oTw[:, c * D + ob * 512: c * D + (ob + 1) * 512],
                                    start=st, stop=sp))
                        pm_o = ph.inc_last("tensor", sPE)
                        ph.wait("vector", sPE, pm_o)
                        ph.wait("vector", sLx[xb], x_ready)
                        ph.raw("vector", lambda e, xb=xb: e.tensor_tensor(
                            y_sb[:, 0:1024], scps[0][:, 0:1024],
                            x_sb[xb][:, 0:1024], ALU.add))
                        dv_y2 = ph.op("vector", lambda e, xb=xb: e.tensor_tensor(
                            y_sb[:, 1024:2048], scps[1][:, 0:1024],
                            x_sb[xb][:, 1024:2048], ALU.add), (sDV, 1))
                        xfree[it] = dv_y2
                        sc_free = {0: (sDV, dv_y2), 1: (sDV, dv_y2)}
                        ph.wait("scalar", sDV, dv_y2)
                        ac_sq = ph.op("scalar", lambda e: e.activation(
                            sqga[:, :], y_sb[:, :], AF.Square, accum_out=SC(39)),
                            (sAC, 1))
                        ph.raw("vector", lambda e: e.reduce_sum(SC(40), y_sb[:, :],
                                                                axis=AX.X))
                        ph.raw("vector", lambda e: e.tensor_scalar(
                            SC(40), SC(40), 1.0 / D, None, ALU.mult))
                        ph.raw("vector", lambda e: e.tensor_tensor(
                            SC(41), SC(40), SC(40), ALU.mult))
                        ph.wait("vector", sAC, ac_sq)
                        dv_var = ph.op("vector", lambda e: e.scalar_tensor_tensor(
                            SC(42), SC(39), 1.0 / D, SC(41), ALU.mult, ALU.subtract),
                            (sDV, 1))
                        ph.wait("scalar", sDV, dv_var)
                        ph.raw("scalar", lambda e: e.activation(
                            SC(43), SC(42), AF.Ln, bias=CST(1)))
                        ph.raw("scalar", lambda e: e.activation(
                            SC(43), SC(43), AF.Identity, scale=-0.5))
                        ac_rstd = ph.op("scalar", lambda e: e.activation(
                            SC(43), SC(43), AF.Exp), (sAC, 1))
                        ph.wait("vector", sAC, ac_rstd)
                        ph.wait("vector", sSx)
                        ph.wait("vector", sSx1)
                        ph.raw("vector", lambda e: e.tensor_scalar(
                            x1_sb[:, :], y_sb[:, :], SC(40), SC(43),
                            ALU.subtract, ALU.mult))
                        ph.raw("vector", lambda e: e.reduce_max(
                            SC(44), x1_sb[:, :], axis=AX.X, apply_absolute_value=True))
                        ph.raw("vector", lambda e: e.tensor_scalar(
                            SC(45), SC(44), 1.0 / 127.0, 1e-8, ALU.mult, ALU.max))
                        dv_q = ph.op("vector", lambda e: e.reciprocal(SC(46), SC(45)),
                                     (sDV, 1))
                        ph.wait("scalar", sDV, dv_q)
                        ph.raw("scalar", lambda e: e.activation(
                            i32a[:, :], x1_sb[:, :], AF.Identity, scale=SC(46)))
                        epi = ph.op("scalar", lambda e: e.activation(
                            qx_sb[:, :], i32a[:, :], AF.Identity, scale=SC(45)),
                            (sAC, 1))
                        ph.wait("scalar", sAC, epi)
                        ph.wait("gpsimd", sAC, epi)
                        ph.dma("gpsimd", x1_d[trow:trow + TT, :], x1_sb[:, :], sSx1)
                        ph.dma("scalar", qx_d[trow:trow + TT, :], qx_sb[:, :], sSx)
                        sx_done[it] = (sSx.n, sSx1.n)
                    qT_free = last_sc_pm
                kT_free = last_sc_pm
            ph.flush()
            esA2.close()
        esAT.close()

        if "B" in phases:
            do_C = "C" in phases
            esB = ExitStack()
            qxT = esB.enter_context(nc.sbuf_tensor("qxT", [128, NC * TB], BF16))
            hbuf = esB.enter_context(nc.sbuf_tensor("hbuf", [128, NTB * I], BF16))
            wg = [esB.enter_context(nc.sbuf_tensor(f"wg{b}", [128, NC * 512], BF16))
                  for b in range(2)]
            wu = [esB.enter_context(nc.sbuf_tensor(f"wu{b}", [128, NC * 512], BF16))
                  for b in range(2)]
            w2T = [esB.enter_context(nc.sbuf_tensor(f"w2T{b}", [128, NCI * 128], BF16))
                   for b in range(2)]
            silu_t = esB.enter_context(nc.sbuf_tensor("silu_t", [128, 512], F32))
            qh_t = esB.enter_context(nc.sbuf_tensor("qh_t", [128, I], BF16))
            i32b = esB.enter_context(nc.sbuf_tensor("i32b", [128, 1024], I32))
            accst = [esB.enter_context(nc.sbuf_tensor(f"accst{b}", [128, 512], BF16))
                     for b in range(2)]
            gps = esB.enter_context(nc.psum_tensor("gps", [128, 1024], F32))
            ups = esB.enter_context(nc.psum_tensor("ups", [128, 1024], F32))
            tps = esB.enter_context(nc.psum_tensor("tps", [128, 1024], BF16))
            fps = [esB.enter_context(nc.psum_tensor(f"fps{b}", [128, 512], F32))
                   for b in range(3)]
            sLqxT = Sem(es, nc, "sLqxT", "sync")
            sLw1 = [Sem(es, nc, f"sLw1{b}", "sync") for b in range(2)]
            sLw2 = [Sem(es, nc, f"sLw2{b}", "sync") for b in range(2)]
            sSacc = [Sem(es, nc, f"sSacc{b}", "scalar") for b in range(2)]
            if do_C:
                acc_c = [esB.enter_context(nc.sbuf_tensor(f"acc_c{b}", [128, D], BF16))
                         for b in range(2)]
                x1c = [esB.enter_context(nc.sbuf_tensor(f"x1c{b}", [128, D], BF16))
                       for b in range(2)]
                yc = esB.enter_context(nc.sbuf_tensor("yc", [128, D], F32))
                yo = esB.enter_context(nc.sbuf_tensor("yo", [128, D], F32))
                sLcc = [Sem(es, nc, f"sLcc{b}", "sync") for b in range(2)]
                sLx1c = [Sem(es, nc, f"sLx1c{b}", "gpsimd") for b in range(2)]
                sSy = Sem(es, nc, "sSy", "scalar")
            add_done = {}
            sq_doneC = {}
            sacc_counts = {}

            def emit_C(blk):
                sc0, sc1 = sacc_counts[blk]
                ph.wait("sync", sSacc[0], sc0)
                ph.wait("sync", sSacc[1], sc1)
                for k2 in range(NTB):
                    it = blk * NTB + k2
                    b = it % 2
                    trow = it * TT
                    if it >= 2:
                        ph.wait("sync", sAC, sq_doneC[it - 2])
                        ph.wait("gpsimd", sAC, sq_doneC[it - 2])
                    if sx_done:
                        ph.wait("gpsimd", sSx1, sx_done[it][1])
                    ph.dma("sync", acc_c[b][:, :], accT_d[:, trow:trow + TT],
                           sLcc[b], transpose=True)
                    ph.dma("gpsimd", x1c[b][:, :], x1_d[trow:trow + TT, :],
                           sLx1c[b])
                    ph.wait("vector", sLcc[b])
                    ph.wait("vector", sLx1c[b])
                    ph.wait("vector", sSy)
                    add_done[it] = ph.op("vector", lambda e, b=b: e.tensor_tensor(
                        yc[:, :], acc_c[b][:, :], x1c[b][:, :], ALU.add), (sDV, 1))
                    ph.wait("scalar", sDV, add_done[it])
                    ac_sq = ph.op("scalar", lambda e, b=b: e.activation(
                        acc_c[b][:, :], yc[:, :], AF.Square, accum_out=SC(7)),
                        (sAC, 1))
                    sq_doneC[it] = ac_sq
                    ph.raw("vector", lambda e: e.reduce_sum(SC(8), yc[:, :],
                                                            axis=AX.X))
                    ph.raw("vector", lambda e: e.tensor_scalar(
                        SC(8), SC(8), 1.0 / D, None, ALU.mult))
                    ph.raw("vector", lambda e: e.tensor_tensor(SC(9), SC(8), SC(8),
                                                               ALU.mult))
                    ph.wait("vector", sAC, ac_sq)
                    dv_var = ph.op("vector", lambda e: e.scalar_tensor_tensor(
                        SC(10), SC(7), 1.0 / D, SC(9), ALU.mult, ALU.subtract),
                        (sDV, 1))
                    ph.wait("scalar", sDV, dv_var)
                    ph.wait("scalar", sLc0, consts_ready)
                    ph.raw("scalar", lambda e: e.activation(SC(11), SC(10), AF.Ln,
                                                            bias=CST(1)))
                    ph.raw("scalar", lambda e: e.activation(
                        SC(11), SC(11), AF.Identity, scale=-0.5))
                    ac_r = ph.op("scalar", lambda e: e.activation(
                        SC(11), SC(11), AF.Exp), (sAC, 1))
                    ph.wait("vector", sAC, ac_r)
                    dv_y = ph.op("vector", lambda e: e.tensor_scalar(
                        yo[:, :], yc[:, :], SC(8), SC(11), ALU.subtract, ALU.mult),
                        (sDV, 1))
                    ph.wait("scalar", sDV, dv_y)
                    ph.dma("scalar", y_out[trow:trow + TT, :], yo[:, :], sSy)

            def load_w1(jj, bg, bu, sem):
                for buf, co in ((bg, jj * 512), (bu, I + jj * 512)):
                    for sp in range(2):
                        c0 = sp * 8
                        srcap = bass.AP(qw1_d.tensor, (c0 * 128) * (2 * I) + co,
                                        [[2 * I, 128], [128 * 2 * I, 8], [1, 512]])
                        dstap = bass.AP(buf, c0 * 512,
                                        [[NC * 512, 128], [512, 8], [1, 512]])
                        ph.dma("sync", dstap, srcap, sem)
                return sem.n

            def load_w2(ocu, buf, sem):
                for sp in range(6):
                    c0, c1 = sp * 8, min((sp + 1) * 8, NCI)
                    if c0 >= c1:
                        break
                    srcap = bass.AP(qw2_d.tensor, (c0 * 128) * D + ocu * 128,
                                    [[D, 128], [128 * D, c1 - c0], [1, 128]])
                    dstap = bass.AP(buf, c0 * 128,
                                    [[NCI * 128, 128], [128, c1 - c0], [1, 128]])
                    ph.dma("sync", dstap, srcap, sem)
                return sem.n

            pe_pair = {}
            pe_down = {}
            hmul = {}
            fps_free = {}
            sil_free = 0
            for s_ in w_store_sems:
                ph.wait("sync", s_)
            for blk_i in range(NB):
                t0 = blk_i * TB
                if do_C and blk_i >= 1:
                    emit_C(blk_i - 1)
                ph.wait("sync", sPE, pe_pair.get((blk_i - 1, NJ - 1), 0))
                if sx_done:
                    ph.wait("sync", sSx, sx_done[blk_i * NTB + NTB - 1][0])
                for c in range(NC):
                    ph.dma("sync", qxT[:, c * TB:(c + 1) * TB],
                           qx_d[t0:t0 + TB, c * 128:(c + 1) * 128], sLqxT,
                           transpose=True)
                qxt_ready = sLqxT.n
                for jj in range(NJ):
                    wb = jj % 2
                    guard = pe_pair.get((blk_i, jj - 2),
                                        pe_pair.get((blk_i - 1, NJ - 2 + jj), 0))
                    ph.wait("sync", sPE, guard)
                    w_ready = load_w1(jj, wg[wb], wu[wb], sLw1[wb])
                    for ti in range(NTB):
                        pp = ti % 2
                        ph.wait("tensor", sLw1[wb], w_ready)
                        ph.wait("tensor", sLqxT, qxt_ready)
                        key = ((blk_i, jj, ti - 2) if ti >= 2
                               else (blk_i, jj - 1, NTB - 2 + ti))
                        ph.wait("tensor", sDV, hmul.get(key, 0))
                        for c in range(NC):
                            lhs = qxT[:, c * TB + ti * 128: c * TB + (ti + 1) * 128]
                            st, sp = (c == 0), (c == NC - 1)
                            ph.raw("tensor",
                                   lambda e, l=lhs, c=c, pp=pp, st=st, sp=sp, wb=wb:
                                   e.matmul(gps[:, pp * 512:(pp + 1) * 512], l,
                                            wg[wb][:, c * 512:(c + 1) * 512],
                                            start=st, stop=sp))
                            ph.raw("tensor",
                                   lambda e, l=lhs, c=c, pp=pp, st=st, sp=sp, wb=wb:
                                   e.matmul(ups[:, pp * 512:(pp + 1) * 512], l,
                                            wu[wb][:, c * 512:(c + 1) * 512],
                                            start=st, stop=sp))
                        pm = ph.inc_last("tensor", sPE)
                        if ti == NTB - 1:
                            pe_pair[(blk_i, jj)] = pm
                        ph.wait("scalar", sPE, pm)
                        if sil_free:
                            ph.wait("scalar", sDV, sil_free)
                        ac_si = ph.op("scalar", lambda e, pp=pp: e.activation(
                            silu_t[:, :], gps[:, pp * 512:(pp + 1) * 512], AF.Silu),
                            (sAC, 1))
                        ph.wait("vector", sAC, ac_si)
                        hmul[(blk_i, jj, ti)] = ph.op(
                            "vector", lambda e, pp=pp, ti=ti, jj=jj: e.tensor_tensor(
                                hbuf[:, ti * I + jj * 512: ti * I + (jj + 1) * 512],
                                silu_t[:, :], ups[:, pp * 512:(pp + 1) * 512], ALU.mult),
                            (sDV, 1))
                        sil_free = hmul[(blk_i, jj, ti)]
                tr_prev = 0
                for ti in range(NTB):
                    cq = 30 + 3 * ti
                    ph.wait("vector", sDV, hmul[(blk_i, NJ - 1, ti)])
                    ph.raw("vector", lambda e, ti=ti, cq=cq: e.reduce_max(
                        SC(cq), hbuf[:, ti * I:(ti + 1) * I], axis=AX.X,
                        apply_absolute_value=True))
                    ph.raw("vector", lambda e, cq=cq: e.tensor_scalar(
                        SC(cq + 1), SC(cq), 1.0 / 127.0, 1e-8, ALU.mult, ALU.max))
                    dv_r = ph.op("vector", lambda e, cq=cq: e.reciprocal(
                        SC(cq + 2), SC(cq + 1)), (sDV, 1))
                    ph.wait("scalar", sDV, dv_r)
                    if tr_prev:
                        ph.wait("scalar", sPE, tr_prev)
                    for sl in range((I + 1023) // 1024):
                        f0, f1 = sl * 1024, min((sl + 1) * 1024, I)
                        ph.raw("scalar", lambda e, ti=ti, f0=f0, f1=f1, cq=cq:
                               e.activation(
                            i32b[:, 0:f1 - f0], hbuf[:, ti * I + f0: ti * I + f1],
                            AF.Identity, scale=SC(cq + 2)))
                        ph.raw("scalar", lambda e, f0=f0, f1=f1, cq=cq: e.activation(
                            qh_t[:, f0:f1], i32b[:, 0:f1 - f0], AF.Identity,
                            scale=SC(cq + 1)))
                    ac_q = ph.mark("scalar", sAC)
                    ph.wait("tensor", sAC, ac_q)
                    for g in range(6):
                        c0, c1 = g * 8, min((g + 1) * 8, NCI)
                        ph.wait("tensor", sDV, sDV.n)
                        for c in range(c0, c1):
                            ph.raw("tensor", lambda e, c=c, c0=c0: e.transpose(
                                tps[:, (c - c0) * 128:(c - c0 + 1) * 128],
                                qh_t[:, c * 128:(c + 1) * 128], id16[:, :]))
                        pm = ph.inc_last("tensor", sPE)
                        ph.wait("vector", sPE, pm)
                        ph.op("vector", lambda e, ti=ti, c0=c0, c1=c1: e.tensor_copy(
                            hbuf[:, ti * I + c0 * 128: ti * I + c1 * 128],
                            tps[:, 0:(c1 - c0) * 128]), (sDV, 1))
                    tr_prev = pm
                qh_ready = sDV.n
                for ocu in range(16):
                    b2 = ocu % 2
                    guard = pe_down.get((blk_i, ocu - 2),
                                        pe_down.get((blk_i - 1, 14 + ocu), 0))
                    ph.wait("sync", sPE, guard)
                    w2_ready = load_w2(ocu, w2T[b2], sLw2[b2])
                    ph.wait("tensor", sLw2[b2], w2_ready)
                    ph.wait("tensor", sDV, qh_ready)
                    ou = blk_i * 16 + ocu
                    fb = ou % 3
                    if (ou - 3) >= 0 and (ou - 3) in fps_free:
                        ph.wait("tensor", sDV, fps_free[ou - 3])
                    for c2 in range(NCI):
                        rhs = bass.AP(hbuf, c2 * 128,
                                      [[NTB * I, 128], [I, NTB], [1, 128]])
                        ph.raw("tensor", lambda e, c2=c2, b2=b2, fb=fb,
                               rhs=rhs: e.matmul(
                            fps[fb][:, 0:512],
                            w2T[b2][:, c2 * 128:(c2 + 1) * 128],
                            rhs, start=(c2 == 0), stop=(c2 == NCI - 1)))
                    pmd = ph.inc_last("tensor", sPE)
                    pe_down[(blk_i, ocu)] = pmd
                    orow = ocu * 128
                    ab = ou % 2
                    ph.wait("vector", sPE, pmd)
                    ph.wait("vector", sSacc[ab])
                    dv_e = ph.op("vector", lambda e, fb=fb, ab=ab: e.tensor_copy(
                        accst[ab][:, :], fps[fb][:, 0:512]), (sDV, 1))
                    fps_free[ou] = dv_e
                    ph.wait("scalar", sDV, dv_e)
                    ph.dma("scalar", accT_d[orow:orow + 128, t0:t0 + TB],
                           accst[ab][:, :], sSacc[ab])
                sacc_counts[blk_i] = (sSacc[0].n, sSacc[1].n)
            if do_C:
                emit_C(NB - 1)
            ph.flush()
            esB.close()

    return nc


def _host_arrays(q_proj, k_proj, v_proj, o_proj, q_norm, k_norm, w_gate_up, w_down):
    f32 = np.float32
    arrs = {}
    arrs["qTw"] = np.ascontiguousarray(np.asarray(q_proj, f32).T)
    arrs["kTw"] = np.ascontiguousarray(np.asarray(k_proj, f32).T)
    arrs["vTw"] = np.ascontiguousarray(np.asarray(v_proj, f32).T)
    arrs["oTw"] = np.ascontiguousarray(np.asarray(o_proj, f32).T)
    arrs["w1n"] = np.ascontiguousarray(np.asarray(w_gate_up, f32))
    arrs["w2n"] = np.ascontiguousarray(np.asarray(w_down, f32))
    arrs["qnb"] = np.ascontiguousarray(
        np.broadcast_to(np.tile(np.asarray(q_norm, f32), NH)[None, :], (128, D)))
    arrs["knb"] = np.ascontiguousarray(
        np.broadcast_to(np.asarray(k_norm, f32)[None, :], (128, HD)))
    hd = np.float32(HD)
    idx = np.arange(128, dtype=f32)
    inv = (1.0 / np.power(np.float32(1000000.0), (2.0 * idx / hd), dtype=f32)).astype(f32)
    pos = np.arange(L, dtype=f32)
    fr = pos[:, None] * inv[None, :]
    c = np.cos(fr).astype(f32)
    s = np.sin(fr).astype(f32)
    arrs["ctab"] = np.ascontiguousarray(np.concatenate([c, c], axis=1))
    arrs["stab"] = np.ascontiguousarray(np.concatenate([-s, s], axis=1))
    m = np.zeros((128, 128), f32)
    m[np.triu_indices(128, 1)] = NEGBIG
    arrs["mskd"] = m
    arrs["id16"] = np.eye(128, dtype=BF)
    cstv = np.zeros((128, 8), f32)
    cstv[:, 0] = 1e-6
    cstv[:, 1] = 1e-5
    arrs["cst"] = cstv
    return arrs


def kernel(L0_attn_x, L0_attn_q_proj, L0_attn_k_proj, L0_attn_v_proj, L0_attn_o_proj,
           L0_attn_q_norm, L0_attn_k_norm, L0_ffn_w_gate_up, L0_ffn_w_down):
    n = 8
    x = np.asarray(L0_attn_x, np.float32)
    B, Lq, Dm = x.shape
    shared = _host_arrays(L0_attn_q_proj, L0_attn_k_proj, L0_attn_v_proj, L0_attn_o_proj,
                          L0_attn_q_norm, L0_attn_k_norm, L0_ffn_w_gate_up, L0_ffn_w_down)
    nc = build()
    in_maps = []
    for c in range(n):
        m = dict(shared)
        m["x_in"] = np.ascontiguousarray(x[2 * c:2 * c + 2].reshape(SEQ * L, D))
        in_maps.append(m)
    res = run_bass_kernel_spmd(nc, in_maps, core_ids=list(range(n)))
    out = np.empty((B, Lq, Dm), np.float32)
    for c in range(n):
        out[2 * c:2 * c + 2] = res.results[c]["y"].reshape(SEQ, L, D)
    return out


# version 09
# speedup vs baseline: 1.0277x; 1.0277x over previous
import os
import numpy as np
import ml_dtypes
import concourse.bass as bass
import concourse.mybir as mybir
from concourse.bass_utils import run_bass_kernel_spmd
from contextlib import ExitStack

F32 = mybir.dt.float32
BF16 = mybir.dt.bfloat16
I32 = mybir.dt.int32
AF = mybir.ActivationFunctionType
ALU = mybir.AluOpType
AX = mybir.AxisListType
BF = ml_dtypes.bfloat16

SEQ = 2
L = 2048
D = 2048
NH = 8
HD = 256
I = 5632
TT = 128
NTS = L // TT
NC = 16
NCI = I // 128
TB = 512
NB = SEQ * L // TB
NTB = TB // TT
NJ = I // 512
TL = SEQ * L
ENGS = ("sync", "tensor", "vector", "scalar", "gpsimd")
NEGBIG = -30000.0
ALLPH = ("A0", "W", "A1", "A2", "B", "C")


class Sem:
    def __init__(self, es, nc, name, queue=None):
        self.h = es.enter_context(nc.semaphore(name))
        self.n = 0
        self.queue = queue


DRAIN_ENGS = ("vector", "scalar", "gpsimd")


class Ph:
    def __init__(self, nc):
        self.nc = nc
        self.ops = {e: [] for e in ENGS}
        self.dma_sems = []
        self.drained = {e: True for e in ENGS}

    def raw(self, eng, fn):
        if eng in DRAIN_ENGS and not self.drained[eng]:
            self.ops[eng].append(lambda e: e.drain())
        self.drained[eng] = False
        self.ops[eng].append(fn)

    def op(self, eng, fn, inc=None):
        self.raw(eng, fn)
        if inc is None:
            return None
        sem, k = inc
        sem.n += k
        self.ops[eng].append(lambda e, h=sem.h, k=k: e.drain().then_inc(h, k))
        self.drained[eng] = True
        return sem.n

    def wait(self, eng, sem, val=None):
        v = sem.n if val is None else val
        if v > 0:
            self.ops[eng].append(lambda e, h=sem.h, v=v: e.wait_ge(h, v))

    def dma(self, q, dst, src, sem, transpose=False):
        assert sem.queue == q
        if sem not in self.dma_sems:
            self.dma_sems.append(sem)
        sem.n += 16
        if transpose:
            self.ops[q].append(lambda e, d=dst, s=src, h=sem.h:
                               e.dma_start(d, s, transpose=True).then_inc(h, 16))
        else:
            self.ops[q].append(lambda e, d=dst, s=src, h=sem.h:
                               e.dma_start(d, s).then_inc(h, 16))
        return sem.n

    def mark(self, eng, sem):
        sem.n += 1
        self.ops[eng].append(lambda e, h=sem.h: e.drain().then_inc(h, 1))
        self.drained[eng] = True
        return sem.n

    def inc_last(self, eng, sem, k=1):
        sem.n += k
        old = self.ops[eng][-1]
        self.ops[eng][-1] = lambda e, old=old, h=sem.h, k=k: old(e).then_inc(h, k)
        return sem.n

    def flush(self):
        for sem in self.dma_sems:
            self.wait(sem.queue, sem)
        self.dma_sems = []
        with self.nc.Block() as blk:
            for e in ENGS:
                lst = self.ops[e]
                if lst:
                    getattr(blk, e)(lambda eng, lst=lst: [f(eng) for f in lst])
        self.ops = {e: [] for e in ENGS}
        self.drained = {e: True for e in ENGS}


def build(debug=False, phases=ALLPH):
    nc = bass.Bass("TRN2", target_bir_lowering=False)

    x_in = nc.dram_tensor("x_in", [TL, D], F32, kind="ExternalInput").ap()
    qTw_h = nc.dram_tensor("qTw", [D, D], F32, kind="ExternalInput").ap()
    kTw_h = nc.dram_tensor("kTw", [D, HD], F32, kind="ExternalInput").ap()
    vTw_h = nc.dram_tensor("vTw", [D, HD], F32, kind="ExternalInput").ap()
    oTw_h = nc.dram_tensor("oTw", [D, D], F32, kind="ExternalInput").ap()
    w1_h = nc.dram_tensor("w1n", [2 * I, D], F32, kind="ExternalInput").ap()
    w2_h = nc.dram_tensor("w2n", [D, I], F32, kind="ExternalInput").ap()
    qnb_h = nc.dram_tensor("qnb", [128, D], F32, kind="ExternalInput").ap()
    knb_h = nc.dram_tensor("knb", [128, HD], F32, kind="ExternalInput").ap()
    ctab_h = nc.dram_tensor("ctab", [L, HD], F32, kind="ExternalInput").ap()
    stab_h = nc.dram_tensor("stab", [L, HD], F32, kind="ExternalInput").ap()
    mskd_h = nc.dram_tensor("mskd", [128, 128], F32, kind="ExternalInput").ap()
    id16_h = nc.dram_tensor("id16", [128, 128], BF16, kind="ExternalInput").ap()
    cst_h = nc.dram_tensor("cst", [128, 8], F32, kind="ExternalInput").ap()
    y_out = nc.dram_tensor("y", [TL, D], F32, kind="ExternalOutput").ap()

    dbg = "ExternalOutput" if debug else "Internal"
    xbf_d = nc.dram_tensor("xbf_d", [TL, D], BF16, kind=dbg).ap()
    qbf_d = nc.dram_tensor("qbf_d", [TL, D], BF16, kind=dbg).ap()
    kbf_d = nc.dram_tensor("kbf_d", [TL, HD], BF16, kind=dbg).ap()
    x1_d = nc.dram_tensor("x1_d", [TL, D], BF16, kind=dbg).ap()
    qx_d = nc.dram_tensor("qx_d", [TL, D], BF16, kind=dbg).ap()
    qw1_d = nc.dram_tensor("qw1_d", [D, 2 * I], BF16, kind=dbg).ap()
    qw2_d = nc.dram_tensor("qw2_d", [I, D], BF16, kind=dbg).ap()
    accT_d = nc.dram_tensor("accT_d", [D, TL], BF16, kind=dbg).ap()

    es = ExitStack()
    with es:
        sPE = Sem(es, nc, "sPE")
        sDV = Sem(es, nc, "sDV")
        sAC = Sem(es, nc, "sAC")
        sGP = Sem(es, nc, "sGP")

        id16 = es.enter_context(nc.sbuf_tensor("id16s", [128, 128], BF16))
        mskd = es.enter_context(nc.sbuf_tensor("mskds", [128, 128], F32))
        sca = es.enter_context(nc.sbuf_tensor("sca", [128, 64], F32))
        cst = es.enter_context(nc.sbuf_tensor("cst_s", [128, 8], F32))

        def CST(c):
            return cst[:, c:c + 1]

        def SC(c, n=1):
            return sca[:, c:c + n]

        ph = Ph(nc)

        sLc0 = Sem(es, nc, "sLc0", "sync")
        ph.dma("sync", id16[:, :], id16_h[:, :], sLc0)
        ph.dma("sync", mskd[:, :], mskd_h[:, :], sLc0)
        ph.dma("sync", cst[:, :], cst_h[:, :], sLc0)
        consts_ready = sLc0.n
        a0_sems = []
        if "A0" in phases:
            sLa = Sem(es, nc, "sLa", "gpsimd")
            a0_sems = [sLa]
            for it in range(TL // TT):
                ph.dma("gpsimd", xbf_d[it * TT:(it + 1) * TT, :],
                       x_in[it * TT:(it + 1) * TT, :], sLa)

        w_store_sems = []
        if "W" in phases:
            esW = ExitStack()
            NWB = 4
            wst = [esW.enter_context(nc.sbuf_tensor(f"wst{b}", [128, I], F32))
                   for b in range(NWB)]
            i32w = [esW.enter_context(nc.sbuf_tensor(f"i32w{b}", [128, I], I32))
                    for b in range(2)]
            qwst = [esW.enter_context(nc.sbuf_tensor(f"qwst{b}", [128, I], BF16))
                    for b in range(2)]
            stg = [esW.enter_context(nc.sbuf_tensor(f"stg{b}", [128, I], BF16))
                   for b in range(NWB)]
            pst = [esW.enter_context(nc.psum_tensor(f"pst{b}", [128, 2048], BF16))
                   for b in range(2)]
            sLw = [Sem(es, nc, f"sLw{b}", "sync") for b in range(NWB)]
            sSw = [Sem(es, nc, f"sSw{b}", "scalar") for b in range(NWB)]
            w_store_sems = sSw

            strips = [("w1", s, D, NC) for s in range(2 * I // 128)] + \
                     [("w2", s, I, NCI) for s in range(D // 128)]
            _wl = os.environ.get("K_W_STRIPS")
            if _wl:
                strips = strips[:int(_wl)]
            amax_done = {}
            rnd_done = {}
            dq_done = {}
            tr_done = {}
            ev_done = {}
            for si, (wn, s, wd, nch) in enumerate(strips):
                b = si % NWB
                b2 = si % 2
                cq = 4 + 3 * (si % 4)
                src_ = (w1_h[s * 128:(s + 1) * 128, :] if wn == "w1"
                        else w2_h[s * 128:(s + 1) * 128, :])
                if si >= NWB:
                    ph.wait("sync", sAC, rnd_done[si - NWB])
                ph.dma("sync", wst[b][:, 0:wd], src_, sLw[b])
                ph.wait("vector", sLw[b])
                ph.raw("vector", lambda e, b=b, wd=wd, cq=cq: e.reduce_max(
                    SC(cq), wst[b][:, 0:wd], axis=AX.X, apply_absolute_value=True))
                ph.raw("vector", lambda e, cq=cq: e.tensor_scalar(
                    SC(cq + 1), SC(cq), 1.0 / 127.0, 1e-8, ALU.mult, ALU.max))
                amax_done[si] = ph.op("vector", lambda e, cq=cq: e.reciprocal(
                    SC(cq + 2), SC(cq + 1)), (sDV, 1))
                ph.wait("scalar", sDV, amax_done[si])
                if si >= 2:
                    ph.wait("scalar", sGP, dq_done[si - 2])
                ph.raw("scalar", lambda e, b=b, b2=b2, wd=wd, cq=cq: e.activation(
                    i32w[b2][:, 0:wd], wst[b][:, 0:wd], AF.Identity, scale=SC(cq + 2)))
                rnd_done[si] = ph.inc_last("scalar", sAC)
                ph.wait("gpsimd", sAC, rnd_done[si])
                if si >= 2:
                    ph.wait("gpsimd", sPE, tr_done[si - 2])
                ph.raw("gpsimd", lambda e, b2=b2, wd=wd, cq=cq: e.tensor_scalar(
                    qwst[b2][:, 0:wd], i32w[b2][:, 0:wd], SC(cq + 1), None, ALU.mult))
                dq_done[si] = ph.mark("gpsimd", sGP)
                ph.wait("tensor", sGP, dq_done[si])
                ph.wait("tensor", sLc0, consts_ready)
                ph.wait("tensor", sSw[b])
                if si >= 2:
                    ph.wait("tensor", sDV, ev_done[si - 2])
                ngr = (nch + 15) // 16
                for g in range(ngr):
                    c0, c1 = g * 16, min((g + 1) * 16, nch)
                    if g > 0:
                        ph.wait("tensor", sDV, sDV.n)
                    for c in range(c0, c1):
                        ph.raw("tensor", lambda e, c=c, c0=c0, b2=b2: e.transpose(
                            pst[b2][:, (c - c0) * 128:(c - c0 + 1) * 128],
                            qwst[b2][:, c * 128:(c + 1) * 128], id16[:, :]))
                    pm = ph.inc_last("tensor", sPE)
                    ph.wait("vector", sPE, pm)
                    ph.op("vector", lambda e, b=b, b2=b2, c0=c0, c1=c1: e.tensor_copy(
                        stg[b][:, c0 * 128:c1 * 128], pst[b2][:, 0:(c1 - c0) * 128]),
                        (sDV, 1))
                tr_done[si] = pm
                ev_done[si] = sDV.n
                ph.wait("scalar", sDV, ev_done[si])
                nsp = 2 if wn == "w1" else 4
                csz = (nch + nsp - 1) // nsp
                dT, drows = (qw1_d, 2 * I) if wn == "w1" else (qw2_d, D)
                for sp in range(nsp):
                    c0, c1 = sp * csz, min((sp + 1) * csz, nch)
                    if c0 >= c1:
                        continue
                    srcap = bass.AP(stg[b], c0 * 128, [[I, 128], [128, c1 - c0], [1, 128]])
                    dstap = bass.AP(dT.tensor, c0 * 128 * drows + s * 128,
                                    [[drows, 128], [128 * drows, c1 - c0], [1, 128]])
                    ph.dma("scalar", dstap, srcap, sSw[b])
            ph.flush()
            esW.close()

        esAT = ExitStack()
        sqk_done = {}
        sSqk = None
        sx_done = {}
        if "A1" in phases or "A2" in phases:
            v_sb = esAT.enter_context(nc.sbuf_tensor("v_sb", [128, SEQ * NTS * HD], BF16))

        if "A1" in phases:
            esA1 = ExitStack()
            qTw = esA1.enter_context(nc.sbuf_tensor("qTwS", [128, NC * D], BF16))
            kTw = esA1.enter_context(nc.sbuf_tensor("kTwS", [128, NC * HD], BF16))
            vTw = esA1.enter_context(nc.sbuf_tensor("vTwS", [128, NC * HD], BF16))
            xT = [esA1.enter_context(nc.sbuf_tensor(f"xT{b}", [128, NC * 512], BF16))
                  for b in range(2)]
            qnb = esA1.enter_context(nc.sbuf_tensor("qnbS", [128, D], F32))
            knb = esA1.enter_context(nc.sbuf_tensor("knbS", [128, HD], F32))
            ctab = [esA1.enter_context(nc.sbuf_tensor(f"ctab{b}", [128, HD], F32))
                    for b in range(2)]
            stab = [esA1.enter_context(nc.sbuf_tensor(f"stab{b}", [128, HD], F32))
                    for b in range(2)]
            qhat = esA1.enter_context(nc.sbuf_tensor("qhat", [128, D], F32))
            rt1 = esA1.enter_context(nc.sbuf_tensor("rt1", [128, D], F32))
            rt2 = esA1.enter_context(nc.sbuf_tensor("rt2", [128, D], F32))
            qbf = esA1.enter_context(nc.sbuf_tensor("qbf", [128, D], BF16))
            khat = esA1.enter_context(nc.sbuf_tensor("khat", [128, HD], F32))
            kr1 = esA1.enter_context(nc.sbuf_tensor("kr1", [128, HD], F32))
            kr2 = esA1.enter_context(nc.sbuf_tensor("kr2", [128, HD], F32))
            kbf = esA1.enter_context(nc.sbuf_tensor("kbf", [128, HD], BF16))
            sqg = esA1.enter_context(nc.sbuf_tensor("sqg", [128, HD], F32))
            qps = esA1.enter_context(nc.psum_tensor("qps", [128, D], F32))
            kps = esA1.enter_context(nc.psum_tensor("kps", [128, 512], F32))
            vps = esA1.enter_context(nc.psum_tensor("vps", [128, 512], F32))
            sLwq = Sem(es, nc, "sLwq", "gpsimd")
            sLnb = Sem(es, nc, "sLnb", "sync")
            sLxT = [Sem(es, nc, f"sLxT{b}", "sync") for b in range(2)]
            sLtab = [Sem(es, nc, f"sLtab{b}", "scalar") for b in range(2)]
            sSqk = Sem(es, nc, "sSqk", "scalar")

            for c in range(NC):
                ph.dma("gpsimd", qTw[:, c * D:(c + 1) * D],
                       qTw_h[c * 128:(c + 1) * 128, :], sLwq)
                ph.dma("gpsimd", kTw[:, c * HD:(c + 1) * HD],
                       kTw_h[c * 128:(c + 1) * 128, :], sLwq)
                ph.dma("gpsimd", vTw[:, c * HD:(c + 1) * HD],
                       vTw_h[c * 128:(c + 1) * 128, :], sLwq)
            wload = sLwq.n
            ph.dma("sync", qnb[:, :], qnb_h[:, :], sLnb)
            ph.dma("sync", knb[:, :], knb_h[:, :], sLnb)
            nb_ready = sLnb.n

            NQ = 4
            pe_tile_done = {}
            psum_free = {}
            gp_done = {}
            dv_rope_done = {}
            for s_ in a0_sems:
                ph.wait("sync", s_)
            for qb_i in range(TL // (NQ * TT)):
                b = qb_i % 2
                sq = qb_i // 4
                q_in_seq = qb_i % 4
                if qb_i >= 2:
                    ph.wait("sync", sPE, pe_tile_done.get((qb_i - 2, NQ - 1), 0))
                for c in range(NC):
                    ph.dma("sync", xT[b][:, c * 512:(c + 1) * 512],
                           xbf_d[sq * L + q_in_seq * 512: sq * L + (q_in_seq + 1) * 512,
                                 c * 128:(c + 1) * 128], sLxT[b], transpose=True)
                xt_ready = sLxT[b].n
                for ti in range(NQ):
                    it = qb_i * NQ + ti
                    trow = it * TT
                    tb_ = it % 2
                    if it >= 2:
                        ph.wait("scalar", sGP, gp_done.get(it - 2, 0))
                        ph.wait("scalar", sDV, dv_rope_done.get(it - 2, 0))
                    ph.dma("scalar", ctab[tb_][:, :],
                           ctab_h[(it % NTS) * TT:((it % NTS) + 1) * TT, :], sLtab[tb_])
                    ph.dma("scalar", stab[tb_][:, :],
                           stab_h[(it % NTS) * TT:((it % NTS) + 1) * TT, :], sLtab[tb_])
                    tabs_ready = sLtab[tb_].n
                    ph.wait("tensor", sLwq, wload)
                    ph.wait("tensor", sLxT[b], xt_ready)
                    if it >= 1:
                        ph.wait("tensor", sDV, psum_free[it - 1])
                    for c in range(NC):
                        lhs = xT[b][:, c * 512 + ti * 128: c * 512 + (ti + 1) * 128]
                        st, sp = (c == 0), (c == NC - 1)
                        for ob in range(4):
                            ph.raw("tensor", lambda e, l=lhs, ob=ob, st=st, sp=sp, c=c:
                                   e.matmul(qps[:, ob * 512:(ob + 1) * 512], l,
                                            qTw[:, c * D + ob * 512: c * D + (ob + 1) * 512],
                                            start=st, stop=sp))
                        ph.raw("tensor", lambda e, l=lhs, st=st, sp=sp, c=c: e.matmul(
                            kps[:, 0:HD], l, kTw[:, c * HD:(c + 1) * HD],
                            start=st, stop=sp))
                        ph.raw("tensor", lambda e, l=lhs, st=st, sp=sp, c=c: e.matmul(
                            vps[:, 0:HD], l, vTw[:, c * HD:(c + 1) * HD],
                            start=st, stop=sp))
                    pe_tile_done[(qb_i, ti)] = ph.inc_last("tensor", sPE)
                    ph.wait("scalar", sPE, pe_tile_done[(qb_i, ti)])
                    for h in range(NH):
                        ph.raw("scalar", lambda e, h=h: e.activation(
                            sqg[:, :], qps[:, h * HD:(h + 1) * HD], AF.Square,
                            accum_out=SC(12 + h)))
                    ph.raw("scalar", lambda e: e.activation(
                        sqg[:, :], kps[:, 0:HD], AF.Square, accum_out=SC(20)))
                    ph.wait("scalar", sLc0, consts_ready)
                    ph.raw("scalar", lambda e: e.activation(
                        SC(21, 9), SC(12, 9), AF.Ln, scale=1.0 / 256.0, bias=CST(0)))
                    ph.raw("scalar", lambda e: e.activation(
                        SC(21, 9), SC(21, 9), AF.Identity, scale=-0.5))
                    ac_rms = ph.op("scalar", lambda e: e.activation(
                        SC(21, 9), SC(21, 9), AF.Exp), (sAC, 1))
                    ph.wait("vector", sAC, ac_rms)
                    ph.wait("vector", sLnb, nb_ready)
                    for h in range(NH):
                        ph.raw("vector", lambda e, h=h: e.scalar_tensor_tensor(
                            qhat[:, h * HD:(h + 1) * HD], qps[:, h * HD:(h + 1) * HD],
                            SC(21 + h), qnb[:, h * HD:(h + 1) * HD], ALU.mult, ALU.mult))
                    ph.raw("vector", lambda e: e.scalar_tensor_tensor(
                        khat[:, :], kps[:, 0:HD], SC(29), knb[:, :], ALU.mult, ALU.mult))
                    psum_free[it] = ph.op("vector", lambda e, it=it: e.tensor_copy(
                        v_sb[:, it * HD:(it + 1) * HD], vps[:, 0:HD]), (sDV, 1))
                    ct, st_ = ctab[tb_], stab[tb_]
                    ph.wait("gpsimd", sDV, psum_free[it])
                    ph.wait("gpsimd", sLtab[tb_], tabs_ready)
                    ph.wait("vector", sLtab[tb_], tabs_ready)
                    for h in range(NH):
                        q0 = h * HD
                        ph.raw("vector", lambda e, q0=q0, ct=ct: e.tensor_tensor(
                            rt1[:, q0:q0 + HD], qhat[:, q0:q0 + HD], ct[:, :], ALU.mult))
                        ph.raw("gpsimd", lambda e, q0=q0, st_=st_: e.tensor_tensor(
                            rt2[:, q0:q0 + 128], qhat[:, q0 + 128:q0 + HD],
                            st_[:, 0:128], ALU.mult))
                        ph.raw("gpsimd", lambda e, q0=q0, st_=st_: e.tensor_tensor(
                            rt2[:, q0 + 128:q0 + HD], qhat[:, q0:q0 + 128],
                            st_[:, 128:HD], ALU.mult))
                    gp_done[it] = ph.mark("gpsimd", sGP)
                    ph.wait("vector", sGP, gp_done[it])
                    ph.wait("vector", sSqk)
                    ph.raw("vector", lambda e: e.tensor_tensor(
                        qbf[:, :], rt1[:, :], rt2[:, :], ALU.add))
                    ph.raw("vector", lambda e, ct=ct: e.tensor_tensor(
                        kr1[:, :], khat[:, :], ct[:, :], ALU.mult))
                    ph.raw("vector", lambda e, st_=st_: e.tensor_tensor(
                        kr2[:, 0:128], khat[:, 128:HD], st_[:, 0:128], ALU.mult))
                    ph.raw("vector", lambda e, st_=st_: e.tensor_tensor(
                        kr2[:, 128:HD], khat[:, 0:128], st_[:, 128:HD], ALU.mult))
                    dv_rope_done[it] = ph.op("vector", lambda e: e.tensor_tensor(
                        kbf[:, :], kr1[:, :], kr2[:, :], ALU.add), (sDV, 1))
                    ph.wait("scalar", sDV, dv_rope_done[it])
                    ph.dma("scalar", qbf_d[trow:trow + TT, :], qbf[:, :], sSqk)
                    ph.dma("scalar", kbf_d[trow:trow + TT, :], kbf[:, :], sSqk)
                    sqk_done[it] = sSqk.n
            ph.flush()
            esA1.close()

        if "A2" in phases:
            esA2 = ExitStack()
            oTw = esA2.enter_context(nc.sbuf_tensor("oTwS", [128, NC * D], BF16))
            qT = esA2.enter_context(nc.sbuf_tensor("qTS", [128, NC * 1024], BF16))
            kT = esA2.enter_context(nc.sbuf_tensor("kTS", [128, 2 * L], BF16))
            w_sb = [esA2.enter_context(nc.sbuf_tensor(f"w_sb{b}", [128, 1024], BF16))
                    for b in range(2)]
            wT_sb = [esA2.enter_context(nc.sbuf_tensor(f"wT_sb{b}", [128, 1024], BF16))
                     for b in range(2)]
            ctxT = esA2.enter_context(nc.sbuf_tensor("ctxT", [128, D], BF16))
            x_sb = [esA2.enter_context(nc.sbuf_tensor(f"x_sb{b}", [128, D], F32))
                    for b in range(2)]
            y_sb = esA2.enter_context(nc.sbuf_tensor("y_sb", [128, D], F32))
            x1_sb = esA2.enter_context(nc.sbuf_tensor("x1_sb", [128, D], F32))
            i32a = esA2.enter_context(nc.sbuf_tensor("i32a", [128, D], I32))
            qx_sb = esA2.enter_context(nc.sbuf_tensor("qx_sb", [128, D], BF16))
            sqga = esA2.enter_context(nc.sbuf_tensor("sqga", [128, D], F32))
            scps = [esA2.enter_context(nc.psum_tensor(f"scps{b}", [128, 1024], F32))
                    for b in range(2)]
            psw = [esA2.enter_context(nc.psum_tensor(f"psw{b}", [128, 1024], BF16))
                   for b in range(2)]
            ctxp = [esA2.enter_context(nc.psum_tensor(f"ctxp{b}", [128, 512], F32))
                    for b in range(2)]
            sLo = Sem(es, nc, "sLo", "gpsimd")
            sLkT = Sem(es, nc, "sLkT", "sync")
            sLqT = Sem(es, nc, "sLqT", "sync")
            sLx = [Sem(es, nc, f"sLx{b}", "sync") for b in range(2)]
            sSx = Sem(es, nc, "sSx", "scalar")
            sSx1 = Sem(es, nc, "sSx1", "gpsimd")

            for c in range(NC):
                ph.dma("gpsimd", oTw[:, c * D:(c + 1) * D],
                       oTw_h[c * 128:(c + 1) * 128, :], sLo)
            oload = sLo.n

            uix = 0
            sc_free = {0: None, 1: None}
            psw_free = {0: 0, 1: 0}
            wsb_free = {0: 0, 1: 0}
            wt_free = {0: 0, 1: 0}
            xfree = {}
            last_sc_pm = 0
            kT_free = 0
            qT_free = 0
            for sq in range(SEQ):
                if sqk_done:
                    ph.wait("sync", sSqk, sqk_done[sq * NTS + NTS - 1])
                if sq > 0:
                    ph.wait("sync", sPE, kT_free)
                for c2 in range(2):
                    ph.dma("sync", kT[:, c2 * L:(c2 + 1) * L],
                           kbf_d[sq * L:(sq + 1) * L, c2 * 128:(c2 + 1) * 128], sLkT,
                           transpose=True)
                kt_ready = sLkT.n
                for hf in range(2):
                    if sq + hf > 0:
                        ph.wait("sync", sPE, qT_free)
                    for c in range(NC):
                        ph.dma("sync", qT[:, c * 1024:(c + 1) * 1024],
                               qbf_d[sq * L + hf * 1024: sq * L + (hf + 1) * 1024,
                                     c * 128:(c + 1) * 128], sLqT, transpose=True)
                    qt_ready = sLqT.n
                    for ti in range(8):
                        i = hf * 8 + ti
                        it = sq * NTS + i
                        trow = it * TT
                        Lk = (i + 1) * 128
                        nparts = (Lk + 1023) // 1024
                        xb = it % 2
                        if it >= 2:
                            ph.wait("sync", sDV, xfree.get(it - 2, 0))
                        ph.dma("sync", x_sb[xb][:, :], x_in[trow:trow + TT, :], sLx[xb])
                        x_ready = sLx[xb].n
                        dv_cx = 0
                        for h in range(NH):
                            B0 = 8 * (uix % 4)
                            uix += 1
                            qts = [qT[:, c_ * 1024 + ti * 128: c_ * 1024 + (ti + 1) * 128]
                                   for c_ in (2 * h, 2 * h + 1)]
                            pinfo = []
                            for p in range(nparts):
                                par = p if nparts == 2 else (uix % 2)
                                tk0 = p * 1024
                                tkw = min(1024, Lk - tk0)
                                fullw = min(tkw, i * 128 - tk0)
                                has_diag = (tk0 + tkw > i * 128)
                                n5 = fullw // 512
                                segs = [(s5 * 512, 512) for s5 in range(n5)]
                                if fullw > n5 * 512:
                                    segs.append((n5 * 512, fullw - n5 * 512))
                                if has_diag:
                                    segs.append((fullw, 128))
                                if h == 0 and p == 0:
                                    ph.wait("tensor", sLqT, qt_ready)
                                    ph.wait("tensor", sLkT, kt_ready)
                                if sc_free[par] is not None:
                                    sem_, v_ = sc_free[par]
                                    ph.wait("tensor", sem_, v_)
                                for o0, wseg in segs:
                                    for cc in range(2):
                                        ph.raw("tensor",
                                               lambda e, q_=qts[cc], cc=cc, o0=o0,
                                               wseg=wseg, par=par, tk0=tk0: e.matmul(
                                            scps[par][:, o0:o0 + wseg], q_,
                                            kT[:, cc * L + tk0 + o0:
                                               cc * L + tk0 + o0 + wseg],
                                            start=(cc == 0), stop=(cc == 1)))
                                pm_sc = ph.inc_last("tensor", sPE)
                                last_sc_pm = pm_sc
                                ph.wait("vector", sPE, pm_sc)
                                if has_diag:
                                    ph.raw("vector", lambda e, par=par, do=fullw:
                                           e.tensor_tensor(
                                        scps[par][:, do:do + 128],
                                        scps[par][:, do:do + 128], mskd[:, :],
                                        ALU.add))
                                mcol = B0 + 5 + p
                                bcol = B0 if p == 0 else B0 + 7
                                ph.raw("vector", lambda e, par=par, tkw=tkw, mcol=mcol:
                                       e.reduce_max(
                                    SC(mcol), scps[par][:, 0:tkw], axis=AX.X,
                                    negate=True))
                                dv_b = ph.op("vector", lambda e, mcol=mcol, bcol=bcol:
                                             e.tensor_scalar(
                                    SC(bcol), SC(mcol), 1.0 / 16.0, None, ALU.mult),
                                    (sDV, 1))
                                ph.wait("scalar", sDV, dv_b)
                                ph.wait("scalar", sPE, wsb_free[par])
                                scol = B0 + 2 + p
                                ac_e = ph.op("scalar", lambda e, par=par, tkw=tkw,
                                             bcol=bcol, scol=scol: e.activation(
                                    w_sb[par][:, 0:tkw], scps[par][:, 0:tkw], AF.Exp,
                                    bias=SC(bcol), scale=1.0 / 16.0,
                                    accum_out=SC(scol)), (sAC, 1))
                                sc_free[par] = (sAC, ac_e)
                                pinfo.append((par, tk0, tkw, scol, ac_e))
                            ph.wait("vector", sAC, pinfo[-1][4])
                            if nparts == 1:
                                (par, tk0, tkw, scol, _) = pinfo[0]
                                ph.raw("vector", lambda e, B0=B0, scol=scol:
                                       e.reciprocal(SC(B0 + 1), SC(scol)))
                                norm_done = {par: None}
                                gcol = {par: B0 + 1}
                            else:
                                ph.raw("vector", lambda e, B0=B0: e.tensor_tensor(
                                    SC(B0 + 7), SC(B0 + 5), SC(B0 + 6), ALU.min))
                                ph.raw("vector", lambda e, B0=B0: e.tensor_scalar(
                                    SC(B0 + 5, 2), SC(B0 + 5, 2), SC(B0 + 7), None,
                                    ALU.subtract))
                                dv_f = ph.op("vector", lambda e, B0=B0:
                                             e.tensor_scalar(
                                    SC(B0 + 5, 2), SC(B0 + 5, 2), -1.0 / 16.0, None,
                                    ALU.mult), (sDV, 1))
                                ph.wait("scalar", sDV, dv_f)
                                ac_f = ph.op("scalar", lambda e, B0=B0: e.activation(
                                    SC(B0 + 5, 2), SC(B0 + 5, 2), AF.Exp), (sAC, 1))
                                ph.wait("vector", sAC, ac_f)
                                ph.raw("vector", lambda e, B0=B0: e.tensor_tensor(
                                    SC(B0 + 2, 2), SC(B0 + 2, 2), SC(B0 + 5, 2),
                                    ALU.mult))
                                ph.raw("vector", lambda e, B0=B0: e.reduce_sum(
                                    SC(B0 + 4), SC(B0 + 2, 2), axis=AX.X))
                                ph.raw("vector", lambda e, B0=B0: e.reciprocal(
                                    SC(B0 + 1), SC(B0 + 4)))
                                ph.raw("vector", lambda e, B0=B0: e.tensor_scalar(
                                    SC(B0 + 5, 2), SC(B0 + 5, 2), SC(B0 + 1), None,
                                    ALU.mult))
                                norm_done = {}
                                gcol = {pinfo[0][0]: B0 + 5, pinfo[1][0]: B0 + 6}
                            nch_tot = (Lk + 127) // 128
                            done_ch = 0
                            for pi, (par, tk0, tkw, scol, _) in enumerate(pinfo):
                                nch = (tkw + 127) // 128
                                dv_n = ph.op("vector", lambda e, par=par, tkw=tkw,
                                             g=gcol[par]: e.tensor_scalar(
                                    w_sb[par][:, 0:tkw], w_sb[par][:, 0:tkw],
                                    SC(g), None, ALU.mult), (sDV, 1))
                                ph.wait("tensor", sDV, dv_n)
                                ph.wait("tensor", sDV, psw_free[par])
                                for j in range(nch):
                                    ph.raw("tensor", lambda e, par=par, j=j:
                                           e.transpose(
                                        psw[par][:, j * 128:(j + 1) * 128],
                                        w_sb[par][:, j * 128:(j + 1) * 128],
                                        id16[:, :]))
                                pm_tr = ph.inc_last("tensor", sPE)
                                wsb_free[par] = pm_tr
                                ph.wait("vector", sPE, pm_tr)
                                ph.wait("vector", sPE, wt_free[par])
                                dv_ev = ph.op("vector", lambda e, par=par, tkw=tkw:
                                              e.tensor_copy(
                                    wT_sb[par][:, 0:tkw], psw[par][:, 0:tkw]),
                                    (sDV, 1))
                                psw_free[par] = dv_ev
                                ph.wait("tensor", sDV, dv_ev)
                                for half in range(2):
                                    for j in range(nch):
                                        ph.raw("tensor",
                                               lambda e, half=half, j=j, sq=sq,
                                               tk0=tk0, par=par,
                                               st=(done_ch == 0 and j == 0),
                                               sp=(done_ch + nch == nch_tot
                                                   and j == nch - 1): e.matmul(
                                            ctxp[half][:, 0:128],
                                            v_sb[:, (sq * NTS + tk0 // 128 + j) * HD
                                                 + half * 128:
                                                 (sq * NTS + tk0 // 128 + j) * HD
                                                 + half * 128 + 128],
                                            wT_sb[par][:, j * 128:(j + 1) * 128],
                                            start=st, stop=sp))
                                pm_cx = ph.inc_last("tensor", sPE)
                                wt_free[par] = pm_cx
                                done_ch += nch
                            ph.wait("vector", sPE, pm_cx)
                            for half in range(2):
                                ph.raw("vector", lambda e, half=half, h=h:
                                       e.tensor_copy(
                                    ctxT[:, (2 * h + half) * 128:
                                         (2 * h + half + 1) * 128],
                                    ctxp[half][:, 0:128]))
                            dv_cx = ph.mark("vector", sDV)
                        ph.wait("tensor", sDV, dv_cx)
                        ph.wait("tensor", sLo, oload)
                        for par in range(2):
                            if sc_free[par] is not None:
                                sem_, v_ = sc_free[par]
                                ph.wait("tensor", sem_, v_)
                        for c in range(NC):
                            st, sp = (c == 0), (c == NC - 1)
                            for ob in range(4):
                                ph.raw("tensor", lambda e, c=c, ob=ob, st=st, sp=sp:
                                       e.matmul(
                                    scps[ob // 2][:, (ob % 2) * 512:
                                                  (ob % 2 + 1) * 512],
                                    ctxT[:, c * 128:(c + 1) * 128],
                                    oTw[:, c * D + ob * 512: c * D + (ob + 1) * 512],
                                    start=st, stop=sp))
                        pm_o = ph.inc_last("tensor", sPE)
                        ph.wait("vector", sPE, pm_o)
                        ph.wait("vector", sLx[xb], x_ready)
                        ph.raw("vector", lambda e, xb=xb: e.tensor_tensor(
                            y_sb[:, 0:1024], scps[0][:, 0:1024],
                            x_sb[xb][:, 0:1024], ALU.add))
                        dv_y2 = ph.op("vector", lambda e, xb=xb: e.tensor_tensor(
                            y_sb[:, 1024:2048], scps[1][:, 0:1024],
                            x_sb[xb][:, 1024:2048], ALU.add), (sDV, 1))
                        xfree[it] = dv_y2
                        sc_free = {0: (sDV, dv_y2), 1: (sDV, dv_y2)}
                        ph.wait("scalar", sDV, dv_y2)
                        ac_sq = ph.op("scalar", lambda e: e.activation(
                            sqga[:, :], y_sb[:, :], AF.Square, accum_out=SC(39)),
                            (sAC, 1))
                        ph.raw("vector", lambda e: e.reduce_sum(SC(40), y_sb[:, :],
                                                                axis=AX.X))
                        ph.raw("vector", lambda e: e.tensor_scalar(
                            SC(40), SC(40), 1.0 / D, None, ALU.mult))
                        ph.raw("vector", lambda e: e.tensor_tensor(
                            SC(41), SC(40), SC(40), ALU.mult))
                        ph.wait("vector", sAC, ac_sq)
                        dv_var = ph.op("vector", lambda e: e.scalar_tensor_tensor(
                            SC(42), SC(39), 1.0 / D, SC(41), ALU.mult, ALU.subtract),
                            (sDV, 1))
                        ph.wait("scalar", sDV, dv_var)
                        ph.raw("scalar", lambda e: e.activation(
                            SC(43), SC(42), AF.Ln, bias=CST(1)))
                        ph.raw("scalar", lambda e: e.activation(
                            SC(43), SC(43), AF.Identity, scale=-0.5))
                        ac_rstd = ph.op("scalar", lambda e: e.activation(
                            SC(43), SC(43), AF.Exp), (sAC, 1))
                        ph.wait("vector", sAC, ac_rstd)
                        ph.wait("vector", sSx)
                        ph.wait("vector", sSx1)
                        ph.raw("vector", lambda e: e.tensor_scalar(
                            x1_sb[:, :], y_sb[:, :], SC(40), SC(43),
                            ALU.subtract, ALU.mult))
                        ph.raw("vector", lambda e: e.reduce_max(
                            SC(44), x1_sb[:, :], axis=AX.X, apply_absolute_value=True))
                        ph.raw("vector", lambda e: e.tensor_scalar(
                            SC(45), SC(44), 1.0 / 127.0, 1e-8, ALU.mult, ALU.max))
                        dv_q = ph.op("vector", lambda e: e.reciprocal(SC(46), SC(45)),
                                     (sDV, 1))
                        ph.wait("scalar", sDV, dv_q)
                        ph.raw("scalar", lambda e: e.activation(
                            i32a[:, :], x1_sb[:, :], AF.Identity, scale=SC(46)))
                        epi = ph.op("scalar", lambda e: e.activation(
                            qx_sb[:, :], i32a[:, :], AF.Identity, scale=SC(45)),
                            (sAC, 1))
                        ph.wait("scalar", sAC, epi)
                        ph.wait("gpsimd", sAC, epi)
                        ph.dma("gpsimd", x1_d[trow:trow + TT, :], x1_sb[:, :], sSx1)
                        ph.dma("scalar", qx_d[trow:trow + TT, :], qx_sb[:, :], sSx)
                        sx_done[it] = (sSx.n, sSx1.n)
                    qT_free = last_sc_pm
                kT_free = last_sc_pm
            ph.flush()
            esA2.close()
        esAT.close()

        if "B" in phases:
            do_C = "C" in phases
            esB = ExitStack()
            qxT = esB.enter_context(nc.sbuf_tensor("qxT", [128, NC * TB], BF16))
            hbuf = esB.enter_context(nc.sbuf_tensor("hbuf", [128, NTB * I], BF16))
            wg = [esB.enter_context(nc.sbuf_tensor(f"wg{b}", [128, NC * 512], BF16))
                  for b in range(2)]
            wu = [esB.enter_context(nc.sbuf_tensor(f"wu{b}", [128, NC * 512], BF16))
                  for b in range(2)]
            w2T = [esB.enter_context(nc.sbuf_tensor(f"w2T{b}", [128, NCI * 128], BF16))
                   for b in range(2)]
            silu_t = esB.enter_context(nc.sbuf_tensor("silu_t", [128, 512], F32))
            qh_t = esB.enter_context(nc.sbuf_tensor("qh_t", [128, I], BF16))
            i32b = esB.enter_context(nc.sbuf_tensor("i32b", [128, 1024], I32))
            accst = [esB.enter_context(nc.sbuf_tensor(f"accst{b}", [128, 512], BF16))
                     for b in range(2)]
            gps = esB.enter_context(nc.psum_tensor("gps", [128, 1024], F32))
            ups = esB.enter_context(nc.psum_tensor("ups", [128, 1024], F32))
            tps = esB.enter_context(nc.psum_tensor("tps", [128, 1024], BF16))
            fps = [esB.enter_context(nc.psum_tensor(f"fps{b}", [128, 512], F32))
                   for b in range(3)]
            sLqxT = Sem(es, nc, "sLqxT", "sync")
            sLw1 = [Sem(es, nc, f"sLw1{b}", "sync") for b in range(2)]
            sLw2 = [Sem(es, nc, f"sLw2{b}", "sync") for b in range(2)]
            sSacc = [Sem(es, nc, f"sSacc{b}", "scalar") for b in range(2)]
            if do_C:
                acc_c = [esB.enter_context(nc.sbuf_tensor(f"acc_c{b}", [128, D], BF16))
                         for b in range(2)]
                x1c = [esB.enter_context(nc.sbuf_tensor(f"x1c{b}", [128, D], BF16))
                       for b in range(2)]
                yc = esB.enter_context(nc.sbuf_tensor("yc", [128, D], F32))
                yo = esB.enter_context(nc.sbuf_tensor("yo", [128, D], F32))
                sLcc = [Sem(es, nc, f"sLcc{b}", "sync") for b in range(2)]
                sLx1c = [Sem(es, nc, f"sLx1c{b}", "gpsimd") for b in range(2)]
                sSy = Sem(es, nc, "sSy", "scalar")
            add_done = {}
            sq_doneC = {}
            sacc_counts = {}

            def emit_C(blk):
                sc0, sc1 = sacc_counts[blk]
                ph.wait("sync", sSacc[0], sc0)
                ph.wait("sync", sSacc[1], sc1)
                for k2 in range(NTB):
                    it = blk * NTB + k2
                    b = it % 2
                    trow = it * TT
                    if it >= 2:
                        ph.wait("sync", sAC, sq_doneC[it - 2])
                        ph.wait("gpsimd", sAC, sq_doneC[it - 2])
                    if sx_done:
                        ph.wait("gpsimd", sSx1, sx_done[it][1])
                    ph.dma("sync", acc_c[b][:, :], accT_d[:, trow:trow + TT],
                           sLcc[b], transpose=True)
                    ph.dma("gpsimd", x1c[b][:, :], x1_d[trow:trow + TT, :],
                           sLx1c[b])
                    ph.wait("vector", sLcc[b])
                    ph.wait("vector", sLx1c[b])
                    ph.wait("vector", sSy)
                    add_done[it] = ph.op("vector", lambda e, b=b: e.tensor_tensor(
                        yc[:, :], acc_c[b][:, :], x1c[b][:, :], ALU.add), (sDV, 1))
                    ph.wait("scalar", sDV, add_done[it])
                    ac_sq = ph.op("scalar", lambda e, b=b: e.activation(
                        acc_c[b][:, :], yc[:, :], AF.Square, accum_out=SC(7)),
                        (sAC, 1))
                    sq_doneC[it] = ac_sq
                    ph.raw("vector", lambda e: e.reduce_sum(SC(8), yc[:, :],
                                                            axis=AX.X))
                    ph.raw("vector", lambda e: e.tensor_scalar(
                        SC(8), SC(8), 1.0 / D, None, ALU.mult))
                    ph.raw("vector", lambda e: e.tensor_tensor(SC(9), SC(8), SC(8),
                                                               ALU.mult))
                    ph.wait("vector", sAC, ac_sq)
                    dv_var = ph.op("vector", lambda e: e.scalar_tensor_tensor(
                        SC(10), SC(7), 1.0 / D, SC(9), ALU.mult, ALU.subtract),
                        (sDV, 1))
                    ph.wait("scalar", sDV, dv_var)
                    ph.wait("scalar", sLc0, consts_ready)
                    ph.raw("scalar", lambda e: e.activation(SC(11), SC(10), AF.Ln,
                                                            bias=CST(1)))
                    ph.raw("scalar", lambda e: e.activation(
                        SC(11), SC(11), AF.Identity, scale=-0.5))
                    ac_r = ph.op("scalar", lambda e: e.activation(
                        SC(11), SC(11), AF.Exp), (sAC, 1))
                    ph.wait("vector", sAC, ac_r)
                    dv_y = ph.op("vector", lambda e: e.tensor_scalar(
                        yo[:, :], yc[:, :], SC(8), SC(11), ALU.subtract, ALU.mult),
                        (sDV, 1))
                    ph.wait("scalar", sDV, dv_y)
                    ph.dma("scalar", y_out[trow:trow + TT, :], yo[:, :], sSy)

            def load_w1(jj, bg, bu, sem):
                for buf, co in ((bg, jj * 512), (bu, I + jj * 512)):
                    for sp in range(2):
                        c0 = sp * 8
                        srcap = bass.AP(qw1_d.tensor, (c0 * 128) * (2 * I) + co,
                                        [[2 * I, 128], [128 * 2 * I, 8], [1, 512]])
                        dstap = bass.AP(buf, c0 * 512,
                                        [[NC * 512, 128], [512, 8], [1, 512]])
                        ph.dma("sync", dstap, srcap, sem)
                return sem.n

            def load_w2(ocu, buf, sem):
                for sp in range(6):
                    c0, c1 = sp * 8, min((sp + 1) * 8, NCI)
                    if c0 >= c1:
                        break
                    srcap = bass.AP(qw2_d.tensor, (c0 * 128) * D + ocu * 128,
                                    [[D, 128], [128 * D, c1 - c0], [1, 128]])
                    dstap = bass.AP(buf, c0 * 128,
                                    [[NCI * 128, 128], [128, c1 - c0], [1, 128]])
                    ph.dma("sync", dstap, srcap, sem)
                return sem.n

            pe_pair = {}
            pe_down = {}
            hmul = {}
            fps_free = {}
            sil_free = 0
            for s_ in w_store_sems:
                ph.wait("sync", s_)
            for blk_i in range(NB):
                t0 = blk_i * TB
                ph.wait("sync", sPE, pe_pair.get((blk_i - 1, NJ - 1), 0))
                if sx_done:
                    ph.wait("sync", sSx, sx_done[blk_i * NTB + NTB - 1][0])
                for c in range(NC):
                    ph.dma("sync", qxT[:, c * TB:(c + 1) * TB],
                           qx_d[t0:t0 + TB, c * 128:(c + 1) * 128], sLqxT,
                           transpose=True)
                qxt_ready = sLqxT.n
                for jj in range(NJ):
                    wb = jj % 2
                    guard = pe_pair.get((blk_i, jj - 2),
                                        pe_pair.get((blk_i - 1, NJ - 2 + jj), 0))
                    ph.wait("sync", sPE, guard)
                    w_ready = load_w1(jj, wg[wb], wu[wb], sLw1[wb])
                    for ti in range(NTB):
                        pp = ti % 2
                        ph.wait("tensor", sLw1[wb], w_ready)
                        ph.wait("tensor", sLqxT, qxt_ready)
                        key = ((blk_i, jj, ti - 2) if ti >= 2
                               else (blk_i, jj - 1, NTB - 2 + ti))
                        ph.wait("tensor", sDV, hmul.get(key, 0))
                        for c in range(NC):
                            lhs = qxT[:, c * TB + ti * 128: c * TB + (ti + 1) * 128]
                            st, sp = (c == 0), (c == NC - 1)
                            ph.raw("tensor",
                                   lambda e, l=lhs, c=c, pp=pp, st=st, sp=sp, wb=wb:
                                   e.matmul(gps[:, pp * 512:(pp + 1) * 512], l,
                                            wg[wb][:, c * 512:(c + 1) * 512],
                                            start=st, stop=sp))
                            ph.raw("tensor",
                                   lambda e, l=lhs, c=c, pp=pp, st=st, sp=sp, wb=wb:
                                   e.matmul(ups[:, pp * 512:(pp + 1) * 512], l,
                                            wu[wb][:, c * 512:(c + 1) * 512],
                                            start=st, stop=sp))
                        pm = ph.inc_last("tensor", sPE)
                        if ti == NTB - 1:
                            pe_pair[(blk_i, jj)] = pm
                        ph.wait("scalar", sPE, pm)
                        if sil_free:
                            ph.wait("scalar", sDV, sil_free)
                        ac_si = ph.op("scalar", lambda e, pp=pp: e.activation(
                            silu_t[:, :], gps[:, pp * 512:(pp + 1) * 512], AF.Silu),
                            (sAC, 1))
                        ph.wait("vector", sAC, ac_si)
                        hmul[(blk_i, jj, ti)] = ph.op(
                            "vector", lambda e, pp=pp, ti=ti, jj=jj: e.tensor_tensor(
                                hbuf[:, ti * I + jj * 512: ti * I + (jj + 1) * 512],
                                silu_t[:, :], ups[:, pp * 512:(pp + 1) * 512], ALU.mult),
                            (sDV, 1))
                        sil_free = hmul[(blk_i, jj, ti)]
                if do_C and blk_i >= 1:
                    emit_C(blk_i - 1)
                tr_prev = 0
                tps_ev = 0
                for ti in range(NTB):
                    cq = 30 + 3 * ti
                    ph.wait("vector", sDV, hmul[(blk_i, NJ - 1, ti)])
                    ph.raw("vector", lambda e, ti=ti, cq=cq: e.reduce_max(
                        SC(cq), hbuf[:, ti * I:(ti + 1) * I], axis=AX.X,
                        apply_absolute_value=True))
                    ph.raw("vector", lambda e, cq=cq: e.tensor_scalar(
                        SC(cq + 1), SC(cq), 1.0 / 127.0, 1e-8, ALU.mult, ALU.max))
                    dv_r = ph.op("vector", lambda e, cq=cq: e.reciprocal(
                        SC(cq + 2), SC(cq + 1)), (sDV, 1))
                    ph.wait("scalar", sDV, dv_r)
                    if tr_prev:
                        ph.wait("scalar", sPE, tr_prev)
                    for sl in range((I + 1023) // 1024):
                        f0, f1 = sl * 1024, min((sl + 1) * 1024, I)
                        ph.raw("scalar", lambda e, ti=ti, f0=f0, f1=f1, cq=cq:
                               e.activation(
                            i32b[:, 0:f1 - f0], hbuf[:, ti * I + f0: ti * I + f1],
                            AF.Identity, scale=SC(cq + 2)))
                        ph.raw("scalar", lambda e, f0=f0, f1=f1, cq=cq: e.activation(
                            qh_t[:, f0:f1], i32b[:, 0:f1 - f0], AF.Identity,
                            scale=SC(cq + 1)))
                    ac_q = ph.mark("scalar", sAC)
                    ph.wait("tensor", sAC, ac_q)
                    for g in range(6):
                        c0, c1 = g * 8, min((g + 1) * 8, NCI)
                        if tps_ev:
                            ph.wait("tensor", sDV, tps_ev)
                        for c in range(c0, c1):
                            ph.raw("tensor", lambda e, c=c, c0=c0: e.transpose(
                                tps[:, (c - c0) * 128:(c - c0 + 1) * 128],
                                qh_t[:, c * 128:(c + 1) * 128], id16[:, :]))
                        pm = ph.inc_last("tensor", sPE)
                        ph.wait("vector", sPE, pm)
                        tps_ev = ph.op(
                            "vector", lambda e, ti=ti, c0=c0, c1=c1: e.tensor_copy(
                                hbuf[:, ti * I + c0 * 128: ti * I + c1 * 128],
                                tps[:, 0:(c1 - c0) * 128]), (sDV, 1))
                    tr_prev = pm
                qh_ready = tps_ev
                for ocu in range(16):
                    b2 = ocu % 2
                    guard = pe_down.get((blk_i, ocu - 2),
                                        pe_down.get((blk_i - 1, 14 + ocu), 0))
                    ph.wait("sync", sPE, guard)
                    w2_ready = load_w2(ocu, w2T[b2], sLw2[b2])
                    ph.wait("tensor", sLw2[b2], w2_ready)
                    ph.wait("tensor", sDV, qh_ready)
                    ou = blk_i * 16 + ocu
                    fb = ou % 3
                    if (ou - 3) >= 0 and (ou - 3) in fps_free:
                        ph.wait("tensor", sDV, fps_free[ou - 3])
                    for c2 in range(NCI):
                        rhs = bass.AP(hbuf, c2 * 128,
                                      [[NTB * I, 128], [I, NTB], [1, 128]])
                        ph.raw("tensor", lambda e, c2=c2, b2=b2, fb=fb,
                               rhs=rhs: e.matmul(
                            fps[fb][:, 0:512],
                            w2T[b2][:, c2 * 128:(c2 + 1) * 128],
                            rhs, start=(c2 == 0), stop=(c2 == NCI - 1)))
                    pmd = ph.inc_last("tensor", sPE)
                    pe_down[(blk_i, ocu)] = pmd
                    orow = ocu * 128
                    ab = ou % 2
                    ph.wait("vector", sPE, pmd)
                    ph.wait("vector", sSacc[ab])
                    dv_e = ph.op("vector", lambda e, fb=fb, ab=ab: e.tensor_copy(
                        accst[ab][:, :], fps[fb][:, 0:512]), (sDV, 1))
                    fps_free[ou] = dv_e
                    ph.wait("scalar", sDV, dv_e)
                    ph.dma("scalar", accT_d[orow:orow + 128, t0:t0 + TB],
                           accst[ab][:, :], sSacc[ab])
                sacc_counts[blk_i] = (sSacc[0].n, sSacc[1].n)
            if do_C:
                emit_C(NB - 1)
            ph.flush()
            esB.close()

    return nc


def _host_arrays(q_proj, k_proj, v_proj, o_proj, q_norm, k_norm, w_gate_up, w_down):
    f32 = np.float32
    arrs = {}
    arrs["qTw"] = np.ascontiguousarray(np.asarray(q_proj, f32).T)
    arrs["kTw"] = np.ascontiguousarray(np.asarray(k_proj, f32).T)
    arrs["vTw"] = np.ascontiguousarray(np.asarray(v_proj, f32).T)
    arrs["oTw"] = np.ascontiguousarray(np.asarray(o_proj, f32).T)
    arrs["w1n"] = np.ascontiguousarray(np.asarray(w_gate_up, f32))
    arrs["w2n"] = np.ascontiguousarray(np.asarray(w_down, f32))
    arrs["qnb"] = np.ascontiguousarray(
        np.broadcast_to(np.tile(np.asarray(q_norm, f32), NH)[None, :], (128, D)))
    arrs["knb"] = np.ascontiguousarray(
        np.broadcast_to(np.asarray(k_norm, f32)[None, :], (128, HD)))
    hd = np.float32(HD)
    idx = np.arange(128, dtype=f32)
    inv = (1.0 / np.power(np.float32(1000000.0), (2.0 * idx / hd), dtype=f32)).astype(f32)
    pos = np.arange(L, dtype=f32)
    fr = pos[:, None] * inv[None, :]
    c = np.cos(fr).astype(f32)
    s = np.sin(fr).astype(f32)
    arrs["ctab"] = np.ascontiguousarray(np.concatenate([c, c], axis=1))
    arrs["stab"] = np.ascontiguousarray(np.concatenate([-s, s], axis=1))
    m = np.zeros((128, 128), f32)
    m[np.triu_indices(128, 1)] = NEGBIG
    arrs["mskd"] = m
    arrs["id16"] = np.eye(128, dtype=BF)
    cstv = np.zeros((128, 8), f32)
    cstv[:, 0] = 1e-6
    cstv[:, 1] = 1e-5
    arrs["cst"] = cstv
    return arrs


def kernel(L0_attn_x, L0_attn_q_proj, L0_attn_k_proj, L0_attn_v_proj, L0_attn_o_proj,
           L0_attn_q_norm, L0_attn_k_norm, L0_ffn_w_gate_up, L0_ffn_w_down):
    n = 8
    x = np.asarray(L0_attn_x, np.float32)
    B, Lq, Dm = x.shape
    shared = _host_arrays(L0_attn_q_proj, L0_attn_k_proj, L0_attn_v_proj, L0_attn_o_proj,
                          L0_attn_q_norm, L0_attn_k_norm, L0_ffn_w_gate_up, L0_ffn_w_down)
    nc = build()
    in_maps = []
    for c in range(n):
        m = dict(shared)
        m["x_in"] = np.ascontiguousarray(x[2 * c:2 * c + 2].reshape(SEQ * L, D))
        in_maps.append(m)
    res = run_bass_kernel_spmd(nc, in_maps, core_ids=list(range(n)))
    out = np.empty((B, Lq, Dm), np.float32)
    for c in range(n):
        out[2 * c:2 * c + 2] = res.results[c]["y"].reshape(SEQ, L, D)
    return out


# version 10
# speedup vs baseline: 1.2804x; 1.2459x over previous
import os
import numpy as np
import ml_dtypes
import concourse.bass as bass
import concourse.mybir as mybir
from concourse.bass_utils import run_bass_kernel_spmd
from contextlib import ExitStack

F32 = mybir.dt.float32
BF16 = mybir.dt.bfloat16
I32 = mybir.dt.int32
AF = mybir.ActivationFunctionType
ALU = mybir.AluOpType
AX = mybir.AxisListType
BF = ml_dtypes.bfloat16

SEQ = 2
L = 2048
D = 2048
NH = 8
HD = 256
I = 5632
TT = 128
NTS = L // TT
NC = 16
NCI = I // 128
TB = 512
NB = SEQ * L // TB
NTB = TB // TT
NJ = I // 512
TL = SEQ * L
ENGS = ("sync", "tensor", "vector", "scalar", "gpsimd")
NEGBIG = -30000.0
ALLPH = ("A0", "W", "A1", "A2", "B", "C")


class Sem:
    def __init__(self, es, nc, name, queue=None):
        self.h = es.enter_context(nc.semaphore(name))
        self.n = 0
        self.queue = queue


DRAIN_ENGS = ("vector", "scalar", "gpsimd")


class Ph:
    def __init__(self, nc):
        self.nc = nc
        self.ops = {e: [] for e in ENGS}
        self.dma_sems = []
        self.drained = {e: True for e in ENGS}

    def raw(self, eng, fn):
        if eng in DRAIN_ENGS and not self.drained[eng]:
            self.ops[eng].append(lambda e: e.drain())
        self.drained[eng] = False
        self.ops[eng].append(fn)

    def op(self, eng, fn, inc=None):
        self.raw(eng, fn)
        if inc is None:
            return None
        sem, k = inc
        sem.n += k
        self.ops[eng].append(lambda e, h=sem.h, k=k: e.drain().then_inc(h, k))
        self.drained[eng] = True
        return sem.n

    def wait(self, eng, sem, val=None):
        v = sem.n if val is None else val
        if v > 0:
            self.ops[eng].append(lambda e, h=sem.h, v=v: e.wait_ge(h, v))

    def dma(self, q, dst, src, sem, transpose=False):
        assert sem.queue == q
        if sem not in self.dma_sems:
            self.dma_sems.append(sem)
        sem.n += 16
        if transpose:
            self.ops[q].append(lambda e, d=dst, s=src, h=sem.h:
                               e.dma_start(d, s, transpose=True).then_inc(h, 16))
        else:
            self.ops[q].append(lambda e, d=dst, s=src, h=sem.h:
                               e.dma_start(d, s).then_inc(h, 16))
        return sem.n

    def mark(self, eng, sem):
        sem.n += 1
        self.ops[eng].append(lambda e, h=sem.h: e.drain().then_inc(h, 1))
        self.drained[eng] = True
        return sem.n

    def inc_last(self, eng, sem, k=1):
        sem.n += k
        old = self.ops[eng][-1]
        self.ops[eng][-1] = lambda e, old=old, h=sem.h, k=k: old(e).then_inc(h, k)
        return sem.n

    def flush(self):
        for sem in self.dma_sems:
            self.wait(sem.queue, sem)
        self.dma_sems = []
        with self.nc.Block() as blk:
            for e in ENGS:
                lst = self.ops[e]
                if lst:
                    getattr(blk, e)(lambda eng, lst=lst: [f(eng) for f in lst])
        self.ops = {e: [] for e in ENGS}
        self.drained = {e: True for e in ENGS}


def build(debug=False, phases=ALLPH):
    nc = bass.Bass("TRN2", target_bir_lowering=False)

    x_in = nc.dram_tensor("x_in", [TL, D], F32, kind="ExternalInput").ap()
    qTw_h = nc.dram_tensor("qTw", [D, D], F32, kind="ExternalInput").ap()
    kTw_h = nc.dram_tensor("kTw", [D, HD], F32, kind="ExternalInput").ap()
    vTw_h = nc.dram_tensor("vTw", [D, HD], F32, kind="ExternalInput").ap()
    oTw_h = nc.dram_tensor("oTw", [D, D], F32, kind="ExternalInput").ap()
    w1_h = nc.dram_tensor("w1n", [2 * I, D], F32, kind="ExternalInput").ap()
    w2_h = nc.dram_tensor("w2n", [D, I], F32, kind="ExternalInput").ap()
    qnb_h = nc.dram_tensor("qnb", [128, D], F32, kind="ExternalInput").ap()
    knb_h = nc.dram_tensor("knb", [128, HD], F32, kind="ExternalInput").ap()
    ctab_h = nc.dram_tensor("ctab", [L, HD], F32, kind="ExternalInput").ap()
    stab_h = nc.dram_tensor("stab", [L, HD], F32, kind="ExternalInput").ap()
    mskd_h = nc.dram_tensor("mskd", [128, 128], F32, kind="ExternalInput").ap()
    id16_h = nc.dram_tensor("id16", [128, 128], BF16, kind="ExternalInput").ap()
    cst_h = nc.dram_tensor("cst", [128, 8], F32, kind="ExternalInput").ap()
    y_out = nc.dram_tensor("y", [TL, D], F32, kind="ExternalOutput").ap()

    dbg = "ExternalOutput" if debug else "Internal"
    xbf_d = nc.dram_tensor("xbf_d", [TL, D], BF16, kind=dbg).ap()
    qbf_d = nc.dram_tensor("qbf_d", [TL, D], BF16, kind=dbg).ap()
    kbf_d = nc.dram_tensor("kbf_d", [TL, HD], BF16, kind=dbg).ap()
    x1_d = nc.dram_tensor("x1_d", [TL, D], BF16, kind=dbg).ap()
    qx_d = nc.dram_tensor("qx_d", [TL, D], BF16, kind=dbg).ap()
    qw1_d = nc.dram_tensor("qw1_d", [D, 2 * I], BF16, kind=dbg).ap()
    qw2_d = nc.dram_tensor("qw2_d", [I, D], BF16, kind=dbg).ap()
    accT_d = nc.dram_tensor("accT_d", [D, TL], BF16, kind=dbg).ap()

    es = ExitStack()
    with es:
        sPE = Sem(es, nc, "sPE")
        sDV = Sem(es, nc, "sDV")
        sAC = Sem(es, nc, "sAC")
        sGP = Sem(es, nc, "sGP")

        id16 = es.enter_context(nc.sbuf_tensor("id16s", [128, 128], BF16))
        mskd = es.enter_context(nc.sbuf_tensor("mskds", [128, 128], F32))
        sca = es.enter_context(nc.sbuf_tensor("sca", [128, 64], F32))
        cst = es.enter_context(nc.sbuf_tensor("cst_s", [128, 8], F32))

        def CST(c):
            return cst[:, c:c + 1]

        def SC(c, n=1):
            return sca[:, c:c + n]

        ph = Ph(nc)

        sLc0 = Sem(es, nc, "sLc0", "sync")
        ph.dma("sync", id16[:, :], id16_h[:, :], sLc0)
        ph.dma("sync", mskd[:, :], mskd_h[:, :], sLc0)
        ph.dma("sync", cst[:, :], cst_h[:, :], sLc0)
        consts_ready = sLc0.n
        a0_sems = []
        if "A0" in phases:
            sLa = Sem(es, nc, "sLa", "gpsimd")
            a0_sems = [sLa]
            for it in range(TL // TT):
                ph.dma("gpsimd", xbf_d[it * TT:(it + 1) * TT, :],
                       x_in[it * TT:(it + 1) * TT, :], sLa)

        w_store_sems = []
        if "W" in phases:
            esW = ExitStack()
            NWB = 4
            wst = [esW.enter_context(nc.sbuf_tensor(f"wst{b}", [128, I], F32))
                   for b in range(NWB)]
            i32w = [esW.enter_context(nc.sbuf_tensor(f"i32w{b}", [128, I], I32))
                    for b in range(2)]
            qwst = [esW.enter_context(nc.sbuf_tensor(f"qwst{b}", [128, I], BF16))
                    for b in range(2)]
            stg = [esW.enter_context(nc.sbuf_tensor(f"stg{b}", [128, I], BF16))
                   for b in range(NWB)]
            pst = [esW.enter_context(nc.psum_tensor(f"pst{b}", [128, 2048], BF16))
                   for b in range(2)]
            sLw = [Sem(es, nc, f"sLw{b}", "sync") for b in range(NWB)]
            sSw = [Sem(es, nc, f"sSw{b}", "scalar") for b in range(NWB)]
            w_store_sems = sSw

            strips = [("w1", s, D, NC) for s in range(2 * I // 128)] + \
                     [("w2", s, I, NCI) for s in range(D // 128)]
            _wl = os.environ.get("K_W_STRIPS")
            if _wl:
                strips = strips[:int(_wl)]
            amax_done = {}
            rnd_done = {}
            dq_done = {}
            tr_done = {}
            ev_done = {}
            for si, (wn, s, wd, nch) in enumerate(strips):
                b = si % NWB
                b2 = si % 2
                cq = 4 + 3 * (si % 4)
                src_ = (w1_h[s * 128:(s + 1) * 128, :] if wn == "w1"
                        else w2_h[s * 128:(s + 1) * 128, :])
                if si >= NWB:
                    ph.wait("sync", sAC, rnd_done[si - NWB])
                ph.dma("sync", wst[b][:, 0:wd], src_, sLw[b])
                ph.wait("vector", sLw[b])
                ph.raw("vector", lambda e, b=b, wd=wd, cq=cq: e.reduce_max(
                    SC(cq), wst[b][:, 0:wd], axis=AX.X, apply_absolute_value=True))
                ph.raw("vector", lambda e, cq=cq: e.tensor_scalar(
                    SC(cq + 1), SC(cq), 1.0 / 127.0, 1e-8, ALU.mult, ALU.max))
                amax_done[si] = ph.op("vector", lambda e, cq=cq: e.reciprocal(
                    SC(cq + 2), SC(cq + 1)), (sDV, 1))
                ph.wait("scalar", sDV, amax_done[si])
                if si >= 2:
                    ph.wait("scalar", sPE, tr_done[si - 2])
                ph.raw("scalar", lambda e, b=b, b2=b2, wd=wd, cq=cq: e.activation(
                    i32w[b2][:, 0:wd], wst[b][:, 0:wd], AF.Identity, scale=SC(cq + 2)))
                rnd_done[si] = ph.inc_last("scalar", sAC)
                ph.raw("scalar", lambda e, b2=b2, wd=wd, cq=cq: e.activation(
                    qwst[b2][:, 0:wd], i32w[b2][:, 0:wd], AF.Identity, scale=SC(cq + 1)))
                dq_done[si] = ph.inc_last("scalar", sAC)
                ph.wait("tensor", sAC, dq_done[si])
                ph.wait("tensor", sLc0, consts_ready)
                ph.wait("tensor", sSw[b])
                if si >= 2:
                    ph.wait("tensor", sDV, ev_done[si - 2])
                ngr = (nch + 15) // 16
                for g in range(ngr):
                    c0, c1 = g * 16, min((g + 1) * 16, nch)
                    if g > 0:
                        ph.wait("tensor", sDV, sDV.n)
                    for c in range(c0, c1):
                        ph.raw("tensor", lambda e, c=c, c0=c0, b2=b2: e.transpose(
                            pst[b2][:, (c - c0) * 128:(c - c0 + 1) * 128],
                            qwst[b2][:, c * 128:(c + 1) * 128], id16[:, :]))
                    pm = ph.inc_last("tensor", sPE)
                    ph.wait("vector", sPE, pm)
                    ph.op("vector", lambda e, b=b, b2=b2, c0=c0, c1=c1: e.tensor_copy(
                        stg[b][:, c0 * 128:c1 * 128], pst[b2][:, 0:(c1 - c0) * 128]),
                        (sDV, 1))
                tr_done[si] = pm
                ev_done[si] = sDV.n
                ph.wait("scalar", sDV, ev_done[si])
                nsp = 2 if wn == "w1" else 4
                csz = (nch + nsp - 1) // nsp
                dT, drows = (qw1_d, 2 * I) if wn == "w1" else (qw2_d, D)
                for sp in range(nsp):
                    c0, c1 = sp * csz, min((sp + 1) * csz, nch)
                    if c0 >= c1:
                        continue
                    srcap = bass.AP(stg[b], c0 * 128, [[I, 128], [128, c1 - c0], [1, 128]])
                    dstap = bass.AP(dT.tensor, c0 * 128 * drows + s * 128,
                                    [[drows, 128], [128 * drows, c1 - c0], [1, 128]])
                    ph.dma("scalar", dstap, srcap, sSw[b])
            ph.flush()
            esW.close()

        esAT = ExitStack()
        sqk_done = {}
        sSqk = None
        sx_done = {}
        if "A1" in phases or "A2" in phases:
            v_sb = esAT.enter_context(nc.sbuf_tensor("v_sb", [128, SEQ * NTS * HD], BF16))

        if "A1" in phases:
            esA1 = ExitStack()
            qTw = esA1.enter_context(nc.sbuf_tensor("qTwS", [128, NC * D], BF16))
            kTw = esA1.enter_context(nc.sbuf_tensor("kTwS", [128, NC * HD], BF16))
            vTw = esA1.enter_context(nc.sbuf_tensor("vTwS", [128, NC * HD], BF16))
            xT = [esA1.enter_context(nc.sbuf_tensor(f"xT{b}", [128, NC * 512], BF16))
                  for b in range(2)]
            qnb = esA1.enter_context(nc.sbuf_tensor("qnbS", [128, D], F32))
            knb = esA1.enter_context(nc.sbuf_tensor("knbS", [128, HD], F32))
            ctab = [esA1.enter_context(nc.sbuf_tensor(f"ctab{b}", [128, HD], F32))
                    for b in range(2)]
            stab = [esA1.enter_context(nc.sbuf_tensor(f"stab{b}", [128, HD], F32))
                    for b in range(2)]
            qhat = esA1.enter_context(nc.sbuf_tensor("qhat", [128, D], F32))
            rt1 = esA1.enter_context(nc.sbuf_tensor("rt1", [128, D], F32))
            rt2 = esA1.enter_context(nc.sbuf_tensor("rt2", [128, D], F32))
            qbf = esA1.enter_context(nc.sbuf_tensor("qbf", [128, D], BF16))
            khat = esA1.enter_context(nc.sbuf_tensor("khat", [128, HD], F32))
            kr1 = esA1.enter_context(nc.sbuf_tensor("kr1", [128, HD], F32))
            kr2 = esA1.enter_context(nc.sbuf_tensor("kr2", [128, HD], F32))
            kbf = esA1.enter_context(nc.sbuf_tensor("kbf", [128, HD], BF16))
            sqg = esA1.enter_context(nc.sbuf_tensor("sqg", [128, HD], F32))
            qps = esA1.enter_context(nc.psum_tensor("qps", [128, D], F32))
            kps = esA1.enter_context(nc.psum_tensor("kps", [128, 512], F32))
            vps = esA1.enter_context(nc.psum_tensor("vps", [128, 512], F32))
            sLwq = Sem(es, nc, "sLwq", "gpsimd")
            sLnb = Sem(es, nc, "sLnb", "sync")
            sLxT = [Sem(es, nc, f"sLxT{b}", "sync") for b in range(2)]
            sLtab = [Sem(es, nc, f"sLtab{b}", "scalar") for b in range(2)]
            sSqk = Sem(es, nc, "sSqk", "scalar")

            for c in range(NC):
                ph.dma("gpsimd", qTw[:, c * D:(c + 1) * D],
                       qTw_h[c * 128:(c + 1) * 128, :], sLwq)
                ph.dma("gpsimd", kTw[:, c * HD:(c + 1) * HD],
                       kTw_h[c * 128:(c + 1) * 128, :], sLwq)
                ph.dma("gpsimd", vTw[:, c * HD:(c + 1) * HD],
                       vTw_h[c * 128:(c + 1) * 128, :], sLwq)
            wload = sLwq.n
            ph.dma("sync", qnb[:, :], qnb_h[:, :], sLnb)
            ph.dma("sync", knb[:, :], knb_h[:, :], sLnb)
            nb_ready = sLnb.n

            NQ = 4
            pe_tile_done = {}
            psum_free = {}
            gp_done = {}
            dv_rope_done = {}
            for s_ in a0_sems:
                ph.wait("sync", s_)
            for qb_i in range(TL // (NQ * TT)):
                b = qb_i % 2
                sq = qb_i // 4
                q_in_seq = qb_i % 4
                if qb_i >= 2:
                    ph.wait("sync", sPE, pe_tile_done.get((qb_i - 2, NQ - 1), 0))
                for c in range(NC):
                    ph.dma("sync", xT[b][:, c * 512:(c + 1) * 512],
                           xbf_d[sq * L + q_in_seq * 512: sq * L + (q_in_seq + 1) * 512,
                                 c * 128:(c + 1) * 128], sLxT[b], transpose=True)
                xt_ready = sLxT[b].n
                for ti in range(NQ):
                    it = qb_i * NQ + ti
                    trow = it * TT
                    tb_ = it % 2
                    if it >= 2:
                        ph.wait("scalar", sGP, gp_done.get(it - 2, 0))
                        ph.wait("scalar", sDV, dv_rope_done.get(it - 2, 0))
                    ph.dma("scalar", ctab[tb_][:, :],
                           ctab_h[(it % NTS) * TT:((it % NTS) + 1) * TT, :], sLtab[tb_])
                    ph.dma("scalar", stab[tb_][:, :],
                           stab_h[(it % NTS) * TT:((it % NTS) + 1) * TT, :], sLtab[tb_])
                    tabs_ready = sLtab[tb_].n
                    ph.wait("tensor", sLwq, wload)
                    ph.wait("tensor", sLxT[b], xt_ready)
                    if it >= 1:
                        ph.wait("tensor", sDV, psum_free[it - 1])
                    for c in range(NC):
                        lhs = xT[b][:, c * 512 + ti * 128: c * 512 + (ti + 1) * 128]
                        st, sp = (c == 0), (c == NC - 1)
                        for ob in range(4):
                            ph.raw("tensor", lambda e, l=lhs, ob=ob, st=st, sp=sp, c=c:
                                   e.matmul(qps[:, ob * 512:(ob + 1) * 512], l,
                                            qTw[:, c * D + ob * 512: c * D + (ob + 1) * 512],
                                            start=st, stop=sp))
                        ph.raw("tensor", lambda e, l=lhs, st=st, sp=sp, c=c: e.matmul(
                            kps[:, 0:HD], l, kTw[:, c * HD:(c + 1) * HD],
                            start=st, stop=sp))
                        ph.raw("tensor", lambda e, l=lhs, st=st, sp=sp, c=c: e.matmul(
                            vps[:, 0:HD], l, vTw[:, c * HD:(c + 1) * HD],
                            start=st, stop=sp))
                    pe_tile_done[(qb_i, ti)] = ph.inc_last("tensor", sPE)
                    ph.wait("scalar", sPE, pe_tile_done[(qb_i, ti)])
                    for h in range(NH):
                        ph.raw("scalar", lambda e, h=h: e.activation(
                            sqg[:, :], qps[:, h * HD:(h + 1) * HD], AF.Square,
                            accum_out=SC(12 + h)))
                    ph.raw("scalar", lambda e: e.activation(
                        sqg[:, :], kps[:, 0:HD], AF.Square, accum_out=SC(20)))
                    ph.wait("scalar", sLc0, consts_ready)
                    ph.raw("scalar", lambda e: e.activation(
                        SC(21, 9), SC(12, 9), AF.Ln, scale=1.0 / 256.0, bias=CST(0)))
                    ph.raw("scalar", lambda e: e.activation(
                        SC(21, 9), SC(21, 9), AF.Identity, scale=-0.5))
                    ac_rms = ph.op("scalar", lambda e: e.activation(
                        SC(21, 9), SC(21, 9), AF.Exp), (sAC, 1))
                    ph.wait("vector", sAC, ac_rms)
                    ph.wait("vector", sLnb, nb_ready)
                    for h in range(NH):
                        ph.raw("vector", lambda e, h=h: e.scalar_tensor_tensor(
                            qhat[:, h * HD:(h + 1) * HD], qps[:, h * HD:(h + 1) * HD],
                            SC(21 + h), qnb[:, h * HD:(h + 1) * HD], ALU.mult, ALU.mult))
                    ph.raw("vector", lambda e: e.scalar_tensor_tensor(
                        khat[:, :], kps[:, 0:HD], SC(29), knb[:, :], ALU.mult, ALU.mult))
                    psum_free[it] = ph.op("vector", lambda e, it=it: e.tensor_copy(
                        v_sb[:, it * HD:(it + 1) * HD], vps[:, 0:HD]), (sDV, 1))
                    ct, st_ = ctab[tb_], stab[tb_]
                    ph.wait("gpsimd", sDV, psum_free[it])
                    ph.wait("gpsimd", sLtab[tb_], tabs_ready)
                    ph.wait("vector", sLtab[tb_], tabs_ready)
                    for h in range(NH):
                        q0 = h * HD
                        ph.raw("vector", lambda e, q0=q0, ct=ct: e.tensor_tensor(
                            rt1[:, q0:q0 + HD], qhat[:, q0:q0 + HD], ct[:, :], ALU.mult))
                        ph.raw("gpsimd", lambda e, q0=q0, st_=st_: e.tensor_tensor(
                            rt2[:, q0:q0 + 128], qhat[:, q0 + 128:q0 + HD],
                            st_[:, 0:128], ALU.mult))
                        ph.raw("gpsimd", lambda e, q0=q0, st_=st_: e.tensor_tensor(
                            rt2[:, q0 + 128:q0 + HD], qhat[:, q0:q0 + 128],
                            st_[:, 128:HD], ALU.mult))
                    gp_done[it] = ph.mark("gpsimd", sGP)
                    ph.wait("vector", sGP, gp_done[it])
                    ph.wait("vector", sSqk)
                    ph.raw("vector", lambda e: e.tensor_tensor(
                        qbf[:, :], rt1[:, :], rt2[:, :], ALU.add))
                    ph.raw("vector", lambda e, ct=ct: e.tensor_tensor(
                        kr1[:, :], khat[:, :], ct[:, :], ALU.mult))
                    ph.raw("vector", lambda e, st_=st_: e.tensor_tensor(
                        kr2[:, 0:128], khat[:, 128:HD], st_[:, 0:128], ALU.mult))
                    ph.raw("vector", lambda e, st_=st_: e.tensor_tensor(
                        kr2[:, 128:HD], khat[:, 0:128], st_[:, 128:HD], ALU.mult))
                    dv_rope_done[it] = ph.op("vector", lambda e: e.tensor_tensor(
                        kbf[:, :], kr1[:, :], kr2[:, :], ALU.add), (sDV, 1))
                    ph.wait("scalar", sDV, dv_rope_done[it])
                    ph.dma("scalar", qbf_d[trow:trow + TT, :], qbf[:, :], sSqk)
                    ph.dma("scalar", kbf_d[trow:trow + TT, :], kbf[:, :], sSqk)
                    sqk_done[it] = sSqk.n
            ph.flush()
            esA1.close()

        if "A2" in phases:
            esA2 = ExitStack()
            oTw = esA2.enter_context(nc.sbuf_tensor("oTwS", [128, NC * D], BF16))
            qT = esA2.enter_context(nc.sbuf_tensor("qTS", [128, NC * 1024], BF16))
            kT = esA2.enter_context(nc.sbuf_tensor("kTS", [128, 2 * L], BF16))
            w_sb = [esA2.enter_context(nc.sbuf_tensor(f"w_sb{b}", [128, 1024], BF16))
                    for b in range(2)]
            wT_sb = [esA2.enter_context(nc.sbuf_tensor(f"wT_sb{b}", [128, 1024], BF16))
                     for b in range(2)]
            ctxT = esA2.enter_context(nc.sbuf_tensor("ctxT", [128, D], BF16))
            x_sb = [esA2.enter_context(nc.sbuf_tensor(f"x_sb{b}", [128, D], F32))
                    for b in range(2)]
            y_sb = esA2.enter_context(nc.sbuf_tensor("y_sb", [128, D], F32))
            x1_sb = esA2.enter_context(nc.sbuf_tensor("x1_sb", [128, D], F32))
            i32a = esA2.enter_context(nc.sbuf_tensor("i32a", [128, D], I32))
            qx_sb = esA2.enter_context(nc.sbuf_tensor("qx_sb", [128, D], BF16))
            sqga = esA2.enter_context(nc.sbuf_tensor("sqga", [128, D], F32))
            scps = [esA2.enter_context(nc.psum_tensor(f"scps{b}", [128, 1024], F32))
                    for b in range(2)]
            psw = [esA2.enter_context(nc.psum_tensor(f"psw{b}", [128, 1024], BF16))
                   for b in range(2)]
            ctxp = [esA2.enter_context(nc.psum_tensor(f"ctxp{b}", [128, 512], F32))
                    for b in range(2)]
            sLo = Sem(es, nc, "sLo", "gpsimd")
            sLkT = Sem(es, nc, "sLkT", "sync")
            sLqT = Sem(es, nc, "sLqT", "sync")
            sLx = [Sem(es, nc, f"sLx{b}", "sync") for b in range(2)]
            sSx = Sem(es, nc, "sSx", "scalar")
            sSx1 = Sem(es, nc, "sSx1", "gpsimd")

            for c in range(NC):
                ph.dma("gpsimd", oTw[:, c * D:(c + 1) * D],
                       oTw_h[c * 128:(c + 1) * 128, :], sLo)
            oload = sLo.n

            uix = 0
            sc_free = {0: None, 1: None}
            psw_free = {0: 0, 1: 0}
            wsb_free = {0: 0, 1: 0}
            wt_free = {0: 0, 1: 0}
            xfree = {}
            last_sc_pm = 0
            kT_free = 0
            qT_free = 0
            for sq in range(SEQ):
                if sqk_done:
                    ph.wait("sync", sSqk, sqk_done[sq * NTS + NTS - 1])
                if sq > 0:
                    ph.wait("sync", sPE, kT_free)
                for c2 in range(2):
                    ph.dma("sync", kT[:, c2 * L:(c2 + 1) * L],
                           kbf_d[sq * L:(sq + 1) * L, c2 * 128:(c2 + 1) * 128], sLkT,
                           transpose=True)
                kt_ready = sLkT.n
                for hf in range(2):
                    if sq + hf > 0:
                        ph.wait("sync", sPE, qT_free)
                    for c in range(NC):
                        ph.dma("sync", qT[:, c * 1024:(c + 1) * 1024],
                               qbf_d[sq * L + hf * 1024: sq * L + (hf + 1) * 1024,
                                     c * 128:(c + 1) * 128], sLqT, transpose=True)
                    qt_ready = sLqT.n
                    for ti in range(8):
                        i = hf * 8 + ti
                        it = sq * NTS + i
                        trow = it * TT
                        Lk = (i + 1) * 128
                        nparts = (Lk + 1023) // 1024
                        xb = it % 2
                        if it >= 2:
                            ph.wait("sync", sDV, xfree.get(it - 2, 0))
                        ph.dma("sync", x_sb[xb][:, :], x_in[trow:trow + TT, :], sLx[xb])
                        x_ready = sLx[xb].n
                        dv_cx = 0
                        for h in range(NH):
                            B0 = 8 * (uix % 4)
                            uix += 1
                            qts = [qT[:, c_ * 1024 + ti * 128: c_ * 1024 + (ti + 1) * 128]
                                   for c_ in (2 * h, 2 * h + 1)]
                            pinfo = []
                            for p in range(nparts):
                                par = p if nparts == 2 else (uix % 2)
                                tk0 = p * 1024
                                tkw = min(1024, Lk - tk0)
                                fullw = min(tkw, i * 128 - tk0)
                                has_diag = (tk0 + tkw > i * 128)
                                n5 = fullw // 512
                                segs = [(s5 * 512, 512) for s5 in range(n5)]
                                if fullw > n5 * 512:
                                    segs.append((n5 * 512, fullw - n5 * 512))
                                if has_diag:
                                    segs.append((fullw, 128))
                                if h == 0 and p == 0:
                                    ph.wait("tensor", sLqT, qt_ready)
                                    ph.wait("tensor", sLkT, kt_ready)
                                if sc_free[par] is not None:
                                    sem_, v_ = sc_free[par]
                                    ph.wait("tensor", sem_, v_)
                                for o0, wseg in segs:
                                    for cc in range(2):
                                        ph.raw("tensor",
                                               lambda e, q_=qts[cc], cc=cc, o0=o0,
                                               wseg=wseg, par=par, tk0=tk0: e.matmul(
                                            scps[par][:, o0:o0 + wseg], q_,
                                            kT[:, cc * L + tk0 + o0:
                                               cc * L + tk0 + o0 + wseg],
                                            start=(cc == 0), stop=(cc == 1)))
                                pm_sc = ph.inc_last("tensor", sPE)
                                last_sc_pm = pm_sc
                                ph.wait("vector", sPE, pm_sc)
                                if has_diag:
                                    ph.raw("vector", lambda e, par=par, do=fullw:
                                           e.tensor_tensor(
                                        scps[par][:, do:do + 128],
                                        scps[par][:, do:do + 128], mskd[:, :],
                                        ALU.add))
                                mcol = B0 + 5 + p
                                bcol = B0 if p == 0 else B0 + 7
                                ph.raw("vector", lambda e, par=par, tkw=tkw, mcol=mcol:
                                       e.reduce_max(
                                    SC(mcol), scps[par][:, 0:tkw], axis=AX.X,
                                    negate=True))
                                dv_b = ph.op("vector", lambda e, mcol=mcol, bcol=bcol:
                                             e.tensor_scalar(
                                    SC(bcol), SC(mcol), 1.0 / 16.0, None, ALU.mult),
                                    (sDV, 1))
                                ph.wait("scalar", sDV, dv_b)
                                ph.wait("scalar", sPE, wsb_free[par])
                                scol = B0 + 2 + p
                                ac_e = ph.op("scalar", lambda e, par=par, tkw=tkw,
                                             bcol=bcol, scol=scol: e.activation(
                                    w_sb[par][:, 0:tkw], scps[par][:, 0:tkw], AF.Exp,
                                    bias=SC(bcol), scale=1.0 / 16.0,
                                    accum_out=SC(scol)), (sAC, 1))
                                sc_free[par] = (sAC, ac_e)
                                pinfo.append((par, tk0, tkw, scol, ac_e))
                            ph.wait("vector", sAC, pinfo[-1][4])
                            if nparts == 1:
                                (par, tk0, tkw, scol, _) = pinfo[0]
                                ph.raw("vector", lambda e, B0=B0, scol=scol:
                                       e.reciprocal(SC(B0 + 1), SC(scol)))
                                norm_done = {par: None}
                                gcol = {par: B0 + 1}
                            else:
                                ph.raw("vector", lambda e, B0=B0: e.tensor_tensor(
                                    SC(B0 + 7), SC(B0 + 5), SC(B0 + 6), ALU.min))
                                ph.raw("vector", lambda e, B0=B0: e.tensor_scalar(
                                    SC(B0 + 5, 2), SC(B0 + 5, 2), SC(B0 + 7), None,
                                    ALU.subtract))
                                dv_f = ph.op("vector", lambda e, B0=B0:
                                             e.tensor_scalar(
                                    SC(B0 + 5, 2), SC(B0 + 5, 2), -1.0 / 16.0, None,
                                    ALU.mult), (sDV, 1))
                                ph.wait("scalar", sDV, dv_f)
                                ac_f = ph.op("scalar", lambda e, B0=B0: e.activation(
                                    SC(B0 + 5, 2), SC(B0 + 5, 2), AF.Exp), (sAC, 1))
                                ph.wait("vector", sAC, ac_f)
                                ph.raw("vector", lambda e, B0=B0: e.tensor_tensor(
                                    SC(B0 + 2, 2), SC(B0 + 2, 2), SC(B0 + 5, 2),
                                    ALU.mult))
                                ph.raw("vector", lambda e, B0=B0: e.reduce_sum(
                                    SC(B0 + 4), SC(B0 + 2, 2), axis=AX.X))
                                ph.raw("vector", lambda e, B0=B0: e.reciprocal(
                                    SC(B0 + 1), SC(B0 + 4)))
                                ph.raw("vector", lambda e, B0=B0: e.tensor_scalar(
                                    SC(B0 + 5, 2), SC(B0 + 5, 2), SC(B0 + 1), None,
                                    ALU.mult))
                                norm_done = {}
                                gcol = {pinfo[0][0]: B0 + 5, pinfo[1][0]: B0 + 6}
                            nch_tot = (Lk + 127) // 128
                            done_ch = 0
                            for pi, (par, tk0, tkw, scol, _) in enumerate(pinfo):
                                nch = (tkw + 127) // 128
                                dv_n = ph.op("vector", lambda e, par=par, tkw=tkw,
                                             g=gcol[par]: e.tensor_scalar(
                                    w_sb[par][:, 0:tkw], w_sb[par][:, 0:tkw],
                                    SC(g), None, ALU.mult), (sDV, 1))
                                ph.wait("tensor", sDV, dv_n)
                                ph.wait("tensor", sDV, psw_free[par])
                                for j in range(nch):
                                    ph.raw("tensor", lambda e, par=par, j=j:
                                           e.transpose(
                                        psw[par][:, j * 128:(j + 1) * 128],
                                        w_sb[par][:, j * 128:(j + 1) * 128],
                                        id16[:, :]))
                                pm_tr = ph.inc_last("tensor", sPE)
                                wsb_free[par] = pm_tr
                                ph.wait("vector", sPE, pm_tr)
                                ph.wait("vector", sPE, wt_free[par])
                                dv_ev = ph.op("vector", lambda e, par=par, tkw=tkw:
                                              e.tensor_copy(
                                    wT_sb[par][:, 0:tkw], psw[par][:, 0:tkw]),
                                    (sDV, 1))
                                psw_free[par] = dv_ev
                                ph.wait("tensor", sDV, dv_ev)
                                for half in range(2):
                                    for j in range(nch):
                                        ph.raw("tensor",
                                               lambda e, half=half, j=j, sq=sq,
                                               tk0=tk0, par=par,
                                               st=(done_ch == 0 and j == 0),
                                               sp=(done_ch + nch == nch_tot
                                                   and j == nch - 1): e.matmul(
                                            ctxp[half][:, 0:128],
                                            v_sb[:, (sq * NTS + tk0 // 128 + j) * HD
                                                 + half * 128:
                                                 (sq * NTS + tk0 // 128 + j) * HD
                                                 + half * 128 + 128],
                                            wT_sb[par][:, j * 128:(j + 1) * 128],
                                            start=st, stop=sp))
                                pm_cx = ph.inc_last("tensor", sPE)
                                wt_free[par] = pm_cx
                                done_ch += nch
                            ph.wait("vector", sPE, pm_cx)
                            for half in range(2):
                                ph.raw("vector", lambda e, half=half, h=h:
                                       e.tensor_copy(
                                    ctxT[:, (2 * h + half) * 128:
                                         (2 * h + half + 1) * 128],
                                    ctxp[half][:, 0:128]))
                            dv_cx = ph.mark("vector", sDV)
                        ph.wait("tensor", sDV, dv_cx)
                        ph.wait("tensor", sLo, oload)
                        for par in range(2):
                            if sc_free[par] is not None:
                                sem_, v_ = sc_free[par]
                                ph.wait("tensor", sem_, v_)
                        for c in range(NC):
                            st, sp = (c == 0), (c == NC - 1)
                            for ob in range(4):
                                ph.raw("tensor", lambda e, c=c, ob=ob, st=st, sp=sp:
                                       e.matmul(
                                    scps[ob // 2][:, (ob % 2) * 512:
                                                  (ob % 2 + 1) * 512],
                                    ctxT[:, c * 128:(c + 1) * 128],
                                    oTw[:, c * D + ob * 512: c * D + (ob + 1) * 512],
                                    start=st, stop=sp))
                        pm_o = ph.inc_last("tensor", sPE)
                        ph.wait("vector", sPE, pm_o)
                        ph.wait("vector", sLx[xb], x_ready)
                        ph.raw("vector", lambda e, xb=xb: e.tensor_tensor(
                            y_sb[:, 0:1024], scps[0][:, 0:1024],
                            x_sb[xb][:, 0:1024], ALU.add))
                        dv_y2 = ph.op("vector", lambda e, xb=xb: e.tensor_tensor(
                            y_sb[:, 1024:2048], scps[1][:, 0:1024],
                            x_sb[xb][:, 1024:2048], ALU.add), (sDV, 1))
                        xfree[it] = dv_y2
                        sc_free = {0: (sDV, dv_y2), 1: (sDV, dv_y2)}
                        ph.wait("scalar", sDV, dv_y2)
                        ac_sq = ph.op("scalar", lambda e: e.activation(
                            sqga[:, :], y_sb[:, :], AF.Square, accum_out=SC(39)),
                            (sAC, 1))
                        ph.raw("vector", lambda e: e.reduce_sum(SC(40), y_sb[:, :],
                                                                axis=AX.X))
                        ph.raw("vector", lambda e: e.tensor_scalar(
                            SC(40), SC(40), 1.0 / D, None, ALU.mult))
                        ph.raw("vector", lambda e: e.tensor_tensor(
                            SC(41), SC(40), SC(40), ALU.mult))
                        ph.wait("vector", sAC, ac_sq)
                        dv_var = ph.op("vector", lambda e: e.scalar_tensor_tensor(
                            SC(42), SC(39), 1.0 / D, SC(41), ALU.mult, ALU.subtract),
                            (sDV, 1))
                        ph.wait("scalar", sDV, dv_var)
                        ph.raw("scalar", lambda e: e.activation(
                            SC(43), SC(42), AF.Ln, bias=CST(1)))
                        ph.raw("scalar", lambda e: e.activation(
                            SC(43), SC(43), AF.Identity, scale=-0.5))
                        ac_rstd = ph.op("scalar", lambda e: e.activation(
                            SC(43), SC(43), AF.Exp), (sAC, 1))
                        ph.wait("vector", sAC, ac_rstd)
                        ph.wait("vector", sSx)
                        ph.wait("vector", sSx1)
                        ph.raw("vector", lambda e: e.tensor_scalar(
                            x1_sb[:, :], y_sb[:, :], SC(40), SC(43),
                            ALU.subtract, ALU.mult))
                        ph.raw("vector", lambda e: e.reduce_max(
                            SC(44), x1_sb[:, :], axis=AX.X, apply_absolute_value=True))
                        ph.raw("vector", lambda e: e.tensor_scalar(
                            SC(45), SC(44), 1.0 / 127.0, 1e-8, ALU.mult, ALU.max))
                        dv_q = ph.op("vector", lambda e: e.reciprocal(SC(46), SC(45)),
                                     (sDV, 1))
                        ph.wait("scalar", sDV, dv_q)
                        ph.raw("scalar", lambda e: e.activation(
                            i32a[:, :], x1_sb[:, :], AF.Identity, scale=SC(46)))
                        epi = ph.op("scalar", lambda e: e.activation(
                            qx_sb[:, :], i32a[:, :], AF.Identity, scale=SC(45)),
                            (sAC, 1))
                        ph.wait("scalar", sAC, epi)
                        ph.wait("gpsimd", sAC, epi)
                        ph.dma("gpsimd", x1_d[trow:trow + TT, :], x1_sb[:, :], sSx1)
                        ph.dma("scalar", qx_d[trow:trow + TT, :], qx_sb[:, :], sSx)
                        sx_done[it] = (sSx.n, sSx1.n)
                    qT_free = last_sc_pm
                kT_free = last_sc_pm
            ph.flush()
            esA2.close()
        esAT.close()

        if "B" in phases:
            do_C = "C" in phases
            esB = ExitStack()
            qxT = esB.enter_context(nc.sbuf_tensor("qxT", [128, NC * TB], BF16))
            hbuf = esB.enter_context(nc.sbuf_tensor("hbuf", [128, NTB * I], BF16))
            wg = [esB.enter_context(nc.sbuf_tensor(f"wg{b}", [128, NC * 512], BF16))
                  for b in range(2)]
            wu = [esB.enter_context(nc.sbuf_tensor(f"wu{b}", [128, NC * 512], BF16))
                  for b in range(2)]
            w2T = [esB.enter_context(nc.sbuf_tensor(f"w2T{b}", [128, NCI * 128], BF16))
                   for b in range(2)]
            silu_t = esB.enter_context(nc.sbuf_tensor("silu_t", [128, 512], F32))
            qh_t = esB.enter_context(nc.sbuf_tensor("qh_t", [128, I], BF16))
            i32b = esB.enter_context(nc.sbuf_tensor("i32b", [128, 1024], I32))
            accst = [esB.enter_context(nc.sbuf_tensor(f"accst{b}", [128, 512], BF16))
                     for b in range(2)]
            gps = esB.enter_context(nc.psum_tensor("gps", [128, 1024], F32))
            ups = esB.enter_context(nc.psum_tensor("ups", [128, 1024], F32))
            tps = esB.enter_context(nc.psum_tensor("tps", [128, 1024], BF16))
            fps = [esB.enter_context(nc.psum_tensor(f"fps{b}", [128, 512], F32))
                   for b in range(3)]
            sLqxT = Sem(es, nc, "sLqxT", "sync")
            sLw1 = [Sem(es, nc, f"sLw1{b}", "sync") for b in range(2)]
            sLw2 = [Sem(es, nc, f"sLw2{b}", "sync") for b in range(2)]
            sSacc = [Sem(es, nc, f"sSacc{b}", "scalar") for b in range(2)]
            if do_C:
                acc_c = [esB.enter_context(nc.sbuf_tensor(f"acc_c{b}", [128, D], BF16))
                         for b in range(2)]
                x1c = [esB.enter_context(nc.sbuf_tensor(f"x1c{b}", [128, D], BF16))
                       for b in range(2)]
                yc = esB.enter_context(nc.sbuf_tensor("yc", [128, D], F32))
                yo = esB.enter_context(nc.sbuf_tensor("yo", [128, D], F32))
                sLcc = [Sem(es, nc, f"sLcc{b}", "sync") for b in range(2)]
                sLx1c = [Sem(es, nc, f"sLx1c{b}", "gpsimd") for b in range(2)]
                sSy = Sem(es, nc, "sSy", "scalar")
            add_done = {}
            sq_doneC = {}
            sacc_counts = {}

            def emit_C(blk):
                sc0, sc1 = sacc_counts[blk]
                ph.wait("sync", sSacc[0], sc0)
                ph.wait("sync", sSacc[1], sc1)
                for k2 in range(NTB):
                    it = blk * NTB + k2
                    b = it % 2
                    trow = it * TT
                    if it >= 2:
                        ph.wait("sync", sAC, sq_doneC[it - 2])
                        ph.wait("gpsimd", sAC, sq_doneC[it - 2])
                    if sx_done:
                        ph.wait("gpsimd", sSx1, sx_done[it][1])
                    ph.dma("sync", acc_c[b][:, :], accT_d[:, trow:trow + TT],
                           sLcc[b], transpose=True)
                    ph.dma("gpsimd", x1c[b][:, :], x1_d[trow:trow + TT, :],
                           sLx1c[b])
                    ph.wait("vector", sLcc[b])
                    ph.wait("vector", sLx1c[b])
                    ph.wait("vector", sSy)
                    add_done[it] = ph.op("vector", lambda e, b=b: e.tensor_tensor(
                        yc[:, :], acc_c[b][:, :], x1c[b][:, :], ALU.add), (sDV, 1))
                    ph.wait("scalar", sDV, add_done[it])
                    ac_sq = ph.op("scalar", lambda e, b=b: e.activation(
                        acc_c[b][:, :], yc[:, :], AF.Square, accum_out=SC(7)),
                        (sAC, 1))
                    sq_doneC[it] = ac_sq
                    ph.raw("vector", lambda e: e.reduce_sum(SC(8), yc[:, :],
                                                            axis=AX.X))
                    ph.raw("vector", lambda e: e.tensor_scalar(
                        SC(8), SC(8), 1.0 / D, None, ALU.mult))
                    ph.raw("vector", lambda e: e.tensor_tensor(SC(9), SC(8), SC(8),
                                                               ALU.mult))
                    ph.wait("vector", sAC, ac_sq)
                    dv_var = ph.op("vector", lambda e: e.scalar_tensor_tensor(
                        SC(10), SC(7), 1.0 / D, SC(9), ALU.mult, ALU.subtract),
                        (sDV, 1))
                    ph.wait("scalar", sDV, dv_var)
                    ph.wait("scalar", sLc0, consts_ready)
                    ph.raw("scalar", lambda e: e.activation(SC(11), SC(10), AF.Ln,
                                                            bias=CST(1)))
                    ph.raw("scalar", lambda e: e.activation(
                        SC(11), SC(11), AF.Identity, scale=-0.5))
                    ac_r = ph.op("scalar", lambda e: e.activation(
                        SC(11), SC(11), AF.Exp), (sAC, 1))
                    ph.wait("vector", sAC, ac_r)
                    dv_y = ph.op("vector", lambda e: e.tensor_scalar(
                        yo[:, :], yc[:, :], SC(8), SC(11), ALU.subtract, ALU.mult),
                        (sDV, 1))
                    ph.wait("scalar", sDV, dv_y)
                    ph.dma("scalar", y_out[trow:trow + TT, :], yo[:, :], sSy)

            def load_w1(jj, bg, bu, sem):
                for buf, co in ((bg, jj * 512), (bu, I + jj * 512)):
                    for sp in range(2):
                        c0 = sp * 8
                        srcap = bass.AP(qw1_d.tensor, (c0 * 128) * (2 * I) + co,
                                        [[2 * I, 128], [128 * 2 * I, 8], [1, 512]])
                        dstap = bass.AP(buf, c0 * 512,
                                        [[NC * 512, 128], [512, 8], [1, 512]])
                        ph.dma("sync", dstap, srcap, sem)
                return sem.n

            def load_w2(ocu, buf, sem):
                for sp in range(6):
                    c0, c1 = sp * 8, min((sp + 1) * 8, NCI)
                    if c0 >= c1:
                        break
                    srcap = bass.AP(qw2_d.tensor, (c0 * 128) * D + ocu * 128,
                                    [[D, 128], [128 * D, c1 - c0], [1, 128]])
                    dstap = bass.AP(buf, c0 * 128,
                                    [[NCI * 128, 128], [128, c1 - c0], [1, 128]])
                    ph.dma("sync", dstap, srcap, sem)
                return sem.n

            pe_pair = {}
            pe_down = {}
            hmul = {}
            fps_free = {}
            sil_free = 0
            for s_ in w_store_sems:
                ph.wait("sync", s_)
            for blk_i in range(NB):
                t0 = blk_i * TB
                ph.wait("sync", sPE, pe_pair.get((blk_i - 1, NJ - 1), 0))
                if sx_done:
                    ph.wait("sync", sSx, sx_done[blk_i * NTB + NTB - 1][0])
                for c in range(NC):
                    ph.dma("sync", qxT[:, c * TB:(c + 1) * TB],
                           qx_d[t0:t0 + TB, c * 128:(c + 1) * 128], sLqxT,
                           transpose=True)
                qxt_ready = sLqxT.n
                for jj in range(NJ):
                    wb = jj % 2
                    guard = pe_pair.get((blk_i, jj - 2),
                                        pe_pair.get((blk_i - 1, NJ - 2 + jj), 0))
                    ph.wait("sync", sPE, guard)
                    w_ready = load_w1(jj, wg[wb], wu[wb], sLw1[wb])
                    for ti in range(NTB):
                        pp = ti % 2
                        ph.wait("tensor", sLw1[wb], w_ready)
                        ph.wait("tensor", sLqxT, qxt_ready)
                        key = ((blk_i, jj, ti - 2) if ti >= 2
                               else (blk_i, jj - 1, NTB - 2 + ti))
                        ph.wait("tensor", sDV, hmul.get(key, 0))
                        for c in range(NC):
                            lhs = qxT[:, c * TB + ti * 128: c * TB + (ti + 1) * 128]
                            st, sp = (c == 0), (c == NC - 1)
                            ph.raw("tensor",
                                   lambda e, l=lhs, c=c, pp=pp, st=st, sp=sp, wb=wb:
                                   e.matmul(gps[:, pp * 512:(pp + 1) * 512], l,
                                            wg[wb][:, c * 512:(c + 1) * 512],
                                            start=st, stop=sp))
                            ph.raw("tensor",
                                   lambda e, l=lhs, c=c, pp=pp, st=st, sp=sp, wb=wb:
                                   e.matmul(ups[:, pp * 512:(pp + 1) * 512], l,
                                            wu[wb][:, c * 512:(c + 1) * 512],
                                            start=st, stop=sp))
                        pm = ph.inc_last("tensor", sPE)
                        if ti == NTB - 1:
                            pe_pair[(blk_i, jj)] = pm
                        ph.wait("scalar", sPE, pm)
                        if sil_free:
                            ph.wait("scalar", sDV, sil_free)
                        ac_si = ph.op("scalar", lambda e, pp=pp: e.activation(
                            silu_t[:, :], gps[:, pp * 512:(pp + 1) * 512], AF.Silu),
                            (sAC, 1))
                        ph.wait("vector", sAC, ac_si)
                        hmul[(blk_i, jj, ti)] = ph.op(
                            "vector", lambda e, pp=pp, ti=ti, jj=jj: e.tensor_tensor(
                                hbuf[:, ti * I + jj * 512: ti * I + (jj + 1) * 512],
                                silu_t[:, :], ups[:, pp * 512:(pp + 1) * 512], ALU.mult),
                            (sDV, 1))
                        sil_free = hmul[(blk_i, jj, ti)]
                if do_C and blk_i >= 1:
                    emit_C(blk_i - 1)
                tr_prev = 0
                tps_ev = 0
                for ti in range(NTB):
                    cq = 30 + 3 * ti
                    ph.wait("vector", sDV, hmul[(blk_i, NJ - 1, ti)])
                    ph.raw("vector", lambda e, ti=ti, cq=cq: e.reduce_max(
                        SC(cq), hbuf[:, ti * I:(ti + 1) * I], axis=AX.X,
                        apply_absolute_value=True))
                    ph.raw("vector", lambda e, cq=cq: e.tensor_scalar(
                        SC(cq + 1), SC(cq), 1.0 / 127.0, 1e-8, ALU.mult, ALU.max))
                    dv_r = ph.op("vector", lambda e, cq=cq: e.reciprocal(
                        SC(cq + 2), SC(cq + 1)), (sDV, 1))
                    ph.wait("scalar", sDV, dv_r)
                    if tr_prev:
                        ph.wait("scalar", sPE, tr_prev)
                    for sl in range((I + 1023) // 1024):
                        f0, f1 = sl * 1024, min((sl + 1) * 1024, I)
                        ph.raw("scalar", lambda e, ti=ti, f0=f0, f1=f1, cq=cq:
                               e.activation(
                            i32b[:, 0:f1 - f0], hbuf[:, ti * I + f0: ti * I + f1],
                            AF.Identity, scale=SC(cq + 2)))
                        ph.raw("scalar", lambda e, f0=f0, f1=f1, cq=cq: e.activation(
                            qh_t[:, f0:f1], i32b[:, 0:f1 - f0], AF.Identity,
                            scale=SC(cq + 1)))
                    ac_q = ph.mark("scalar", sAC)
                    ph.wait("tensor", sAC, ac_q)
                    for g in range(6):
                        c0, c1 = g * 8, min((g + 1) * 8, NCI)
                        if tps_ev:
                            ph.wait("tensor", sDV, tps_ev)
                        for c in range(c0, c1):
                            ph.raw("tensor", lambda e, c=c, c0=c0: e.transpose(
                                tps[:, (c - c0) * 128:(c - c0 + 1) * 128],
                                qh_t[:, c * 128:(c + 1) * 128], id16[:, :]))
                        pm = ph.inc_last("tensor", sPE)
                        ph.wait("vector", sPE, pm)
                        tps_ev = ph.op(
                            "vector", lambda e, ti=ti, c0=c0, c1=c1: e.tensor_copy(
                                hbuf[:, ti * I + c0 * 128: ti * I + c1 * 128],
                                tps[:, 0:(c1 - c0) * 128]), (sDV, 1))
                    tr_prev = pm
                qh_ready = tps_ev
                for ocu in range(16):
                    b2 = ocu % 2
                    guard = pe_down.get((blk_i, ocu - 2),
                                        pe_down.get((blk_i - 1, 14 + ocu), 0))
                    ph.wait("sync", sPE, guard)
                    w2_ready = load_w2(ocu, w2T[b2], sLw2[b2])
                    ph.wait("tensor", sLw2[b2], w2_ready)
                    ph.wait("tensor", sDV, qh_ready)
                    ou = blk_i * 16 + ocu
                    fb = ou % 3
                    if (ou - 3) >= 0 and (ou - 3) in fps_free:
                        ph.wait("tensor", sDV, fps_free[ou - 3])
                    for c2 in range(NCI):
                        rhs = bass.AP(hbuf, c2 * 128,
                                      [[NTB * I, 128], [I, NTB], [1, 128]])
                        ph.raw("tensor", lambda e, c2=c2, b2=b2, fb=fb,
                               rhs=rhs: e.matmul(
                            fps[fb][:, 0:512],
                            w2T[b2][:, c2 * 128:(c2 + 1) * 128],
                            rhs, start=(c2 == 0), stop=(c2 == NCI - 1)))
                    pmd = ph.inc_last("tensor", sPE)
                    pe_down[(blk_i, ocu)] = pmd
                    orow = ocu * 128
                    ab = ou % 2
                    ph.wait("vector", sPE, pmd)
                    ph.wait("vector", sSacc[ab])
                    dv_e = ph.op("vector", lambda e, fb=fb, ab=ab: e.tensor_copy(
                        accst[ab][:, :], fps[fb][:, 0:512]), (sDV, 1))
                    fps_free[ou] = dv_e
                    ph.wait("scalar", sDV, dv_e)
                    ph.dma("scalar", accT_d[orow:orow + 128, t0:t0 + TB],
                           accst[ab][:, :], sSacc[ab])
                sacc_counts[blk_i] = (sSacc[0].n, sSacc[1].n)
            if do_C:
                emit_C(NB - 1)
            ph.flush()
            esB.close()

    return nc


def _host_arrays(q_proj, k_proj, v_proj, o_proj, q_norm, k_norm, w_gate_up, w_down):
    f32 = np.float32
    arrs = {}
    arrs["qTw"] = np.ascontiguousarray(np.asarray(q_proj, f32).T)
    arrs["kTw"] = np.ascontiguousarray(np.asarray(k_proj, f32).T)
    arrs["vTw"] = np.ascontiguousarray(np.asarray(v_proj, f32).T)
    arrs["oTw"] = np.ascontiguousarray(np.asarray(o_proj, f32).T)
    arrs["w1n"] = np.ascontiguousarray(np.asarray(w_gate_up, f32))
    arrs["w2n"] = np.ascontiguousarray(np.asarray(w_down, f32))
    arrs["qnb"] = np.ascontiguousarray(
        np.broadcast_to(np.tile(np.asarray(q_norm, f32), NH)[None, :], (128, D)))
    arrs["knb"] = np.ascontiguousarray(
        np.broadcast_to(np.asarray(k_norm, f32)[None, :], (128, HD)))
    hd = np.float32(HD)
    idx = np.arange(128, dtype=f32)
    inv = (1.0 / np.power(np.float32(1000000.0), (2.0 * idx / hd), dtype=f32)).astype(f32)
    pos = np.arange(L, dtype=f32)
    fr = pos[:, None] * inv[None, :]
    c = np.cos(fr).astype(f32)
    s = np.sin(fr).astype(f32)
    arrs["ctab"] = np.ascontiguousarray(np.concatenate([c, c], axis=1))
    arrs["stab"] = np.ascontiguousarray(np.concatenate([-s, s], axis=1))
    m = np.zeros((128, 128), f32)
    m[np.triu_indices(128, 1)] = NEGBIG
    arrs["mskd"] = m
    arrs["id16"] = np.eye(128, dtype=BF)
    cstv = np.zeros((128, 8), f32)
    cstv[:, 0] = 1e-6
    cstv[:, 1] = 1e-5
    arrs["cst"] = cstv
    return arrs


def kernel(L0_attn_x, L0_attn_q_proj, L0_attn_k_proj, L0_attn_v_proj, L0_attn_o_proj,
           L0_attn_q_norm, L0_attn_k_norm, L0_ffn_w_gate_up, L0_ffn_w_down):
    n = 8
    x = np.asarray(L0_attn_x, np.float32)
    B, Lq, Dm = x.shape
    shared = _host_arrays(L0_attn_q_proj, L0_attn_k_proj, L0_attn_v_proj, L0_attn_o_proj,
                          L0_attn_q_norm, L0_attn_k_norm, L0_ffn_w_gate_up, L0_ffn_w_down)
    nc = build()
    in_maps = []
    for c in range(n):
        m = dict(shared)
        m["x_in"] = np.ascontiguousarray(x[2 * c:2 * c + 2].reshape(SEQ * L, D))
        in_maps.append(m)
    res = run_bass_kernel_spmd(nc, in_maps, core_ids=list(range(n)))
    out = np.empty((B, Lq, Dm), np.float32)
    for c in range(n):
        out[2 * c:2 * c + 2] = res.results[c]["y"].reshape(SEQ, L, D)
    return out


# version 11
# speedup vs baseline: 1.4345x; 1.1204x over previous
import os
import numpy as np
import ml_dtypes
import concourse.bass as bass
import concourse.mybir as mybir
from concourse.bass_utils import run_bass_kernel_spmd
from contextlib import ExitStack

F32 = mybir.dt.float32
BF16 = mybir.dt.bfloat16
I32 = mybir.dt.int32
AF = mybir.ActivationFunctionType
ALU = mybir.AluOpType
AX = mybir.AxisListType
BF = ml_dtypes.bfloat16

SEQ = 2
L = 2048
D = 2048
NH = 8
HD = 256
I = 5632
TT = 128
NTS = L // TT
NC = 16
NCI = I // 128
TB = 512
NB = SEQ * L // TB
NTB = TB // TT
NJ = I // 512
TL = SEQ * L
ENGS = ("sync", "tensor", "vector", "scalar", "gpsimd")
NEGBIG = -30000.0
ALLPH = ("A0", "W", "A1", "A2", "B", "C")


class Sem:
    def __init__(self, es, nc, name, queue=None):
        self.h = es.enter_context(nc.semaphore(name))
        self.n = 0
        self.queue = queue


DRAIN_ENGS = ("vector", "scalar", "gpsimd")


class Ph:
    def __init__(self, nc):
        self.nc = nc
        self.ops = {e: [] for e in ENGS}
        self.dma_sems = []
        self.drained = {e: True for e in ENGS}

    def raw(self, eng, fn):
        if eng in DRAIN_ENGS and not self.drained[eng]:
            self.ops[eng].append(lambda e: e.drain())
        self.drained[eng] = False
        self.ops[eng].append(fn)

    def op(self, eng, fn, inc=None):
        self.raw(eng, fn)
        if inc is None:
            return None
        sem, k = inc
        sem.n += k
        self.ops[eng].append(lambda e, h=sem.h, k=k: e.drain().then_inc(h, k))
        self.drained[eng] = True
        return sem.n

    def wait(self, eng, sem, val=None):
        v = sem.n if val is None else val
        if v > 0:
            self.ops[eng].append(lambda e, h=sem.h, v=v: e.wait_ge(h, v))

    def dma(self, q, dst, src, sem, transpose=False):
        assert sem.queue == q
        if sem not in self.dma_sems:
            self.dma_sems.append(sem)
        sem.n += 16
        if transpose:
            self.ops[q].append(lambda e, d=dst, s=src, h=sem.h:
                               e.dma_start(d, s, transpose=True).then_inc(h, 16))
        else:
            self.ops[q].append(lambda e, d=dst, s=src, h=sem.h:
                               e.dma_start(d, s).then_inc(h, 16))
        return sem.n

    def mark(self, eng, sem):
        sem.n += 1
        self.ops[eng].append(lambda e, h=sem.h: e.drain().then_inc(h, 1))
        self.drained[eng] = True
        return sem.n

    def inc_last(self, eng, sem, k=1):
        sem.n += k
        old = self.ops[eng][-1]
        self.ops[eng][-1] = lambda e, old=old, h=sem.h, k=k: old(e).then_inc(h, k)
        return sem.n

    def flush(self):
        for sem in self.dma_sems:
            self.wait(sem.queue, sem)
        self.dma_sems = []
        with self.nc.Block() as blk:
            for e in ENGS:
                lst = self.ops[e]
                if lst:
                    getattr(blk, e)(lambda eng, lst=lst: [f(eng) for f in lst])
        self.ops = {e: [] for e in ENGS}
        self.drained = {e: True for e in ENGS}


def build(debug=False, phases=ALLPH):
    nc = bass.Bass("TRN2", target_bir_lowering=False)

    x_in = nc.dram_tensor("x_in", [TL, D], F32, kind="ExternalInput").ap()
    qTw_h = nc.dram_tensor("qTw", [D, D], F32, kind="ExternalInput").ap()
    kTw_h = nc.dram_tensor("kTw", [D, HD], F32, kind="ExternalInput").ap()
    vTw_h = nc.dram_tensor("vTw", [D, HD], F32, kind="ExternalInput").ap()
    oTw_h = nc.dram_tensor("oTw", [D, D], F32, kind="ExternalInput").ap()
    w1_h = nc.dram_tensor("w1n", [2 * I, D], F32, kind="ExternalInput").ap()
    w2_h = nc.dram_tensor("w2n", [D, I], F32, kind="ExternalInput").ap()
    qnb_h = nc.dram_tensor("qnb", [128, D], F32, kind="ExternalInput").ap()
    knb_h = nc.dram_tensor("knb", [128, HD], F32, kind="ExternalInput").ap()
    ctab_h = nc.dram_tensor("ctab", [L, HD], F32, kind="ExternalInput").ap()
    stab_h = nc.dram_tensor("stab", [L, HD], F32, kind="ExternalInput").ap()
    mskd_h = nc.dram_tensor("mskd", [128, 128], F32, kind="ExternalInput").ap()
    id16_h = nc.dram_tensor("id16", [128, 128], BF16, kind="ExternalInput").ap()
    cst_h = nc.dram_tensor("cst", [128, 8], F32, kind="ExternalInput").ap()
    y_out = nc.dram_tensor("y", [TL, D], F32, kind="ExternalOutput").ap()

    dbg = "ExternalOutput" if debug else "Internal"
    xbf_d = nc.dram_tensor("xbf_d", [TL, D], BF16, kind=dbg).ap()
    qbf_d = nc.dram_tensor("qbf_d", [TL, D], BF16, kind=dbg).ap()
    kbf_d = nc.dram_tensor("kbf_d", [TL, HD], BF16, kind=dbg).ap()
    x1_d = nc.dram_tensor("x1_d", [TL, D], BF16, kind=dbg).ap()
    qx_d = nc.dram_tensor("qx_d", [TL, D], BF16, kind=dbg).ap()
    qw1_d = nc.dram_tensor("qw1_d", [D, 2 * I], BF16, kind=dbg).ap()
    qw2_d = nc.dram_tensor("qw2_d", [I, D], BF16, kind=dbg).ap()
    accT_d = nc.dram_tensor("accT_d", [D, TL], BF16, kind=dbg).ap()

    es = ExitStack()
    with es:
        sPE = Sem(es, nc, "sPE")
        sDV = Sem(es, nc, "sDV")
        sAC = Sem(es, nc, "sAC")
        sGP = Sem(es, nc, "sGP")

        id16 = es.enter_context(nc.sbuf_tensor("id16s", [128, 128], BF16))
        mskd = es.enter_context(nc.sbuf_tensor("mskds", [128, 128], F32))
        sca = es.enter_context(nc.sbuf_tensor("sca", [128, 64], F32))
        cst = es.enter_context(nc.sbuf_tensor("cst_s", [128, 8], F32))

        def CST(c):
            return cst[:, c:c + 1]

        def SC(c, n=1):
            return sca[:, c:c + n]

        ph = Ph(nc)

        sLc0 = Sem(es, nc, "sLc0", "sync")
        ph.dma("sync", id16[:, :], id16_h[:, :], sLc0)
        ph.dma("sync", mskd[:, :], mskd_h[:, :], sLc0)
        ph.dma("sync", cst[:, :], cst_h[:, :], sLc0)
        consts_ready = sLc0.n
        a0_sems = []
        if "A0" in phases:
            sLa = Sem(es, nc, "sLa", "gpsimd")
            a0_sems = [sLa]
            for it in range(TL // TT):
                ph.dma("gpsimd", xbf_d[it * TT:(it + 1) * TT, :],
                       x_in[it * TT:(it + 1) * TT, :], sLa)

        w_store_sems = []
        if "W" in phases:
            esW = ExitStack()
            NWB = 4
            wst = [esW.enter_context(nc.sbuf_tensor(f"wst{b}", [128, I], F32))
                   for b in range(NWB)]
            i32w = [esW.enter_context(nc.sbuf_tensor(f"i32w{b}", [128, I], I32))
                    for b in range(2)]
            qwst = [esW.enter_context(nc.sbuf_tensor(f"qwst{b}", [128, I], BF16))
                    for b in range(2)]
            stg = [esW.enter_context(nc.sbuf_tensor(f"stg{b}", [128, I], BF16))
                   for b in range(NWB)]
            pst = [esW.enter_context(nc.psum_tensor(f"pst{b}", [128, 2048], BF16))
                   for b in range(2)]
            sLw = [Sem(es, nc, f"sLw{b}", "sync") for b in range(NWB)]
            sSw = [Sem(es, nc, f"sSw{b}", "scalar") for b in range(NWB)]
            w_store_sems = sSw

            strips = [("w1", s, D, NC) for s in range(2 * I // 128)] + \
                     [("w2", s, I, NCI) for s in range(D // 128)]
            _wl = os.environ.get("K_W_STRIPS")
            if _wl:
                strips = strips[:int(_wl)]
            amax_done = {}
            rnd_done = {}
            dq_done = {}
            tr_done = {}
            ev_done = {}
            for si, (wn, s, wd, nch) in enumerate(strips):
                b = si % NWB
                b2 = si % 2
                cq = 4 + 3 * (si % 4)
                src_ = (w1_h[s * 128:(s + 1) * 128, :] if wn == "w1"
                        else w2_h[s * 128:(s + 1) * 128, :])
                if si >= NWB:
                    ph.wait("sync", sAC, rnd_done[si - NWB])
                ph.dma("sync", wst[b][:, 0:wd], src_, sLw[b])
                ph.wait("vector", sLw[b])
                ph.raw("vector", lambda e, b=b, wd=wd, cq=cq: e.reduce_max(
                    SC(cq), wst[b][:, 0:wd], axis=AX.X, apply_absolute_value=True))
                ph.raw("vector", lambda e, cq=cq: e.tensor_scalar(
                    SC(cq + 1), SC(cq), 1.0 / 127.0, 1e-8, ALU.mult, ALU.max))
                amax_done[si] = ph.op("vector", lambda e, cq=cq: e.reciprocal(
                    SC(cq + 2), SC(cq + 1)), (sDV, 1))
                ph.wait("scalar", sDV, amax_done[si])
                if si >= 2:
                    ph.wait("scalar", sPE, tr_done[si - 2])
                ph.raw("scalar", lambda e, b=b, b2=b2, wd=wd, cq=cq: e.activation(
                    i32w[b2][:, 0:wd], wst[b][:, 0:wd], AF.Identity, scale=SC(cq + 2)))
                rnd_done[si] = ph.inc_last("scalar", sAC)
                ph.raw("scalar", lambda e, b2=b2, wd=wd, cq=cq: e.activation(
                    qwst[b2][:, 0:wd], i32w[b2][:, 0:wd], AF.Identity, scale=SC(cq + 1)))
                dq_done[si] = ph.inc_last("scalar", sAC)
                ph.wait("tensor", sAC, dq_done[si])
                ph.wait("tensor", sLc0, consts_ready)
                ph.wait("tensor", sSw[b])
                if si >= 2:
                    ph.wait("tensor", sDV, ev_done[si - 2])
                ngr = (nch + 15) // 16
                for g in range(ngr):
                    c0, c1 = g * 16, min((g + 1) * 16, nch)
                    if g > 0:
                        ph.wait("tensor", sDV, sDV.n)
                    for c in range(c0, c1):
                        ph.raw("tensor", lambda e, c=c, c0=c0, b2=b2: e.transpose(
                            pst[b2][:, (c - c0) * 128:(c - c0 + 1) * 128],
                            qwst[b2][:, c * 128:(c + 1) * 128], id16[:, :]))
                    pm = ph.inc_last("tensor", sPE)
                    ph.wait("vector", sPE, pm)
                    ph.op("vector", lambda e, b=b, b2=b2, c0=c0, c1=c1: e.tensor_copy(
                        stg[b][:, c0 * 128:c1 * 128], pst[b2][:, 0:(c1 - c0) * 128]),
                        (sDV, 1))
                tr_done[si] = pm
                ev_done[si] = sDV.n
                ph.wait("scalar", sDV, ev_done[si])
                nsp = 2 if wn == "w1" else 4
                csz = (nch + nsp - 1) // nsp
                dT, drows = (qw1_d, 2 * I) if wn == "w1" else (qw2_d, D)
                for sp in range(nsp):
                    c0, c1 = sp * csz, min((sp + 1) * csz, nch)
                    if c0 >= c1:
                        continue
                    srcap = bass.AP(stg[b], c0 * 128, [[I, 128], [128, c1 - c0], [1, 128]])
                    dstap = bass.AP(dT.tensor, c0 * 128 * drows + s * 128,
                                    [[drows, 128], [128 * drows, c1 - c0], [1, 128]])
                    ph.dma("scalar", dstap, srcap, sSw[b])
            ph.flush()
            esW.close()

        esAT = ExitStack()
        sqk_done = {}
        sSqk = None
        sx_done = {}
        if "A1" in phases or "A2" in phases:
            v_sb = esAT.enter_context(nc.sbuf_tensor("v_sb", [128, SEQ * NTS * HD], BF16))

        if "A1" in phases:
            esA1 = ExitStack()
            qTw = esA1.enter_context(nc.sbuf_tensor("qTwS", [128, NC * D], BF16))
            kTw = esA1.enter_context(nc.sbuf_tensor("kTwS", [128, NC * HD], BF16))
            vTw = esA1.enter_context(nc.sbuf_tensor("vTwS", [128, NC * HD], BF16))
            xT = [esA1.enter_context(nc.sbuf_tensor(f"xT{b}", [128, NC * 512], BF16))
                  for b in range(2)]
            qnb = esA1.enter_context(nc.sbuf_tensor("qnbS", [128, D], F32))
            knb = esA1.enter_context(nc.sbuf_tensor("knbS", [128, HD], F32))
            ctab = [esA1.enter_context(nc.sbuf_tensor(f"ctab{b}", [128, HD], F32))
                    for b in range(2)]
            stab = [esA1.enter_context(nc.sbuf_tensor(f"stab{b}", [128, HD], F32))
                    for b in range(2)]
            qhat = esA1.enter_context(nc.sbuf_tensor("qhat", [128, D], F32))
            rt1 = esA1.enter_context(nc.sbuf_tensor("rt1", [128, D], F32))
            rt2 = esA1.enter_context(nc.sbuf_tensor("rt2", [128, D], F32))
            qbf = esA1.enter_context(nc.sbuf_tensor("qbf", [128, D], BF16))
            khat = esA1.enter_context(nc.sbuf_tensor("khat", [128, HD], F32))
            kr1 = esA1.enter_context(nc.sbuf_tensor("kr1", [128, HD], F32))
            kr2 = esA1.enter_context(nc.sbuf_tensor("kr2", [128, HD], F32))
            kbf = esA1.enter_context(nc.sbuf_tensor("kbf", [128, HD], BF16))
            sqg = esA1.enter_context(nc.sbuf_tensor("sqg", [128, HD], F32))
            qps = esA1.enter_context(nc.psum_tensor("qps", [128, D], F32))
            kps = esA1.enter_context(nc.psum_tensor("kps", [128, 512], F32))
            vps = esA1.enter_context(nc.psum_tensor("vps", [128, 512], F32))
            sLwq = Sem(es, nc, "sLwq", "gpsimd")
            sLnb = Sem(es, nc, "sLnb", "sync")
            sLxT = [Sem(es, nc, f"sLxT{b}", "sync") for b in range(2)]
            sLtab = [Sem(es, nc, f"sLtab{b}", "scalar") for b in range(2)]
            sSqk = Sem(es, nc, "sSqk", "scalar")

            for c in range(NC):
                ph.dma("gpsimd", qTw[:, c * D:(c + 1) * D],
                       qTw_h[c * 128:(c + 1) * 128, :], sLwq)
                ph.dma("gpsimd", kTw[:, c * HD:(c + 1) * HD],
                       kTw_h[c * 128:(c + 1) * 128, :], sLwq)
                ph.dma("gpsimd", vTw[:, c * HD:(c + 1) * HD],
                       vTw_h[c * 128:(c + 1) * 128, :], sLwq)
            wload = sLwq.n
            ph.dma("sync", qnb[:, :], qnb_h[:, :], sLnb)
            ph.dma("sync", knb[:, :], knb_h[:, :], sLnb)
            nb_ready = sLnb.n

            NQ = 4
            pe_tile_done = {}
            psum_free = {}
            gp_done = {}
            dv_rope_done = {}
            for s_ in a0_sems:
                ph.wait("sync", s_)
            for qb_i in range(TL // (NQ * TT)):
                b = qb_i % 2
                sq = qb_i // 4
                q_in_seq = qb_i % 4
                if qb_i >= 2:
                    ph.wait("sync", sPE, pe_tile_done.get((qb_i - 2, NQ - 1), 0))
                for c in range(NC):
                    ph.dma("sync", xT[b][:, c * 512:(c + 1) * 512],
                           xbf_d[sq * L + q_in_seq * 512: sq * L + (q_in_seq + 1) * 512,
                                 c * 128:(c + 1) * 128], sLxT[b], transpose=True)
                xt_ready = sLxT[b].n
                for ti in range(NQ):
                    it = qb_i * NQ + ti
                    trow = it * TT
                    tb_ = it % 2
                    if it >= 2:
                        ph.wait("scalar", sGP, gp_done.get(it - 2, 0))
                        ph.wait("scalar", sDV, dv_rope_done.get(it - 2, 0))
                    ph.dma("scalar", ctab[tb_][:, :],
                           ctab_h[(it % NTS) * TT:((it % NTS) + 1) * TT, :], sLtab[tb_])
                    ph.dma("scalar", stab[tb_][:, :],
                           stab_h[(it % NTS) * TT:((it % NTS) + 1) * TT, :], sLtab[tb_])
                    tabs_ready = sLtab[tb_].n
                    ph.wait("tensor", sLwq, wload)
                    ph.wait("tensor", sLxT[b], xt_ready)
                    if it >= 1:
                        ph.wait("tensor", sDV, psum_free[it - 1])
                    for c in range(NC):
                        lhs = xT[b][:, c * 512 + ti * 128: c * 512 + (ti + 1) * 128]
                        st, sp = (c == 0), (c == NC - 1)
                        for ob in range(4):
                            ph.raw("tensor", lambda e, l=lhs, ob=ob, st=st, sp=sp, c=c:
                                   e.matmul(qps[:, ob * 512:(ob + 1) * 512], l,
                                            qTw[:, c * D + ob * 512: c * D + (ob + 1) * 512],
                                            start=st, stop=sp))
                        ph.raw("tensor", lambda e, l=lhs, st=st, sp=sp, c=c: e.matmul(
                            kps[:, 0:HD], l, kTw[:, c * HD:(c + 1) * HD],
                            start=st, stop=sp))
                        ph.raw("tensor", lambda e, l=lhs, st=st, sp=sp, c=c: e.matmul(
                            vps[:, 0:HD], l, vTw[:, c * HD:(c + 1) * HD],
                            start=st, stop=sp))
                    pe_tile_done[(qb_i, ti)] = ph.inc_last("tensor", sPE)
                    ph.wait("scalar", sPE, pe_tile_done[(qb_i, ti)])
                    for h in range(NH):
                        ph.raw("scalar", lambda e, h=h: e.activation(
                            sqg[:, :], qps[:, h * HD:(h + 1) * HD], AF.Square,
                            accum_out=SC(12 + h)))
                    ph.raw("scalar", lambda e: e.activation(
                        sqg[:, :], kps[:, 0:HD], AF.Square, accum_out=SC(20)))
                    ph.wait("scalar", sLc0, consts_ready)
                    ph.raw("scalar", lambda e: e.activation(
                        SC(21, 9), SC(12, 9), AF.Ln, scale=1.0 / 256.0, bias=CST(0)))
                    ph.raw("scalar", lambda e: e.activation(
                        SC(21, 9), SC(21, 9), AF.Identity, scale=-0.5))
                    ac_rms = ph.op("scalar", lambda e: e.activation(
                        SC(21, 9), SC(21, 9), AF.Exp), (sAC, 1))
                    ph.wait("vector", sAC, ac_rms)
                    ph.wait("vector", sLnb, nb_ready)
                    for h in range(NH):
                        ph.raw("vector", lambda e, h=h: e.scalar_tensor_tensor(
                            qhat[:, h * HD:(h + 1) * HD], qps[:, h * HD:(h + 1) * HD],
                            SC(21 + h), qnb[:, h * HD:(h + 1) * HD], ALU.mult, ALU.mult))
                    ph.raw("vector", lambda e: e.scalar_tensor_tensor(
                        khat[:, :], kps[:, 0:HD], SC(29), knb[:, :], ALU.mult, ALU.mult))
                    psum_free[it] = ph.op("vector", lambda e, it=it: e.tensor_copy(
                        v_sb[:, it * HD:(it + 1) * HD], vps[:, 0:HD]), (sDV, 1))
                    ct, st_ = ctab[tb_], stab[tb_]
                    ph.wait("gpsimd", sDV, psum_free[it])
                    ph.wait("gpsimd", sLtab[tb_], tabs_ready)
                    ph.wait("vector", sLtab[tb_], tabs_ready)
                    for h in range(NH):
                        q0 = h * HD
                        ph.raw("vector", lambda e, q0=q0, ct=ct: e.tensor_tensor(
                            rt1[:, q0:q0 + HD], qhat[:, q0:q0 + HD], ct[:, :], ALU.mult))
                        ph.raw("gpsimd", lambda e, q0=q0, st_=st_: e.tensor_tensor(
                            rt2[:, q0:q0 + 128], qhat[:, q0 + 128:q0 + HD],
                            st_[:, 0:128], ALU.mult))
                        ph.raw("gpsimd", lambda e, q0=q0, st_=st_: e.tensor_tensor(
                            rt2[:, q0 + 128:q0 + HD], qhat[:, q0:q0 + 128],
                            st_[:, 128:HD], ALU.mult))
                    gp_done[it] = ph.mark("gpsimd", sGP)
                    ph.wait("vector", sGP, gp_done[it])
                    ph.wait("vector", sSqk)
                    ph.raw("vector", lambda e: e.tensor_tensor(
                        qbf[:, :], rt1[:, :], rt2[:, :], ALU.add))
                    ph.raw("vector", lambda e, ct=ct: e.tensor_tensor(
                        kr1[:, :], khat[:, :], ct[:, :], ALU.mult))
                    ph.raw("vector", lambda e, st_=st_: e.tensor_tensor(
                        kr2[:, 0:128], khat[:, 128:HD], st_[:, 0:128], ALU.mult))
                    ph.raw("vector", lambda e, st_=st_: e.tensor_tensor(
                        kr2[:, 128:HD], khat[:, 0:128], st_[:, 128:HD], ALU.mult))
                    dv_rope_done[it] = ph.op("vector", lambda e: e.tensor_tensor(
                        kbf[:, :], kr1[:, :], kr2[:, :], ALU.add), (sDV, 1))
                    ph.wait("scalar", sDV, dv_rope_done[it])
                    ph.dma("scalar", qbf_d[trow:trow + TT, :], qbf[:, :], sSqk)
                    ph.dma("scalar", kbf_d[trow:trow + TT, :], kbf[:, :], sSqk)
                    sqk_done[it] = sSqk.n
            ph.flush()
            esA1.close()

        if "A2" in phases:
            esA2 = ExitStack()
            oTw = esA2.enter_context(nc.sbuf_tensor("oTwS", [128, NC * D], BF16))
            qT = esA2.enter_context(nc.sbuf_tensor("qTS", [128, NC * 512], BF16))
            kT = esA2.enter_context(nc.sbuf_tensor("kTS", [128, 2 * L], BF16))
            w_sb = [esA2.enter_context(nc.sbuf_tensor(f"w_sb{b}", [128, 2048], BF16))
                    for b in range(2)]
            wT_all = esA2.enter_context(
                nc.sbuf_tensor("wT_all", [128, NH * 2048], BF16))
            ctxT = esA2.enter_context(nc.sbuf_tensor("ctxT", [128, D], BF16))
            x_sb = [esA2.enter_context(nc.sbuf_tensor(f"x_sb{b}", [128, D], F32))
                    for b in range(2)]
            y_sb = esA2.enter_context(nc.sbuf_tensor("y_sb", [128, D], F32))
            i32a = esA2.enter_context(nc.sbuf_tensor("i32a", [128, D], I32))
            qx_sb = esA2.enter_context(nc.sbuf_tensor("qx_sb", [128, D], BF16))
            sqga = esA2.enter_context(nc.sbuf_tensor("sqga", [128, D], F32))
            scp = [esA2.enter_context(nc.psum_tensor(f"scp{b}", [128, 1024], F32))
                   for b in range(2)]
            psw = [esA2.enter_context(nc.psum_tensor(f"psw{b}", [128, 1024], BF16))
                   for b in range(2)]
            cop = esA2.enter_context(nc.psum_tensor("cop", [128, 1024], F32))
            sLo = Sem(es, nc, "sLo", "gpsimd")
            sLkT = Sem(es, nc, "sLkT", "sync")
            sLqT = Sem(es, nc, "sLqT", "sync")
            sLx = [Sem(es, nc, f"sLx{b}", "sync") for b in range(2)]
            sSx = Sem(es, nc, "sSx", "scalar")
            sSx1 = Sem(es, nc, "sSx1", "gpsimd")

            for c in range(NC):
                ph.dma("gpsimd", oTw[:, c * D:(c + 1) * D],
                       oTw_h[c * 128:(c + 1) * 128, :], sLo)
            oload = sLo.n

            scp_free = {0: 0, 1: 0}
            psw_ev = {0: 0, 1: 0}
            tr_done = {}
            norm_done = {}
            pend_tr = []
            last_sc_pm = 0
            kT_free = 0
            qT_free = 0
            xfree = {}
            cop_free = 0
            pswc = 0

            def emit_tr(gh, h, nch, Lk):
                nonlocal pswc
                wpar = gh % 2
                lastpm = 0
                for j0 in range(0, nch, 8):
                    j1 = min(j0 + 8, nch)
                    gp = pswc % 2
                    pswc += 1
                    ph.wait("tensor", sDV, norm_done[gh])
                    ph.wait("tensor", sDV, psw_ev[gp])
                    for j in range(j0, j1):
                        ph.raw("tensor", lambda e, gp=gp, j=j, j0=j0, wpar=wpar:
                               e.transpose(
                            psw[gp][:, (j - j0) * 128:(j - j0 + 1) * 128],
                            w_sb[wpar][:, j * 128:(j + 1) * 128], id16[:, :]))
                    lastpm = ph.inc_last("tensor", sPE)
                    ph.wait("vector", sPE, lastpm)
                    psw_ev[gp] = ph.op(
                        "vector", lambda e, gp=gp, h=h, j0=j0, j1=j1: e.tensor_copy(
                            wT_all[:, h * 2048 + j0 * 128: h * 2048 + j1 * 128],
                            psw[gp][:, 0:(j1 - j0) * 128]), (sDV, 1))
                tr_done[gh] = lastpm

            for sq in range(SEQ):
                if sqk_done:
                    ph.wait("sync", sSqk, sqk_done[sq * NTS + NTS - 1])
                if sq > 0:
                    ph.wait("sync", sPE, kT_free)
                for c2 in range(2):
                    ph.dma("sync", kT[:, c2 * L:(c2 + 1) * L],
                           kbf_d[sq * L:(sq + 1) * L, c2 * 128:(c2 + 1) * 128], sLkT,
                           transpose=True)
                kt_ready = sLkT.n
                for qh in range(4):
                    if sq + qh > 0:
                        ph.wait("sync", sPE, qT_free)
                    for c in range(NC):
                        ph.dma("sync", qT[:, c * 512:(c + 1) * 512],
                               qbf_d[sq * L + qh * 512: sq * L + (qh + 1) * 512,
                                     c * 128:(c + 1) * 128], sLqT, transpose=True)
                    qt_ready = sLqT.n
                    for ti in range(4):
                        i = qh * 4 + ti
                        it = sq * NTS + i
                        trow = it * TT
                        nch = i + 1
                        Lk = nch * 128
                        npt = 1 if nch <= 8 else 2
                        xb = it % 2
                        if it >= 2:
                            ph.wait("sync", sDV, xfree.get(it - 2, 0))
                        ph.dma("sync", x_sb[xb][:, :], x_in[trow:trow + TT, :],
                               sLx[xb])
                        x_ready = sLx[xb].n
                        for h in range(NH):
                            gh = it * NH + h
                            wpar = gh % 2
                            B0 = 12 + 8 * (gh % 2)
                            qts = [qT[:, c_ * 512 + ti * 128:
                                      c_ * 512 + (ti + 1) * 128]
                                   for c_ in (2 * h, 2 * h + 1)]
                            if h == 0 and ti == 0:
                                ph.wait("tensor", sLqT, qt_ready)
                                ph.wait("tensor", sLkT, kt_ready)
                            pids = []
                            for p in range(npt):
                                pidx = (gh % 2) if npt == 1 else p
                                pids.append(pidx)
                                pw = min(8, nch - p * 8)
                                ph.wait("tensor", sAC, scp_free[pidx])
                                nsg = (pw * 128 + 511) // 512
                                for c2 in range(2):
                                    for sg in range(nsg):
                                        s0 = sg * 512
                                        sw = min(512, pw * 128 - s0)
                                        ph.raw("tensor", lambda e, pidx=pidx,
                                               s0=s0, sw=sw, c2=c2, p=p, q_=qts[c2]:
                                               e.matmul(
                                            scp[pidx][:, s0:s0 + sw], q_,
                                            kT[:, c2 * L + p * 1024 + s0:
                                               c2 * L + p * 1024 + s0 + sw],
                                            start=(c2 == 0), stop=(c2 == 1)))
                                pm_sc = ph.inc_last("tensor", sPE)
                                last_sc_pm = pm_sc
                                ph.wait("vector", sPE, pm_sc)
                                if p == npt - 1:
                                    dcol = (nch - 1 - p * 8) * 128
                                    ph.raw("vector", lambda e, pidx=pidx, dcol=dcol:
                                           e.tensor_tensor(
                                        scp[pidx][:, dcol:dcol + 128],
                                        scp[pidx][:, dcol:dcol + 128],
                                        mskd[:, :], ALU.add))
                                ph.raw("vector", lambda e, pidx=pidx, pw=pw,
                                       B0=B0, p=p: e.reduce_max(
                                    SC(B0 + p), scp[pidx][:, 0:pw * 128],
                                    axis=AX.X, negate=True))
                            if npt == 2:
                                ph.raw("vector", lambda e, B0=B0: e.tensor_tensor(
                                    SC(B0 + 2), SC(B0), SC(B0 + 1), ALU.min))
                                bsrc = B0 + 2
                            else:
                                bsrc = B0
                            dv_b = ph.op("vector", lambda e, B0=B0, bsrc=bsrc:
                                         e.tensor_scalar(
                                SC(B0 + 3), SC(bsrc), 1.0 / 16.0, None, ALU.mult),
                                (sDV, 1))
                            if pend_tr:
                                emit_tr(*pend_tr.pop(0))
                            ph.wait("scalar", sDV, dv_b)
                            ph.wait("scalar", sPE, tr_done.get(gh - 2, 0))
                            for p in range(npt):
                                pidx = pids[p]
                                pw = min(8, nch - p * 8)
                                ph.raw("scalar", lambda e, pidx=pidx, pw=pw, p=p,
                                       wpar=wpar, B0=B0: e.activation(
                                    w_sb[wpar][:, p * 1024: p * 1024 + pw * 128],
                                    scp[pidx][:, 0:pw * 128], AF.Exp,
                                    bias=SC(B0 + 3), scale=1.0 / 16.0,
                                    accum_out=SC(B0 + 4 + p)))
                            ed = ph.inc_last("scalar", sAC)
                            for p in range(npt):
                                scp_free[pids[p]] = ed
                            ph.wait("vector", sAC, ed)
                            if npt == 2:
                                ph.raw("vector", lambda e, B0=B0: e.tensor_tensor(
                                    SC(B0 + 6), SC(B0 + 4), SC(B0 + 5), ALU.add))
                                ssrc = B0 + 6
                            else:
                                ssrc = B0 + 4
                            ph.raw("vector", lambda e, B0=B0, ssrc=ssrc:
                                   e.reciprocal(SC(B0 + 7), SC(ssrc)))
                            norm_done[gh] = ph.op(
                                "vector", lambda e, wpar=wpar, Lk=Lk, B0=B0:
                                e.tensor_scalar(
                                    w_sb[wpar][:, 0:Lk], w_sb[wpar][:, 0:Lk],
                                    SC(B0 + 7), None, ALU.mult), (sDV, 1))
                            pend_tr.append((gh, h, nch, Lk))
                        while pend_tr:
                            emit_tr(*pend_tr.pop(0))
                        ph.wait("tensor", sDV, psw_ev[0])
                        ph.wait("tensor", sDV, psw_ev[1])
                        ph.wait("tensor", sDV, cop_free)
                        for half in range(2):
                            for j in range(nch):
                                lhs = v_sb[:, (sq * NTS + j) * HD + half * 128:
                                           (sq * NTS + j) * HD + half * 128 + 128]
                                for sg in range(2):
                                    rhs = bass.AP(wT_all, (sg * 4) * 2048 + j * 128,
                                                  [[NH * 2048, 128], [2048, 4],
                                                   [1, 128]])
                                    ph.raw("tensor", lambda e, lhs=lhs, rhs=rhs,
                                           sg=sg, st=(j == 0), sp=(j == nch - 1):
                                           e.matmul(
                                        cop[:, sg * 512:(sg + 1) * 512], lhs, rhs,
                                        start=st, stop=sp))
                            pm_cx = ph.inc_last("tensor", sPE)
                            ph.wait("vector", sPE, pm_cx)
                            cop_free = ph.op(
                                "vector", lambda e, half=half: e.tensor_copy(
                                    ctxT[:, half * 1024:(half + 1) * 1024],
                                    cop[:, 0:1024]), (sDV, 1))
                            if half == 0:
                                ph.wait("tensor", sDV, cop_free)
                        ph.wait("tensor", sDV, cop_free)
                        ph.wait("tensor", sLo, oload)
                        ph.wait("vector", sLx[xb], x_ready)
                        ph.wait("vector", sSx)
                        ph.wait("vector", sSx1)
                        dv_y2 = 0
                        for h2 in range(2):
                            for c in range(16):
                                r = 2 * (c % 8) + (c // 8)
                                for sg in range(2):
                                    ph.raw("tensor", lambda e, c=c, r=r, sg=sg,
                                           h2=h2, st=(c == 0), sp=(c == 15):
                                           e.matmul(
                                        cop[:, sg * 512:(sg + 1) * 512],
                                        ctxT[:, c * 128:(c + 1) * 128],
                                        oTw[:, r * D + h2 * 1024 + sg * 512:
                                            r * D + h2 * 1024 + (sg + 1) * 512],
                                        start=st, stop=sp))
                            pm_o = ph.inc_last("tensor", sPE)
                            ph.wait("vector", sPE, pm_o)
                            dv_y2 = ph.op(
                                "vector", lambda e, h2=h2, xb=xb: e.tensor_tensor(
                                    y_sb[:, h2 * 1024:(h2 + 1) * 1024],
                                    cop[:, 0:1024],
                                    x_sb[xb][:, h2 * 1024:(h2 + 1) * 1024],
                                    ALU.add), (sDV, 1))
                            cop_free = dv_y2
                            if h2 == 0:
                                ph.wait("tensor", sDV, cop_free)
                        xfree[it] = dv_y2
                        ph.wait("scalar", sDV, dv_y2)
                        ac_sq = ph.op("scalar", lambda e: e.activation(
                            sqga[:, :], y_sb[:, :], AF.Square, accum_out=SC(39)),
                            (sAC, 1))
                        ph.raw("vector", lambda e: e.reduce_sum(SC(40), y_sb[:, :],
                                                                axis=AX.X))
                        ph.raw("vector", lambda e: e.tensor_scalar(
                            SC(40), SC(40), 1.0 / D, None, ALU.mult))
                        ph.raw("vector", lambda e: e.tensor_tensor(
                            SC(41), SC(40), SC(40), ALU.mult))
                        ph.wait("vector", sAC, ac_sq)
                        dv_var = ph.op("vector", lambda e: e.scalar_tensor_tensor(
                            SC(42), SC(39), 1.0 / D, SC(41), ALU.mult, ALU.subtract),
                            (sDV, 1))
                        ph.wait("scalar", sDV, dv_var)
                        ph.wait("scalar", sLc0, consts_ready)
                        ph.raw("scalar", lambda e: e.activation(
                            SC(43), SC(42), AF.Ln, bias=CST(1)))
                        ph.raw("scalar", lambda e: e.activation(
                            SC(43), SC(43), AF.Identity, scale=-0.5))
                        ac_rstd = ph.op("scalar", lambda e: e.activation(
                            SC(43), SC(43), AF.Exp), (sAC, 1))
                        ph.wait("vector", sAC, ac_rstd)
                        ph.raw("vector", lambda e: e.tensor_scalar(
                            y_sb[:, :], y_sb[:, :], SC(40), SC(43),
                            ALU.subtract, ALU.mult))
                        ph.raw("vector", lambda e: e.reduce_max(
                            SC(44), y_sb[:, :], axis=AX.X,
                            apply_absolute_value=True))
                        ph.raw("vector", lambda e: e.tensor_scalar(
                            SC(45), SC(44), 1.0 / 127.0, 1e-8, ALU.mult, ALU.max))
                        dv_q = ph.op("vector", lambda e: e.reciprocal(
                            SC(46), SC(45)), (sDV, 1))
                        ph.wait("scalar", sDV, dv_q)
                        ph.wait("scalar", sSx)
                        ph.raw("scalar", lambda e: e.activation(
                            i32a[:, :], y_sb[:, :], AF.Identity, scale=SC(46)))
                        epi = ph.op("scalar", lambda e: e.activation(
                            qx_sb[:, :], i32a[:, :], AF.Identity, scale=SC(45)),
                            (sAC, 1))
                        ph.wait("scalar", sAC, epi)
                        ph.wait("gpsimd", sAC, epi)
                        ph.dma("gpsimd", x1_d[trow:trow + TT, :], y_sb[:, :], sSx1)
                        ph.dma("scalar", qx_d[trow:trow + TT, :], qx_sb[:, :], sSx)
                        sx_done[it] = (sSx.n, sSx1.n)
                    qT_free = last_sc_pm
                kT_free = last_sc_pm
            ph.flush()
            esA2.close()
        esAT.close()

        if "B" in phases:
            do_C = "C" in phases
            esB = ExitStack()
            qxT = esB.enter_context(nc.sbuf_tensor("qxT", [128, NC * TB], BF16))
            hbuf = esB.enter_context(nc.sbuf_tensor("hbuf", [128, NTB * I], BF16))
            wg = [esB.enter_context(nc.sbuf_tensor(f"wg{b}", [128, NC * 512], BF16))
                  for b in range(2)]
            wu = [esB.enter_context(nc.sbuf_tensor(f"wu{b}", [128, NC * 512], BF16))
                  for b in range(2)]
            w2T = [esB.enter_context(nc.sbuf_tensor(f"w2T{b}", [128, NCI * 128], BF16))
                   for b in range(2)]
            silu_t = esB.enter_context(nc.sbuf_tensor("silu_t", [128, 512], F32))
            qh_t = esB.enter_context(nc.sbuf_tensor("qh_t", [128, I], BF16))
            i32b = esB.enter_context(nc.sbuf_tensor("i32b", [128, 1024], I32))
            accst = [esB.enter_context(nc.sbuf_tensor(f"accst{b}", [128, 512], BF16))
                     for b in range(2)]
            gps = esB.enter_context(nc.psum_tensor("gps", [128, 1024], F32))
            ups = esB.enter_context(nc.psum_tensor("ups", [128, 1024], F32))
            tps = esB.enter_context(nc.psum_tensor("tps", [128, 1024], BF16))
            fps = [esB.enter_context(nc.psum_tensor(f"fps{b}", [128, 512], F32))
                   for b in range(3)]
            sLqxT = Sem(es, nc, "sLqxT", "sync")
            sLw1 = [Sem(es, nc, f"sLw1{b}", "sync") for b in range(2)]
            sLw2 = [Sem(es, nc, f"sLw2{b}", "sync") for b in range(2)]
            sSacc = [Sem(es, nc, f"sSacc{b}", "scalar") for b in range(2)]
            if do_C:
                acc_c = [esB.enter_context(nc.sbuf_tensor(f"acc_c{b}", [128, D], BF16))
                         for b in range(2)]
                x1c = [esB.enter_context(nc.sbuf_tensor(f"x1c{b}", [128, D], BF16))
                       for b in range(2)]
                yc = esB.enter_context(nc.sbuf_tensor("yc", [128, D], F32))
                yo = esB.enter_context(nc.sbuf_tensor("yo", [128, D], F32))
                sLcc = [Sem(es, nc, f"sLcc{b}", "sync") for b in range(2)]
                sLx1c = [Sem(es, nc, f"sLx1c{b}", "gpsimd") for b in range(2)]
                sSy = Sem(es, nc, "sSy", "scalar")
            add_done = {}
            sq_doneC = {}
            sacc_counts = {}

            def emit_C(blk):
                sc0, sc1 = sacc_counts[blk]
                ph.wait("sync", sSacc[0], sc0)
                ph.wait("sync", sSacc[1], sc1)
                for k2 in range(NTB):
                    it = blk * NTB + k2
                    b = it % 2
                    trow = it * TT
                    if it >= 2:
                        ph.wait("sync", sAC, sq_doneC[it - 2])
                        ph.wait("gpsimd", sAC, sq_doneC[it - 2])
                    if sx_done:
                        ph.wait("gpsimd", sSx1, sx_done[it][1])
                    ph.dma("sync", acc_c[b][:, :], accT_d[:, trow:trow + TT],
                           sLcc[b], transpose=True)
                    ph.dma("gpsimd", x1c[b][:, :], x1_d[trow:trow + TT, :],
                           sLx1c[b])
                    ph.wait("vector", sLcc[b])
                    ph.wait("vector", sLx1c[b])
                    ph.wait("vector", sSy)
                    add_done[it] = ph.op("vector", lambda e, b=b: e.tensor_tensor(
                        yc[:, :], acc_c[b][:, :], x1c[b][:, :], ALU.add), (sDV, 1))
                    ph.wait("scalar", sDV, add_done[it])
                    ac_sq = ph.op("scalar", lambda e, b=b: e.activation(
                        acc_c[b][:, :], yc[:, :], AF.Square, accum_out=SC(7)),
                        (sAC, 1))
                    sq_doneC[it] = ac_sq
                    ph.raw("vector", lambda e: e.reduce_sum(SC(8), yc[:, :],
                                                            axis=AX.X))
                    ph.raw("vector", lambda e: e.tensor_scalar(
                        SC(8), SC(8), 1.0 / D, None, ALU.mult))
                    ph.raw("vector", lambda e: e.tensor_tensor(SC(9), SC(8), SC(8),
                                                               ALU.mult))
                    ph.wait("vector", sAC, ac_sq)
                    dv_var = ph.op("vector", lambda e: e.scalar_tensor_tensor(
                        SC(10), SC(7), 1.0 / D, SC(9), ALU.mult, ALU.subtract),
                        (sDV, 1))
                    ph.wait("scalar", sDV, dv_var)
                    ph.wait("scalar", sLc0, consts_ready)
                    ph.raw("scalar", lambda e: e.activation(SC(11), SC(10), AF.Ln,
                                                            bias=CST(1)))
                    ph.raw("scalar", lambda e: e.activation(
                        SC(11), SC(11), AF.Identity, scale=-0.5))
                    ac_r = ph.op("scalar", lambda e: e.activation(
                        SC(11), SC(11), AF.Exp), (sAC, 1))
                    ph.wait("vector", sAC, ac_r)
                    dv_y = ph.op("vector", lambda e: e.tensor_scalar(
                        yo[:, :], yc[:, :], SC(8), SC(11), ALU.subtract, ALU.mult),
                        (sDV, 1))
                    ph.wait("scalar", sDV, dv_y)
                    ph.dma("scalar", y_out[trow:trow + TT, :], yo[:, :], sSy)

            def load_w1(jj, bg, bu, sem):
                for buf, co in ((bg, jj * 512), (bu, I + jj * 512)):
                    for sp in range(2):
                        c0 = sp * 8
                        srcap = bass.AP(qw1_d.tensor, (c0 * 128) * (2 * I) + co,
                                        [[2 * I, 128], [128 * 2 * I, 8], [1, 512]])
                        dstap = bass.AP(buf, c0 * 512,
                                        [[NC * 512, 128], [512, 8], [1, 512]])
                        ph.dma("sync", dstap, srcap, sem)
                return sem.n

            def load_w2(ocu, buf, sem):
                for sp in range(6):
                    c0, c1 = sp * 8, min((sp + 1) * 8, NCI)
                    if c0 >= c1:
                        break
                    srcap = bass.AP(qw2_d.tensor, (c0 * 128) * D + ocu * 128,
                                    [[D, 128], [128 * D, c1 - c0], [1, 128]])
                    dstap = bass.AP(buf, c0 * 128,
                                    [[NCI * 128, 128], [128, c1 - c0], [1, 128]])
                    ph.dma("sync", dstap, srcap, sem)
                return sem.n

            pe_pair = {}
            pe_down = {}
            hmul = {}
            fps_free = {}
            sil_free = 0
            for s_ in w_store_sems:
                ph.wait("sync", s_)
            for blk_i in range(NB):
                t0 = blk_i * TB
                ph.wait("sync", sPE, pe_pair.get((blk_i - 1, NJ - 1), 0))
                if sx_done:
                    ph.wait("sync", sSx, sx_done[blk_i * NTB + NTB - 1][0])
                for c in range(NC):
                    ph.dma("sync", qxT[:, c * TB:(c + 1) * TB],
                           qx_d[t0:t0 + TB, c * 128:(c + 1) * 128], sLqxT,
                           transpose=True)
                qxt_ready = sLqxT.n
                for jj in range(NJ):
                    wb = jj % 2
                    guard = pe_pair.get((blk_i, jj - 2),
                                        pe_pair.get((blk_i - 1, NJ - 2 + jj), 0))
                    ph.wait("sync", sPE, guard)
                    w_ready = load_w1(jj, wg[wb], wu[wb], sLw1[wb])
                    for ti in range(NTB):
                        pp = ti % 2
                        ph.wait("tensor", sLw1[wb], w_ready)
                        ph.wait("tensor", sLqxT, qxt_ready)
                        key = ((blk_i, jj, ti - 2) if ti >= 2
                               else (blk_i, jj - 1, NTB - 2 + ti))
                        ph.wait("tensor", sDV, hmul.get(key, 0))
                        for c in range(NC):
                            lhs = qxT[:, c * TB + ti * 128: c * TB + (ti + 1) * 128]
                            st, sp = (c == 0), (c == NC - 1)
                            ph.raw("tensor",
                                   lambda e, l=lhs, c=c, pp=pp, st=st, sp=sp, wb=wb:
                                   e.matmul(gps[:, pp * 512:(pp + 1) * 512], l,
                                            wg[wb][:, c * 512:(c + 1) * 512],
                                            start=st, stop=sp))
                            ph.raw("tensor",
                                   lambda e, l=lhs, c=c, pp=pp, st=st, sp=sp, wb=wb:
                                   e.matmul(ups[:, pp * 512:(pp + 1) * 512], l,
                                            wu[wb][:, c * 512:(c + 1) * 512],
                                            start=st, stop=sp))
                        pm = ph.inc_last("tensor", sPE)
                        if ti == NTB - 1:
                            pe_pair[(blk_i, jj)] = pm
                        ph.wait("scalar", sPE, pm)
                        if sil_free:
                            ph.wait("scalar", sDV, sil_free)
                        ac_si = ph.op("scalar", lambda e, pp=pp: e.activation(
                            silu_t[:, :], gps[:, pp * 512:(pp + 1) * 512], AF.Silu),
                            (sAC, 1))
                        ph.wait("vector", sAC, ac_si)
                        hmul[(blk_i, jj, ti)] = ph.op(
                            "vector", lambda e, pp=pp, ti=ti, jj=jj: e.tensor_tensor(
                                hbuf[:, ti * I + jj * 512: ti * I + (jj + 1) * 512],
                                silu_t[:, :], ups[:, pp * 512:(pp + 1) * 512], ALU.mult),
                            (sDV, 1))
                        sil_free = hmul[(blk_i, jj, ti)]
                if do_C and blk_i >= 1:
                    emit_C(blk_i - 1)
                tr_prev = 0
                tps_ev = 0
                for ti in range(NTB):
                    cq = 30 + 3 * ti
                    ph.wait("vector", sDV, hmul[(blk_i, NJ - 1, ti)])
                    ph.raw("vector", lambda e, ti=ti, cq=cq: e.reduce_max(
                        SC(cq), hbuf[:, ti * I:(ti + 1) * I], axis=AX.X,
                        apply_absolute_value=True))
                    ph.raw("vector", lambda e, cq=cq: e.tensor_scalar(
                        SC(cq + 1), SC(cq), 1.0 / 127.0, 1e-8, ALU.mult, ALU.max))
                    dv_r = ph.op("vector", lambda e, cq=cq: e.reciprocal(
                        SC(cq + 2), SC(cq + 1)), (sDV, 1))
                    ph.wait("scalar", sDV, dv_r)
                    if tr_prev:
                        ph.wait("scalar", sPE, tr_prev)
                    for sl in range((I + 1023) // 1024):
                        f0, f1 = sl * 1024, min((sl + 1) * 1024, I)
                        ph.raw("scalar", lambda e, ti=ti, f0=f0, f1=f1, cq=cq:
                               e.activation(
                            i32b[:, 0:f1 - f0], hbuf[:, ti * I + f0: ti * I + f1],
                            AF.Identity, scale=SC(cq + 2)))
                        ph.raw("scalar", lambda e, f0=f0, f1=f1, cq=cq: e.activation(
                            qh_t[:, f0:f1], i32b[:, 0:f1 - f0], AF.Identity,
                            scale=SC(cq + 1)))
                    ac_q = ph.mark("scalar", sAC)
                    ph.wait("tensor", sAC, ac_q)
                    for g in range(6):
                        c0, c1 = g * 8, min((g + 1) * 8, NCI)
                        if tps_ev:
                            ph.wait("tensor", sDV, tps_ev)
                        for c in range(c0, c1):
                            ph.raw("tensor", lambda e, c=c, c0=c0: e.transpose(
                                tps[:, (c - c0) * 128:(c - c0 + 1) * 128],
                                qh_t[:, c * 128:(c + 1) * 128], id16[:, :]))
                        pm = ph.inc_last("tensor", sPE)
                        ph.wait("vector", sPE, pm)
                        tps_ev = ph.op(
                            "vector", lambda e, ti=ti, c0=c0, c1=c1: e.tensor_copy(
                                hbuf[:, ti * I + c0 * 128: ti * I + c1 * 128],
                                tps[:, 0:(c1 - c0) * 128]), (sDV, 1))
                    tr_prev = pm
                qh_ready = tps_ev
                for ocu in range(16):
                    b2 = ocu % 2
                    guard = pe_down.get((blk_i, ocu - 2),
                                        pe_down.get((blk_i - 1, 14 + ocu), 0))
                    ph.wait("sync", sPE, guard)
                    w2_ready = load_w2(ocu, w2T[b2], sLw2[b2])
                    ph.wait("tensor", sLw2[b2], w2_ready)
                    ph.wait("tensor", sDV, qh_ready)
                    ou = blk_i * 16 + ocu
                    fb = ou % 3
                    if (ou - 3) >= 0 and (ou - 3) in fps_free:
                        ph.wait("tensor", sDV, fps_free[ou - 3])
                    for c2 in range(NCI):
                        rhs = bass.AP(hbuf, c2 * 128,
                                      [[NTB * I, 128], [I, NTB], [1, 128]])
                        ph.raw("tensor", lambda e, c2=c2, b2=b2, fb=fb,
                               rhs=rhs: e.matmul(
                            fps[fb][:, 0:512],
                            w2T[b2][:, c2 * 128:(c2 + 1) * 128],
                            rhs, start=(c2 == 0), stop=(c2 == NCI - 1)))
                    pmd = ph.inc_last("tensor", sPE)
                    pe_down[(blk_i, ocu)] = pmd
                    orow = ocu * 128
                    ab = ou % 2
                    ph.wait("vector", sPE, pmd)
                    ph.wait("vector", sSacc[ab])
                    dv_e = ph.op("vector", lambda e, fb=fb, ab=ab: e.tensor_copy(
                        accst[ab][:, :], fps[fb][:, 0:512]), (sDV, 1))
                    fps_free[ou] = dv_e
                    ph.wait("scalar", sDV, dv_e)
                    ph.dma("scalar", accT_d[orow:orow + 128, t0:t0 + TB],
                           accst[ab][:, :], sSacc[ab])
                sacc_counts[blk_i] = (sSacc[0].n, sSacc[1].n)
            if do_C:
                emit_C(NB - 1)
            ph.flush()
            esB.close()

    return nc


def _host_arrays(q_proj, k_proj, v_proj, o_proj, q_norm, k_norm, w_gate_up, w_down):
    f32 = np.float32
    arrs = {}
    arrs["qTw"] = np.ascontiguousarray(np.asarray(q_proj, f32).T)
    arrs["kTw"] = np.ascontiguousarray(np.asarray(k_proj, f32).T)
    arrs["vTw"] = np.ascontiguousarray(np.asarray(v_proj, f32).T)
    arrs["oTw"] = np.ascontiguousarray(np.asarray(o_proj, f32).T)
    arrs["w1n"] = np.ascontiguousarray(np.asarray(w_gate_up, f32))
    arrs["w2n"] = np.ascontiguousarray(np.asarray(w_down, f32))
    arrs["qnb"] = np.ascontiguousarray(
        np.broadcast_to(np.tile(np.asarray(q_norm, f32), NH)[None, :], (128, D)))
    arrs["knb"] = np.ascontiguousarray(
        np.broadcast_to(np.asarray(k_norm, f32)[None, :], (128, HD)))
    hd = np.float32(HD)
    idx = np.arange(128, dtype=f32)
    inv = (1.0 / np.power(np.float32(1000000.0), (2.0 * idx / hd), dtype=f32)).astype(f32)
    pos = np.arange(L, dtype=f32)
    fr = pos[:, None] * inv[None, :]
    c = np.cos(fr).astype(f32)
    s = np.sin(fr).astype(f32)
    arrs["ctab"] = np.ascontiguousarray(np.concatenate([c, c], axis=1))
    arrs["stab"] = np.ascontiguousarray(np.concatenate([-s, s], axis=1))
    m = np.zeros((128, 128), f32)
    m[np.triu_indices(128, 1)] = NEGBIG
    arrs["mskd"] = m
    arrs["id16"] = np.eye(128, dtype=BF)
    cstv = np.zeros((128, 8), f32)
    cstv[:, 0] = 1e-6
    cstv[:, 1] = 1e-5
    arrs["cst"] = cstv
    return arrs


def kernel(L0_attn_x, L0_attn_q_proj, L0_attn_k_proj, L0_attn_v_proj, L0_attn_o_proj,
           L0_attn_q_norm, L0_attn_k_norm, L0_ffn_w_gate_up, L0_ffn_w_down):
    n = 8
    x = np.asarray(L0_attn_x, np.float32)
    B, Lq, Dm = x.shape
    shared = _host_arrays(L0_attn_q_proj, L0_attn_k_proj, L0_attn_v_proj, L0_attn_o_proj,
                          L0_attn_q_norm, L0_attn_k_norm, L0_ffn_w_gate_up, L0_ffn_w_down)
    nc = build()
    in_maps = []
    for c in range(n):
        m = dict(shared)
        m["x_in"] = np.ascontiguousarray(x[2 * c:2 * c + 2].reshape(SEQ * L, D))
        in_maps.append(m)
    res = run_bass_kernel_spmd(nc, in_maps, core_ids=list(range(n)))
    out = np.empty((B, Lq, Dm), np.float32)
    for c in range(n):
        out[2 * c:2 * c + 2] = res.results[c]["y"].reshape(SEQ, L, D)
    return out
